# version 03
# baseline (speedup 1.0000x reference)
import numpy as np
from contextlib import ExitStack

import concourse.bass as bass
import concourse.mybir as mybir
from concourse.bass_utils import run_bass_kernel_spmd

F32 = mybir.dt.float32
BF16 = mybir.dt.bfloat16
I32 = mybir.dt.int32
AF = mybir.ActivationFunctionType
ALU = mybir.AluOpType
AX = mybir.AxisListType

D = 2048
T = 2048
HEADS = 16
HD = 128
FI = 8192
NFC = D // 128
NTC = T // 128
NIC = FI // 128
NHG = HEADS // 2
LN_EPS = 1e-5
SUB_EPS = 1e-6

ENGS = ["sync", "gpsimd", "vector", "scalar", "tensor"]


class Ctx:
    def __init__(self, nc, block, es):
        self.nc, self.block, self.es = nc, block, es
        self.sem, self.cnt = {}, {}
        self.mark = {}
        self.io = {}
        self.ps = {}
        self.t = {}

    def add_sem(self, name):
        self.sem[name] = self.es.enter_context(self.nc.semaphore(name))
        self.cnt[name] = 0

    def inc(self, instr, name, amt=1, mark=None):
        instr.then_inc(self.sem[name], amt)
        self.cnt[name] += amt
        if mark is not None:
            self.mark[mark] = (name, self.cnt[name])
        return self.cnt[name]

    def setmark(self, mark, name):
        self.mark[mark] = (name, self.cnt[name])

    def wait(self, eng, name, value=None):
        eng.wait_ge(self.sem[name], self.cnt[name] if value is None else value)

    def waitm(self, eng, mark):
        name, val = self.mark[mark]
        eng.wait_ge(self.sem[name], val)

    def fence(self, eng, instr, name):
        self.inc(instr, name)
        self.wait(eng, name)

    def barrier(self):
        vals = {}
        allsems = ["dsy", "dgp", "dsc", "ten", "vec", "scl", "gps", "cc",
                   "ch0", "ch1", "ch2", "ch3", "ch4", "ch5", "fv", "fs", "fg",
                   "da", "db", "dc", "dd", "de", "df", "dg", "dh", "di", "dj"]
        cur = {s: self.cnt[s] for s in allsems if s in self.cnt}
        for e in ENGS:
            self.cnt["b_" + e] += 1
            vals[e] = self.cnt["b_" + e]

        def mk(e):
            def _(eng):
                for s, v in cur.items():
                    if v:
                        eng.wait_ge(self.sem[s], v)
                eng.sem_inc(self.sem["b_" + e], 1)
                for o in ENGS:
                    if o != e:
                        eng.wait_ge(self.sem["b_" + o], vals[o])
            return _
        for e in ENGS:
            getattr(self.block, e)(mk(e))

    def alloc(self, es, name, shape, dtype):
        t = es.enter_context(self.nc.sbuf_tensor(name, shape, dtype))
        self.t[name] = t
        return t



def nrecip(ctx, v, out_ap, in_ap, tmp_ap):
    ins = v.reciprocal(out=out_ap, in_=in_ap)
    ctx.fence(v, ins, "fv")
    ins = v.tensor_tensor(out=tmp_ap, in0=out_ap, in1=in_ap, op=ALU.mult)
    ctx.fence(v, ins, "fv")
    ins = v.tensor_scalar(out=tmp_ap, in0=tmp_ap, scalar1=-1.0, scalar2=2.0,
                          op0=ALU.mult, op1=ALU.add)
    ctx.fence(v, ins, "fv")
    ins = v.tensor_tensor(out=out_ap, in0=out_ap, in1=tmp_ap, op=ALU.mult)
    ctx.fence(v, ins, "fv")



def bcast_dram_row(ctx, io_ap, dstname, tag):
    block, t = ctx.block, ctx.t
    for k in range(4):
        @block.sync
        def _(sy, k=k):
            if k > 0:
                ctx.waitm(sy, "bdm_%s%d" % (tag, k - 1))
            ctx.inc(sy.dma_start(out=t["bqvr2"][0:1, 0:512],
                                 in_=io_ap[0:1, 512 * k:512 * k + 512]),
                    "dc", 16, mark="bdr_%s%d" % (tag, k))

        @block.tensor
        def _(te, k=k):
            ctx.waitm(te, "bdr_%s%d" % (tag, k))
            if k == 0 and "bc_pBfree" in ctx.mark:
                ctx.waitm(te, "bc_pBfree")
            if k > 0:
                ctx.waitm(te, "bde_%s%d" % (tag, k - 1))
            ctx.inc(te.matmul(ctx.ps["pB"][:, 0:512], t["ones1f"][:],
                              t["bqvr2"][0:1, 0:512], start=True, stop=True),
                    "ten", mark="bdm_%s%d" % (tag, k))

        @block.scalar
        def _(s, k=k):
            ctx.waitm(s, "bdm_%s%d" % (tag, k))
            ctx.inc(s.copy(out=t[dstname][:, 512 * k:512 * k + 512],
                           in_=ctx.ps["pB"][:, 0:512]), "scl",
                    mark="bde_%s%d" % (tag, k))
    ctx.setmark("bc_%s" % dstname, "scl")
    ctx.setmark("bc_pBfree", "scl")


def emit_const(ctx):
    block, t = ctx.block, ctx.t

    @block.sync
    def _(sy):
        for name, dst in [("bqkv_t", t["bqkvs"]), ("gsub_t", t["gsub"]),
                          ("slopes_r", t["slr"]), ("mrecv", t["mrecvs"]),
                          ("msend", t["msends"])]:
            ctx.inc(sy.dma_start(out=dst[:], in_=ctx.io[name]), "dsy", 16)
        ctx.setmark("c_loads", "dsy")

    @block.gpsimd
    def _(g):
        ctx.inc(g.iota(out=t["iota_p"][:], pattern=[[0, 1]], base=0,
                       channel_multiplier=1, allow_small_or_imprecise_dtypes=True),
                "gps")
        ctx.inc(g.iota(out=t["c_row"][:], pattern=[[1, 128]], base=0,
                       channel_multiplier=0, allow_small_or_imprecise_dtypes=True),
                "gps")
        ctx.inc(g.iota(out=t["c_col"][:], pattern=[[0, 128]], base=0,
                       channel_multiplier=1,
                       allow_small_or_imprecise_dtypes=True), "gps", mark="c_iota")

    @block.vector
    def _(v):
        ctx.inc(v.memset(t["ones1f"][:], 1.0), "vec")
        ctx.inc(v.memset(t["onescf"][:], 1.0), "vec")
        ctx.inc(v.memset(t["cpool"][:, 0:1], LN_EPS), "vec")
        ctx.inc(v.memset(t["cpool"][:, 1:2], SUB_EPS), "vec")
        ctx.waitm(v, "c_iota")
        ctx.inc(v.tensor_tensor(out=t["identf"][:], in0=t["c_row"][:],
                                in1=t["c_col"][:], op=ALU.is_equal), "vec")
        ctx.inc(v.tensor_tensor(out=t["ident"][:], in0=t["c_row"][:],
                                in1=t["c_col"][:], op=ALU.is_equal), "vec",
                mark="c_base")

    @block.tensor
    def _(te):
        ctx.waitm(te, "c_loads")
        ctx.waitm(te, "c_base")
        ctx.inc(te.matmul(ctx.ps["pA"][:, 0:16], t["ones1f"][:], t["slr"][:],
                          start=True, stop=True), "ten", mark="c_slmm")

    @block.scalar
    def _(s):
        ctx.waitm(s, "c_slmm")
        ctx.fence(s, s.copy(out=t["slb"][:], in_=ctx.ps["pA"][:, 0:16]), "fs")
        ctx.inc(s.activation(out=t["dhv"][:], in_=t["slb"][:], func=AF.Exp,
                             scale=-128.0), "scl", mark="c_slb")

    @block.vector
    def _(v):
        ctx.waitm(v, "c_slb")
        for tc in range(NTC):
            ctx.inc(v.tensor_scalar(out=t["st_a"][:, tc:tc + 1], in0=t["iota_p"][:],
                                    scalar1=float(128 * tc - 2048), scalar2=None,
                                    op0=ALU.add), "vec")
        ins = v.tensor_scalar(out=t["st_b"][:, 0:1], in0=t["iota_p"][:],
                              scalar1=-128.0, scalar2=None, op0=ALU.add)
        ctx.fence(v, ins, "fv")
        ctx.inc(v.tensor_scalar(out=t["c_t16a"][:], in0=t["slb"][:],
                                scalar1=t["st_b"][:, 0:1].opt(), scalar2=None,
                                op0=ALU.mult), "vec")
        ctx.inc(v.tensor_scalar(out=t["c_t16b"][:], in0=t["slb"][:],
                                scalar1=t["iota_p"][:, 0:1].opt(), scalar2=-1.0,
                                op0=ALU.mult, op1=ALU.mult), "vec")
        for tc in range(NTC):
            ctx.inc(v.tensor_scalar(out=t["khw"][:, 16 * tc:16 * tc + 16],
                                    in0=t["slb"][:],
                                    scalar1=t["st_a"][:, tc:tc + 1].opt(),
                                    scalar2=None, op0=ALU.mult), "vec")
        ctx.inc(v.tensor_scalar(out=t["nslb"][:], in0=t["slb"][:], scalar1=-1.0,
                                scalar2=None, op0=ALU.mult), "vec", mark="c_pre")

    @block.scalar
    def _(s):
        ctx.waitm(s, "c_pre")
        ctx.inc(s.activation(out=t["kdv"][:], in_=t["c_t16a"][:], func=AF.Exp),
                "scl")
        ctx.inc(s.activation(out=t["qdt"][:], in_=t["c_t16b"][:], func=AF.Exp),
                "scl")
        ctx.inc(s.activation(out=t["khw"][:], in_=t["khw"][:], func=AF.Exp),
                "scl", mark="c_exps")



def bcast_row(ctx, rowname, rowoff, width, dstname, wait_mark=None, dtype_note=""):
    block, t = ctx.block, ctx.t
    nk = width // 512
    for k in range(nk):
        @block.tensor
        def _(te, k=k):
            if k == 0 and wait_mark is not None:
                ctx.waitm(te, wait_mark)
            if k == 0 and "bc_pBfree" in ctx.mark:
                ctx.waitm(te, "bc_pBfree")
            if k > 0:
                ctx.waitm(te, "bc_%s%d" % (dstname, k - 1))
            ctx.inc(te.matmul(ctx.ps["pB"][:, 0:512], t["ones1f"][:],
                              t[rowname][0:1, rowoff + 512 * k:rowoff + 512 * k + 512],
                              start=True, stop=True), "ten",
                    mark="bcm_%s%d" % (dstname, k))

        @block.scalar
        def _(s, k=k):
            ctx.waitm(s, "bcm_%s%d" % (dstname, k))
            ctx.inc(s.copy(out=t[dstname][:, 512 * k:512 * k + 512],
                           in_=ctx.ps["pB"][:, 0:512]), "scl",
                    mark="bc_%s%d" % (dstname, k))
    ctx.setmark("bc_%s" % dstname, "scl")
    ctx.setmark("bc_pBfree", "scl")


def row_from_cols(ctx, src_ap, ncols, rowname, rowoff, tag):
    block, t = ctx.block, ctx.t

    @block.tensor
    def _(te):
        ctx.wait(te, "vec")
        ctx.wait(te, "scl")
        ctx.inc(te.transpose(out=ctx.ps["pA"][0:ncols, 0:128], in_=src_ap,
                             identity=t["identf"][:]), "ten", mark="rf_t" + tag)

    @block.scalar
    def _(s):
        ctx.waitm(s, "rf_t" + tag)
        ctx.wait(s, "dg")
        ctx.inc(s.copy(out=t["c_t16x"][0:ncols, :], in_=ctx.ps["pA"][0:ncols, 0:128]),
                "scl", mark="rf_c" + tag)

    @block.gpsimd
    def _(g):
        ctx.waitm(g, "rf_c" + tag)
        ctx.inc(g.dma_start(out=t[rowname][0:1, rowoff:rowoff + 128 * ncols],
                            in_=t["c_t16x"][0:ncols, :]), "dg", 16, mark="rf_d" + tag)


def emit_wmean(ctx):
    block, t = ctx.block, ctx.t
    stg, acc = t["wm_stage"], t["wm_acc"]

    for idx in range(12):
        w, k, buf = idx // 4, idx % 4, idx % 2

        @block.sync
        def _(sy, idx=idx, w=w, k=k, buf=buf):
            if idx >= 2:
                ctx.waitm(sy, "wm_a%d" % (idx - 2))
            ctx.inc(sy.dma_start(out=stg[:, 4096 * buf:4096 * buf + 4096],
                                 in_=ctx.io["wmsl"][w, :, 4096 * k:4096 * k + 4096]),
                    "da" if buf == 0 else "db", 16, mark="wm_d%d" % idx)

        @block.scalar
        def _(s, idx=idx, buf=buf):
            ctx.waitm(s, "wm_d%d" % idx)
            sl = stg[:, 4096 * buf:4096 * buf + 4096]
            ctx.inc(s.activation(out=sl, in_=sl, func=AF.Abs,
                                 accum_out=acc[:, idx:idx + 1]),
                    "scl", mark="wm_a%d" % idx)

    @block.vector
    def _(v):
        ctx.waitm(v, "wm_a11")
        for w in range(3):
            ctx.inc(v.tensor_reduce(out=t["wm_p"][:, w:w + 1], in_=acc[:, 4 * w:4 * w + 4],
                                    axis=AX.X, op=ALU.add), "vec")
        ctx.inc(v.memset(t["wm_p"][:, 3:4], 0.0), "vec", mark="wm_p")

    @block.tensor
    def _(te):
        ctx.waitm(te, "wm_p")
        ctx.inc(te.matmul(ctx.ps["pA"][0:1, 0:4], t["onescf"][:], t["wm_p"][:, 0:4],
                          start=True, stop=True), "ten", mark="wm_mm")

    @block.scalar
    def _(s):
        ctx.waitm(s, "wm_mm")
        ctx.fence(s, s.copy(out=t["wm_r"][0:1, 0:4], in_=ctx.ps["pA"][0:1, 0:4]),
                  "fs")
        ctx.inc(s.memzero(t["wm_r"][0:1, 4:8]), "scl", mark="wm_r0")

    @block.gpsimd
    def _(g):
        ctx.waitm(g, "wm_r0")
        ctx.inc(g.dma_start(out=ctx.io["ccw_in"], in_=t["wm_r"][0:1, 0:8]),
                "dgp", 16)
        ctx.wait(g, "dgp")
        ctx.inc(g.collective_compute(
            "AllReduce", ALU.add, replica_groups=[list(range(8))],
            ins=[ctx.io["ccw_in"]], outs=[ctx.io["ccw_out"]]), "cc")
        ctx.wait(g, "cc")
        ctx.inc(g.dma_start(out=t["wm_r"][0:1, 0:8], in_=ctx.io["ccw_out"]),
                "dgp", 16, mark="wm_rb")

    @block.vector
    def _(v):
        ctx.waitm(v, "wm_rb")
        ins = v.tensor_scalar(out=t["wm_r"][0:1, 4:7], in0=t["wm_r"][0:1, 0:3],
                              scalar1=1.0 / 16777216.0, scalar2=1e-5, op0=ALU.mult,
                              op1=ALU.max)
        ctx.fence(v, ins, "fv")
        ctx.inc(v.memset(t["wm_r"][0:1, 7:8], 1.0), "vec")
        ctx.inc(v.memset(t["wm_r"][0:1, 11:16], 1.0), "vec")
        nrecip(ctx, v, t["wm_r"][0:1, 8:11], t["wm_r"][0:1, 4:7],
               t["wm_r"][0:1, 12:15])
        ctx.inc(v.memset(t["wm_r"][0:1, 11:16], 1.0), "vec", mark="wm_sc")

    @block.tensor
    def _(te):
        ctx.waitm(te, "wm_sc")
        ctx.inc(te.matmul(ctx.ps["pA"][:, 0:16], t["ones1f"][:], t["wm_r"][0:1, 0:16],
                          start=True, stop=True), "ten", mark="wm_bm")

    @block.scalar
    def _(s):
        ctx.waitm(s, "wm_bm")
        ctx.inc(s.copy(out=t["swb"][:], in_=ctx.ps["pA"][:, 0:16]), "scl",
                mark="wm_done")


def emit_p0(ctx):
    block, t = ctx.block, ctx.t
    xstat = t["xh"]

    for tc in range(NTC):
        buf = tc % 2

        @block.sync
        def _(sy, tc=tc, buf=buf):
            if tc >= 2:
                ctx.waitm(sy, "p0_s%d" % (tc - 2))
            ctx.inc(sy.dma_start(out=xstat[:, 2048 * buf:2048 * buf + 2048],
                                 in_=ctx.io["xtok"][128 * tc:128 * tc + 128, :]),
                    "da" if buf == 0 else "db", 16, mark="p0_d%d" % tc)

        @block.scalar
        def _(s, tc=tc, buf=buf):
            ctx.waitm(s, "p0_d%d" % tc)
            sl = xstat[:, 2048 * buf:2048 * buf + 2048]
            ins = s.activation(out=sl, in_=sl, func=AF.Copy, bias=0.0,
                               accum_out=t["msum"][:, tc:tc + 1])
            ctx.fence(s, ins, "fs")
            ctx.inc(s.activation(out=sl, in_=sl, func=AF.Square,
                                 accum_out=t["sqsum"][:, tc:tc + 1]),
                    "scl", mark="p0_s%d" % tc)

    @block.vector
    def _(v):
        ctx.waitm(v, "p0_s%d" % (NTC - 1))
        ctx.inc(v.tensor_scalar(out=t["m_t"][:], in0=t["msum"][:],
                                scalar1=1.0 / 2048.0, scalar2=None, op0=ALU.mult),
                "vec")
        ins = v.tensor_scalar(out=t["st_a"][:], in0=t["sqsum"][:],
                              scalar1=1.0 / 2048.0, scalar2=None, op0=ALU.mult)
        ctx.fence(v, ins, "fv")
        ins = v.tensor_tensor(out=t["st_b"][:], in0=t["m_t"][:], in1=t["m_t"][:],
                              op=ALU.mult)
        ctx.fence(v, ins, "fv")
        ctx.inc(v.tensor_tensor(out=t["st_a"][:], in0=t["st_a"][:], in1=t["st_b"][:],
                                op=ALU.subtract), "vec", mark="p0_var")

    @block.scalar
    def _(s):
        ctx.waitm(s, "p0_var")
        ctx.inc(s.activation(out=t["st_b"][:], in_=t["st_a"][:], func=AF.Sqrt,
                             bias=LN_EPS), "scl", mark="p0_sd")

    @block.vector
    def _(v):
        ctx.waitm(v, "p0_sd")
        nrecip(ctx, v, t["rstd_t"][:], t["st_b"][:], t["st_c"][:])
        ctx.inc(v.tensor_copy(out=t["st_c"][:], in_=t["rstd_t"][:]), "vec",
                mark="p0_rstd")

    row_from_cols(ctx, t["m_t"][:], 16, "c_qdrow", 0, "p0m")
    row_from_cols(ctx, t["rstd_t"][:], 16, "rrow", 0, "p0r")
    bcast_row(ctx, "c_qdrow", 0, 2048, "mbf", wait_mark="rf_dp0m")
    bcast_row(ctx, "rrow", 0, 2048, "rbf", wait_mark="rf_dp0r")

    for hf in range(2):
        hsl = slice(1024 * hf, 1024 * hf + 1024)

        @block.sync
        def _(sy, hf=hf):
            if hf == 1:
                ctx.wait(sy, "gps")
                ctx.wait(sy, "vec")
            else:
                ctx.waitm(sy, "bc_rbf")
            ctx.inc(sy.dma_start(
                out=t["xh"][:].rearrange("p (fc t) -> p fc t", fc=NFC),
                in_=ctx.io["xT"].rearrange("(fc p) t -> p fc t", p=128)[:, :, hsl]),
                "dc", 16, mark="p0_xh%d" % hf)

        @block.vector
        def _(v, hf=hf):
            if hf == 1:
                ctx.wait(v, "ten")
                ctx.waitm(v, "p0_q0_%d" % (NFC - 1))
            ctx.inc(v.memset(t["mxa"][:], 0.0), "vec", mark="p0_mx0_%d" % hf)

        for fc in range(NFC):
            @block.vector
            def _(v, hf=hf, fc=fc):
                if fc == 0:
                    ctx.waitm(v, "p0_xh%d" % hf)
                xsl = t["xh"][:, 1024 * fc:1024 * fc + 1024]
                ins = v.tensor_tensor(out=xsl, in0=xsl, in1=t["mbf"][:, hsl],
                                      op=ALU.subtract)
                ctx.fence(v, ins, "fv")
                ins = v.tensor_tensor(out=xsl, in0=xsl, in1=t["rbf"][:, hsl],
                                      op=ALU.mult)
                ctx.fence(v, ins, "fv")
                ins = v.scalar_tensor_tensor(out=t["mxa"][:], in0=xsl, scalar=1.0,
                                             in1=t["mxa"][:], op0=ALU.mult,
                                             op1=ALU.max)
                ctx.fence(v, ins, "fv")
                ctx.inc(v.scalar_tensor_tensor(out=t["mxa"][:], in0=xsl, scalar=-1.0,
                                               in1=t["mxa"][:], op0=ALU.mult,
                                               op1=ALU.max),
                        "vec", mark="p0_mx%d_%d" % (hf, fc))

        for tp in range(8):
            pbuf = "pA" if tp % 2 == 0 else "pB"

            @block.tensor
            def _(te, hf=hf, tp=tp, pbuf=pbuf):
                if tp == 0:
                    ctx.waitm(te, "p0_mx%d_%d" % (hf, NFC - 1))
                if tp >= 2:
                    ctx.waitm(te, "p0_red%d_%d" % (hf, tp - 2))
                ctx.inc(te.transpose(out=ctx.ps[pbuf][:, 0:128],
                                     in_=t["mxa"][:, 128 * tp:128 * tp + 128],
                                     identity=t["identf"][:]), "ten",
                        mark="p0_tp%d_%d" % (hf, tp))

            @block.vector
            def _(v, hf=hf, tp=tp, pbuf=pbuf):
                ctx.waitm(v, "p0_tp%d_%d" % (hf, tp))
                ctx.inc(v.tensor_reduce(out=t["mx_t"][:, 8 * hf + tp:8 * hf + tp + 1],
                                        in_=ctx.ps[pbuf][:, 0:128],
                                        axis=AX.X, op=ALU.max), "vec",
                        mark="p0_red%d_%d" % (hf, tp))

        c8 = slice(8 * hf, 8 * hf + 8)

        @block.vector
        def _(v, hf=hf, c8=c8):
            ctx.waitm(v, "p0_red%d_7" % hf)
            ins = v.tensor_scalar(out=t["st_a"][:, c8], in0=t["mx_t"][:, c8],
                                  scalar1=1e-5, scalar2=None, op0=ALU.max)
            ctx.fence(v, ins, "fv")
            nrecip(ctx, v, t["st_b"][:, c8], t["st_a"][:, c8], t["st_d"][:, c8])
            ins = v.tensor_scalar(out=t["sx_t"][:, c8], in0=t["st_b"][:, c8],
                                  scalar1=127.0, scalar2=None, op0=ALU.mult)
            ctx.fence(v, ins, "fv")
            nrecip(ctx, v, t["invsx_t"][:, c8], t["sx_t"][:, c8], t["st_d"][:, c8])
            ctx.inc(v.tensor_copy(out=t["st_d"][:, c8], in_=t["invsx_t"][:, c8]),
                    "vec", mark="p0_sx%d" % hf)

        row_from_cols(ctx, t["sx_t"][:, c8], 8, "rrow", 0, "p0x%d" % hf)
        bcast_row(ctx, "rrow", 0, 1024, "sbq", wait_mark="rf_dp0x%d" % hf)

        for fc in range(NFC):
            b2 = fc % 2

            @block.vector
            def _(v, hf=hf, fc=fc, b2=b2):
                if fc == 0:
                    ctx.waitm(v, "bc_sbq1")
                if fc >= 2:
                    ctx.waitm(v, "p0_cl%d_%d" % (hf, fc - 2))
                ctx.inc(v.tensor_tensor(out=t["ti32"][:, 1024 * b2:1024 * b2 + 1024],
                                        in0=t["xh"][:, 1024 * fc:1024 * fc + 1024],
                                        in1=t["sbq"][:, 0:1024], op=ALU.mult),
                        "vec", mark="p0_q%d_%d" % (hf, fc))

            @block.gpsimd
            def _(g, hf=hf, fc=fc, b2=b2):
                ctx.waitm(g, "p0_q%d_%d" % (hf, fc))
                dst = ctx.t["xbig"][:, 2048 * fc + 1024 * hf:
                                    2048 * fc + 1024 * hf + 1024]
                ctx.inc(g.tensor_scalar(out=dst,
                                        in0=t["ti32"][:, 1024 * b2:1024 * b2 + 1024],
                                        scalar1=127.0, scalar2=-128.0,
                                        op0=ALU.min, op1=ALU.max), "gps",
                        mark="p0_cl%d_%d" % (hf, fc))


def emit_gateup(ctx):
    block, t = ctx.block, ctx.t
    swb = t["swb"]

    @block.vector
    def _(v):
        ctx.inc(v.memset(t["mxz"][:], 0.0), "vec")
        ctx.inc(v.memset(t["sq"][:], 0.0), "vec")
        ctx.waitm(v, "c_loads")
        ctx.inc(v.tensor_tensor(out=t["gsq"][:], in0=t["gsub"][:], in1=t["gsub"][:],
                                op=ALU.mult), "vec", mark="gu_z")

    for ic in range(NIC):
        buf = ic % 2

        @block.sync
        def _(sy, ic=ic, buf=buf):
            if ic >= 1:
                ctx.waitm(sy, "gu_rg%d_1" % (ic - 1))
                ctx.inc(sy.dma_start(out=ctx.io["rgd"][128 * (ic - 1):128 * ic, :],
                                     in_=t["rgt"][:]), "de", 16,
                        mark="gu_rgd%d" % (ic - 1))
            if ic >= 2:
                ctx.waitm(sy, "gu_q%d" % (ic - 2))
            for wio, stg in [("wgS", t["wstg_g"]), ("wuS", t["wstg_u"])]:
                ctx.inc(sy.dma_start(out=stg[:, 2048 * (ic % 2):2048 * (ic % 2) + 2048],
                                     in_=ctx.io[wio][ic]), "dd", 16)
            ctx.setmark("gu_w%d" % ic, "dd")
            if ic >= 1:
                ctx.waitm(sy, "gu_wd1_%d" % (ic - 1))
            ctx.inc(sy.dma_start(out=t["wdstg"][:],
                                 in_=ctx.io["wdT"][128 * ic:128 * ic + 128, :]),
                    "df", 16, mark="gu_wdd%d" % ic)

        @block.vector
        def _(v, ic=ic, buf=buf):
            ctx.waitm(v, "gu_w%d" % ic)
            if ic == 0:
                ctx.waitm(v, "gu_z")
            if ic >= 2:
                ctx.waitm(v, "gu_mm%d_1" % (ic - 2))
            if ic >= 1:
                ctx.waitm(v, "gu_wd2_%d" % (ic - 1))
            for wi, (stg, wq) in enumerate([(t["wstg_g"], t["wqg"]),
                                            (t["wstg_u"], t["wqu"])]):
                ins = v.tensor_scalar(out=t["ti32"][:, 0:2048],
                                      in0=stg[:, 2048 * buf:2048 * buf + 2048],
                                      scalar1=swb[:, 8 + wi:9 + wi].opt(),
                                      scalar2=None, op0=ALU.mult)
                ctx.fence(v, ins, "fv")
                ins = v.tensor_scalar(out=wq[:, 2048 * buf:2048 * buf + 2048],
                                      in0=t["ti32"][:, 0:2048],
                                      scalar1=1.0, scalar2=-1.0,
                                      op0=ALU.min, op1=ALU.max)
                if wi == 0:
                    ctx.fence(v, ins, "fv")
                else:
                    ctx.inc(ins, "ch1")
            ctx.setmark("gu_q%d" % ic, "ch1")
            ctx.waitm(v, "gu_q%d" % ic)
            ctx.waitm(v, "gu_wdd%d" % ic)
            ins = v.tensor_scalar(out=t["ti32"][:, 0:2048], in0=t["wdstg"][:],
                                  scalar1=swb[:, 10:11].opt(), scalar2=None,
                                  op0=ALU.mult)
            ctx.inc(ins, "ch4", mark="gu_wd1_%d" % ic)

        for half in range(2):
            e = 2 * ic + half

            @block.tensor
            def _(te, ic=ic, buf=buf, half=half):
                if half == 0:
                    ctx.waitm(te, "gu_q%d" % ic)
                if ic >= 1:
                    ctx.waitm(te, "gu_rel%d_%d" % (ic - 1, half))
                    ctx.waitm(te, "gu_ps%d_%d" % (ic - 1, half))
                last = None
                for fc in range(NFC):
                    for gu, wq in [(0, t["wqg"]), (1, t["wqu"])]:
                        lhsT = wq[:, 2048 * buf + 128 * fc:2048 * buf + 128 * fc + 128]
                        for tg in (2 * half, 2 * half + 1):
                            ps = ctx.ps["pG" if gu == 0 else "pU"]
                            last = te.matmul(
                                ps[:, 512 * tg:512 * tg + 512], lhsT,
                                ctx.t["xbig"][:, 2048 * fc + 512 * tg:
                                              2048 * fc + 512 * tg + 512],
                                start=(fc == 0), stop=(fc == NFC - 1))
                ctx.inc(last, "ten", mark="gu_mm%d_%d" % (ic, half))

            @block.scalar
            def _(s, ic=ic, half=half, e=e):
                ctx.waitm(s, "gu_mm%d_%d" % (ic, half))
                if e >= 1:
                    ctx.waitm(s, "gu_ps%d_%d" % ((e - 1) // 2, (e - 1) % 2))
                for k, tg in enumerate((2 * half, 2 * half + 1)):
                    sl = t["gu_t2"][:, 512 * k:512 * k + 512]
                    ctx.inc(s.activation(out=sl,
                                         in_=ctx.ps["pG"][:, 512 * tg:512 * tg + 512],
                                         func=AF.Relu), "ch2")
                ctx.setmark("gu_rel%d_%d" % (ic, half), "ch2")

            @block.gpsimd
            def _(g, ic=ic, half=half):
                ctx.waitm(g, "gu_rel%d_%d" % (ic, half))
                for k in range(2):
                    sl = t["gu_t2"][:, 512 * k:512 * k + 512]
                    ctx.inc(g.tensor_tensor(out=sl, in0=sl, in1=sl, op=ALU.mult),
                            "gps")
                ctx.setmark("gu_sq%d_%d" % (ic, half), "gps")

            @block.vector
            def _(v, ic=ic, half=half):
                ctx.waitm(v, "gu_sq%d_%d" % (ic, half))
                e_ = 2 * ic + half
                if e_ >= 1:
                    ctx.waitm(v, "gu_rg%d_%d" % ((e_ - 1) // 2, (e_ - 1) % 2))
                for k, tg in enumerate((2 * half, 2 * half + 1)):
                    ctx.inc(v.tensor_tensor(out=t["gu_r"][:, 512 * k:512 * k + 512],
                                            in0=t["gu_t2"][:, 512 * k:512 * k + 512],
                                            in1=ctx.ps["pU"][:, 512 * tg:512 * tg + 512],
                                            op=ALU.mult), "ch2")
                ctx.setmark("gu_ps%d_%d" % (ic, half), "ch2")

            @block.gpsimd
            def _(g, ic=ic, half=half, e=e):
                ctx.waitm(g, "gu_ps%d_%d" % (ic, half))
                if e >= 1:
                    ctx.waitm(g, "gu_mxq%d_%d" % ((e - 1) // 2, (e - 1) % 2))
                ctx.inc(g.tensor_tensor(out=t["gu_t3"][:], in0=t["gu_r"][:],
                                        in1=t["gu_r"][:], op=ALU.mult), "ch5")
                ctx.setmark("gu_t3_%d_%d" % (ic, half), "ch5")

            @block.vector
            def _(v, ic=ic, half=half):
                ctx.waitm(v, "gu_ps%d_%d" % (ic, half))
                ctx.waitm(v, "gu_t3_%d_%d" % (ic, half))
                if ic >= 1 and half == 0:
                    ctx.waitm(v, "gu_rgd%d" % (ic - 1))
                hsl = slice(1024 * half, 1024 * half + 1024)
                ctx.inc(v.tensor_scalar(out=t["rgt"][:, hsl], in0=t["gu_r"][:],
                                        scalar1=t["gsub"][:, ic:ic + 1].opt(),
                                        scalar2=None, op0=ALU.mult), "ch3",
                        mark="gu_rg%d_%d" % (ic, half))
                ctx.inc(v.tensor_tensor(out=t["sq"][:, hsl], in0=t["sq"][:, hsl],
                                        in1=t["gu_t3"][:], op=ALU.add), "vec")
                ctx.inc(v.scalar_tensor_tensor(out=t["mxz"][:, hsl],
                                               in0=t["gu_t3"][:],
                                               scalar=t["gsq"][:, ic:ic + 1].opt(),
                                               in1=t["mxz"][:, hsl],
                                               op0=ALU.mult, op1=ALU.max), "vec",
                        mark="gu_mxq%d_%d" % (ic, half))

        @block.gpsimd
        def _(g, ic=ic):
            ctx.waitm(g, "gu_wd1_%d" % ic)
            if ic >= 1:
                ctx.waitm(g, "gu_wdq%d" % (ic - 1))
            ins = g.tensor_scalar(out=t["wdqb"][:], in0=t["ti32"][:, 0:2048],
                                  scalar1=1.0, scalar2=-1.0,
                                  op0=ALU.min, op1=ALU.max)
            ctx.inc(ins, "ch5", mark="gu_wd2_%d" % ic)
            ctx.waitm(g, "gu_wd2_%d" % ic)
            ctx.inc(g.dma_start(out=ctx.io["wdq"][128 * ic:128 * ic + 128, :],
                                in_=t["wdqb"][:]), "dgp", 16, mark="gu_wdq%d" % ic)

    @block.sync
    def _(sy):
        ctx.waitm(sy, "gu_rg%d_1" % (NIC - 1))
        ctx.inc(sy.dma_start(out=ctx.io["rgd"][128 * (NIC - 1):128 * NIC, :],
                             in_=t["rgt"][:]), "de", 16, mark="gu_rgd%d" % (NIC - 1))

    for src_, dst, op in [("mxz", "mxz_t", ALU.max), ("sq", "sq_t", ALU.add)]:
        for tp in range(NTC):
            pbuf = "pA" if tp % 2 == 0 else "pB"

            @block.tensor
            def _(te, src_=src_, tp=tp, pbuf=pbuf):
                if tp == 0 and src_ == "mxz":
                    ctx.waitm(te, "gu_mxq%d_1" % (NIC - 1))
                    ctx.wait(te, "vec")
                if tp == 0 and src_ == "sq":
                    ctx.waitm(te, "ep_mxz_%d" % (NTC - 1))
                if tp >= 2:
                    ctx.waitm(te, "ep_%s_%d" % (src_, tp - 2))
                ctx.inc(te.transpose(out=ctx.ps[pbuf][:, 0:128],
                                     in_=t[src_][:, 128 * tp:128 * tp + 128],
                                     identity=t["identf"][:]), "ten",
                        mark="ept_%s_%d" % (src_, tp))

            @block.vector
            def _(v, src_=src_, dst=dst, tp=tp, op=op, pbuf=pbuf):
                ctx.waitm(v, "ept_%s_%d" % (src_, tp))
                ctx.inc(v.tensor_reduce(out=t[dst][:, tp:tp + 1],
                                        in_=ctx.ps[pbuf][:, 0:128],
                                        axis=AX.X, op=op), "vec",
                        mark="ep_%s_%d" % (src_, tp))

    @block.scalar
    def _(s):
        ctx.waitm(s, "ep_mxz_%d" % (NTC - 1))
        ctx.wait(s, "vec")
        ctx.inc(s.activation(out=t["mxz_t"][:], in_=t["mxz_t"][:], func=AF.Sqrt),
                "scl", mark="gu_mxs")

    @block.vector
    def _(v):
        ctx.waitm(v, "ep_sq_%d" % (NTC - 1))
        ctx.waitm(v, "gu_mxs")
        k1, v1 = t["st_a"], t["st_b"]
        ins = v.tensor_tensor(out=k1[:], in0=t["invsx_t"][:], in1=t["invsx_t"][:],
                              op=ALU.mult)
        ctx.fence(v, ins, "fv")
        ins = v.tensor_tensor(out=k1[:], in0=k1[:], in1=t["invsx_t"][:], op=ALU.mult)
        ctx.fence(v, ins, "fv")
        ins = v.tensor_scalar(out=k1[:], in0=k1[:], scalar1=swb[:, 4:5].opt(),
                              scalar2=swb[:, 4:5].opt(), op0=ALU.mult, op1=ALU.mult)
        ctx.fence(v, ins, "fv")
        ins = v.tensor_scalar(out=k1[:], in0=k1[:], scalar1=swb[:, 5:6].opt(),
                              scalar2=None, op0=ALU.mult)
        ctx.fence(v, ins, "fv")
        ins = v.tensor_tensor(out=v1[:], in0=k1[:], in1=k1[:], op=ALU.mult)
        ctx.fence(v, ins, "fv")
        ins = v.tensor_tensor(out=v1[:], in0=v1[:], in1=t["sq_t"][:], op=ALU.mult)
        ctx.fence(v, ins, "fv")
        ctx.inc(v.tensor_scalar(out=v1[:], in0=v1[:], scalar1=1.0 / 8192.0,
                                scalar2=None, op0=ALU.mult), "vec", mark="gu_var")

    @block.scalar
    def _(s):
        ctx.waitm(s, "gu_var")
        ctx.inc(s.activation(out=t["st_c"][:], in_=t["st_b"][:], func=AF.Sqrt,
                             bias=SUB_EPS), "scl", mark="gu_sd")

    @block.vector
    def _(v):
        ctx.waitm(v, "gu_sd")
        nrecip(ctx, v, t["rstdz_t"][:], t["st_c"][:], t["st_d"][:])
        ins = v.tensor_tensor(out=t["st_b"][:], in0=t["rstdz_t"][:], in1=t["st_a"][:],
                              op=ALU.mult)
        ctx.fence(v, ins, "fv")
        ins = v.tensor_tensor(out=t["st_c"][:], in0=t["st_b"][:], in1=t["mxz_t"][:],
                              op=ALU.mult)
        ctx.fence(v, ins, "fv")
        ins = v.tensor_scalar(out=t["st_c"][:], in0=t["st_c"][:], scalar1=1e-5,
                              scalar2=None, op0=ALU.max)
        ctx.fence(v, ins, "fv")
        ins = v.reciprocal(out=t["st_d"][:], in_=t["st_c"][:])
        ctx.fence(v, ins, "fv")
        ins = v.tensor_scalar(out=t["sz_t"][:], in0=t["st_d"][:], scalar1=127.0,
                              scalar2=None, op0=ALU.mult)
        ctx.fence(v, ins, "fv")
        ins = v.tensor_tensor(out=t["qsc_t"][:], in0=t["st_b"][:], in1=t["sz_t"][:],
                              op=ALU.mult)
        ctx.fence(v, ins, "fv")
        ins = v.reciprocal(out=t["st_d"][:], in_=t["sz_t"][:])
        ctx.fence(v, ins, "fv")
        ctx.inc(v.tensor_scalar(out=t["invsd_t"][:], in0=t["st_d"][:],
                                scalar1=swb[:, 6:7].opt(), scalar2=None,
                                op0=ALU.mult), "vec", mark="gu_tok")

    row_from_cols(ctx, t["qsc_t"][:], 16, "c_qdrow", 0, "guq")
    bcast_row(ctx, "c_qdrow", 0, 2048, "qscb", wait_mark="rf_dguq")


def emit_down(ctx):
    block, t = ctx.block, ctx.t
    for q in range(4):
        qsl = slice(512 * q, 512 * q + 512)

        for i in range(NIC):
            buf = i % 2

            @block.sync
            def _(sy, q=q, i=i, buf=buf):
                if i >= 2:
                    ctx.waitm(sy, "dn_q%d_%d" % (q, i - 2))
                elif q >= 1:
                    ctx.waitm(sy, "dn_q%d_%d" % (q - 1, NIC - 2 + i))
                ctx.inc(sy.dma_start(out=t["rstg"][:, 512 * buf:512 * buf + 512],
                                     in_=ctx.io["rgd"][128 * i:128 * i + 128, qsl]),
                        "da" if buf == 0 else "db", 16, mark="dn_d%d_%d" % (q, i))

            @block.vector
            def _(v, q=q, i=i, buf=buf):
                ctx.waitm(v, "dn_d%d_%d" % (q, i))
                if i >= 2:
                    ctx.waitm(v, "dn_c%d_%d" % (q, i - 2))
                elif q >= 1:
                    ctx.waitm(v, "dn_c%d_%d" % (q - 1, NIC - 2 + i))
                ctx.inc(v.tensor_tensor(out=t["tiq"][:, 512 * buf:512 * buf + 512],
                                        in0=t["rstg"][:, 512 * buf:512 * buf + 512],
                                        in1=t["qscb"][:, qsl], op=ALU.mult),
                        "ch4", mark="dn_q%d_%d" % (q, i))

            @block.gpsimd
            def _(g, q=q, i=i, buf=buf):
                ctx.waitm(g, "dn_q%d_%d" % (q, i))
                if q >= 1 and i == 0:
                    ctx.waitm(g, "dn_mm%d_%d" % (q - 1, 2 * NIC - 1))
                ctx.inc(g.tensor_scalar(out=ctx.t["xbig"][:, 512 * i:512 * i + 512],
                                        in0=t["tiq"][:, 512 * buf:512 * buf + 512],
                                        scalar1=127.0, scalar2=-128.0,
                                        op0=ALU.min, op1=ALU.max), "ch0",
                        mark="dn_c%d_%d" % (q, i))

        for fgp in range(2):
            for i in range(NIC):
                wbuf = i % 2
                idx = fgp * NIC + i

                @block.scalar
                def _(s, q=q, fgp=fgp, i=i, wbuf=wbuf, idx=idx):
                    if idx >= 2:
                        ctx.waitm(s, "dn_mm%d_%d" % (q, idx - 2))
                    elif q >= 1:
                        ctx.waitm(s, "dn_mm%d_%d" % (q - 1, 2 * NIC - 2 + idx))
                    ctx.inc(s.dma_start(
                        out=t["wdt2"][:, 1024 * wbuf:1024 * wbuf + 1024],
                        in_=ctx.io["wdq"][128 * i:128 * i + 128,
                                          1024 * fgp:1024 * fgp + 1024]),
                        "dsc" if wbuf == 0 else "df", 16,
                        mark="dn_wt%d_%d" % (q, idx))

                @block.tensor
                def _(te, q=q, fgp=fgp, i=i, wbuf=wbuf, idx=idx):
                    ctx.waitm(te, "dn_wt%d_%d" % (q, idx))
                    if fgp == 0:
                        ctx.waitm(te, "dn_c%d_%d" % (q, i))
                    if i == 0:
                        if fgp == 1:
                            ctx.waitm(te, "dn_ev%d_0" % q)
                        elif q >= 1:
                            ctx.waitm(te, "dn_ev%d_1" % (q - 1))
                    last = None
                    for fg2 in range(2):
                        ps = ctx.ps["pG" if fg2 == 0 else "pU"]
                        for tc in range(4):
                            last = te.matmul(
                                ps[:, 512 * tc:512 * tc + 512],
                                ctx.t["xbig"][:, 512 * i + 128 * tc:
                                              512 * i + 128 * tc + 128],
                                t["wdt2"][:, 1024 * wbuf + 512 * fg2:
                                          1024 * wbuf + 512 * fg2 + 512],
                                start=(i == 0), stop=(i == NIC - 1))
                    ctx.inc(last, "ten", mark="dn_mm%d_%d" % (q, idx))

            @block.vector
            def _(v, q=q, fgp=fgp):
                ctx.waitm(v, "dn_mm%d_%d" % (q, fgp * NIC + NIC - 1))
                if q >= 1 and fgp == 0:
                    ctx.waitm(v, "dn_od%d" % (q - 1))
                for fg2 in range(2):
                    ps = ctx.ps["pG" if fg2 == 0 else "pU"]
                    fg = 2 * fgp + fg2
                    for tc in range(4):
                        ctx.inc(v.tensor_scalar(
                            out=t["ot"][:, 2048 * tc + 512 * fg:
                                        2048 * tc + 512 * fg + 512],
                            in0=ps[:, 512 * tc:512 * tc + 512],
                            scalar1=t["invsd_t"][:, 4 * q + tc:4 * q + tc + 1].opt(),
                            scalar2=None, op0=ALU.mult), "ch1")
                ctx.setmark("dn_ev%d_%d" % (q, fgp), "ch1")

        @block.sync
        def _(sy, q=q):
            ctx.waitm(sy, "dn_ev%d_1" % q)
            for tc in range(4):
                ctx.inc(sy.dma_start(
                    out=ctx.io["out"][512 * q + 128 * tc:512 * q + 128 * tc + 128, :],
                    in_=t["ot"][:, 2048 * tc:2048 * tc + 2048]), "de", 16)
            ctx.setmark("dn_od%d" % q, "de")


def emit_xn(ctx):
    block, t = ctx.block, ctx.t

    row_from_cols(ctx, t["m_t"][:], 16, "c_qdrow2", 0, "x2m")
    row_from_cols(ctx, t["rstd_t"][:], 16, "rrow2", 0, "x2r")
    bcast_row(ctx, "c_qdrow2", 0, 2048, "mbf2", wait_mark="rf_dx2m")
    bcast_row(ctx, "rrow2", 0, 2048, "rbf2", wait_mark="rf_dx2r")

    for fc in range(NFC):
        buf = fc % 2

        @block.sync
        def _(sy, fc=fc, buf=buf):
            if fc >= 2:
                ctx.waitm(sy, "x2_%d" % (fc - 2))
            ctx.inc(sy.dma_start(out=t["xst2"][:, 2048 * buf:2048 * buf + 2048],
                                 in_=ctx.io["xT"][128 * fc:128 * fc + 128, :]),
                    "da" if buf == 0 else "db", 16, mark="x2d_%d" % fc)

        @block.vector
        def _(v, fc=fc, buf=buf):
            ctx.waitm(v, "x2d_%d" % fc)
            if fc == 0:
                ctx.waitm(v, "bc_rbf2")
            sl = t["xst2"][:, 2048 * buf:2048 * buf + 2048]
            ins = v.tensor_tensor(out=sl, in0=sl, in1=t["mbf2"][:], op=ALU.subtract)
            ctx.fence(v, ins, "fv")
            ctx.inc(v.tensor_tensor(out=ctx.t["xbig"][:, 2048 * fc:2048 * fc + 2048],
                                    in0=sl, in1=t["rbf2"][:], op=ALU.mult),
                    "ch0", mark="x2_%d" % fc)



def emit_decay(ctx):
    block, t = ctx.block, ctx.t

    @block.gpsimd
    def _(g):
        ctx.inc(g.iota(out=t["d_row"][:], pattern=[[1, 128]], base=0,
                       channel_multiplier=0, allow_small_or_imprecise_dtypes=True),
                "gps")
        ctx.inc(g.iota(out=t["d_col"][:], pattern=[[0, 128]], base=0,
                       channel_multiplier=1,
                       allow_small_or_imprecise_dtypes=True), "gps", mark="d_iota")

    @block.vector
    def _(v):
        ctx.waitm(v, "d_iota")
        ctx.inc(v.tensor_tensor(out=t["d_ige"][:], in0=t["d_row"][:],
                                in1=t["d_col"][:], op=ALU.is_ge), "vec")
        ins = v.tensor_tensor(out=t["d_diff"][:], in0=t["d_row"][:],
                              in1=t["d_col"][:], op=ALU.subtract)
        ctx.fence(v, ins, "fv")
        ctx.inc(v.tensor_scalar(out=t["d_diff"][:], in0=t["d_diff"][:],
                                scalar1=0.0, scalar2=None, op0=ALU.max),
                "vec", mark="d_base")

    for h in range(HEADS):
        @block.scalar
        def _(s, h=h):
            if h == 0:
                ctx.waitm(s, "d_base")
            if h >= 2:
                ctx.waitm(s, "d_dtv%d" % (h - 2))
            ctx.inc(s.activation(out=t["d_dte"][:, 128 * (h % 2):128 * (h % 2) + 128],
                                 in_=t["d_diff"][:], func=AF.Exp,
                                 scale=t["nslb"][:, h:h + 1].opt()),
                    "scl", mark="d_dts%d" % h)

        @block.vector
        def _(v, h=h):
            ctx.waitm(v, "d_dts%d" % h)
            ctx.inc(v.tensor_tensor(out=t["dt"][:, 128 * h:128 * h + 128],
                                    in0=t["d_dte"][:, 128 * (h % 2):128 * (h % 2) + 128],
                                    in1=t["d_ige"][:], op=ALU.mult),
                    "vec", mark="d_dtv%d" % h)

    for h in range(HEADS):
        @block.scalar
        def _(s, h=h):
            if h == 0:
                ctx.waitm(s, "d_base")
            ctx.inc(s.activation(out=t["qd"][:, 128 * h:128 * h + 128],
                                 in_=t["d_row"][:], func=AF.Exp,
                                 scale=t["nslb"][:, h:h + 1].opt()),
                    "scl", mark="d_qb%d" % h)


def emit_attn(ctx):
    block, t = ctx.block, ctx.t
    pG, pU = ctx.ps["pG"], ctx.ps["pU"]
    pufree = [None, None]

    @block.tensor
    def _(te):
        ctx.wait(te, "vec")
        ctx.wait(te, "scl")
        te.matmul(pG[:, 0:1], t["ones1f"][:], t["msends"][0:1, 0:1],
                  start=True, stop=True)
        ctx.inc(te.matmul(pG[:, 512:513], t["ones1f"][:], t["mrecvs"][0:1, 0:1],
                          start=True, stop=True), "ten", mark="at_msk")

    @block.scalar
    def _(s):
        ctx.waitm(s, "at_msk")
        s.copy(out=t["msb"][:, 0:1], in_=pG[:, 0:1])
        ctx.inc(s.copy(out=t["mrb"][:, 0:1], in_=pG[:, 512:513]), "scl",
                mark="at_msk2")

    bcast_dram_row(ctx, ctx.io["bqv"], "bvb", "atb")

    for hg in range(NHG):
        h0 = 2 * hg
        for proj, colbase in [("q", h0 * 128), ("k", D + h0 * 128),
                              ("v", 2 * D + h0 * 128)]:
            @block.gpsimd
            def _(g, hg=hg, proj=proj, colbase=colbase):
                if not (hg == 0 and proj == "q"):
                    ctx.waitm(g, "at_wcons")
                ctx.inc(g.dma_start(
                    out=t["wstg2"][:].rearrange("p (fc o) -> p fc o", fc=NFC),
                    in_=ctx.io["wqkv"].rearrange("(fc p) o -> p fc o", p=128)
                    [:, :, colbase:colbase + 256]), "dh", 16,
                    mark="at_w%s%d" % (proj, hg))

            if proj in ("q", "k"):
                dst = t["qt"] if proj == "q" else t["kt"]
                bcol = (0 if proj == "q" else 16) + h0
                for qh in range(2):
                    for tg in range(4):
                        gi = qh * 4 + tg
                        par = gi % 2
                        pb = pU[:, 1024 + 512 * par:1536 + 512 * par]

                        @block.tensor
                        def _(te, hg=hg, proj=proj, qh=qh, tg=tg, gi=gi, pb=pb,
                              par=par):
                            ctx.waitm(te, "at_w%s%d" % (proj, hg))
                            if pufree[par] is not None:
                                ctx.waitm(te, pufree[par])
                            last = None
                            for fc in range(NFC):
                                last = te.matmul(
                                    pb, t["wstg2"][:, 256 * fc + 128 * qh:
                                                   256 * fc + 128 * qh + 128],
                                    ctx.t["xbig"][:, 2048 * fc + 512 * tg:
                                                  2048 * fc + 512 * tg + 512],
                                    start=(fc == 0), stop=(fc == NFC - 1))
                            ctx.inc(last, "ten", mark="at_pm%s%d_%d" % (proj, hg, gi))

                        @block.vector
                        def _(v, hg=hg, proj=proj, qh=qh, tg=tg, gi=gi, pb=pb,
                              dst=dst, bcol=bcol):
                            ctx.waitm(v, "at_pm%s%d_%d" % (proj, hg, gi))
                            if hg >= 1 and gi == 0 and proj == "q":
                                ctx.waitm(v, "at_scandone%d" % (hg - 1))
                                ctx.waitm(v, "at_scanmm%d" % (hg - 1))
                            ctx.inc(v.tensor_scalar(
                                out=dst[:, 2048 * qh + 512 * tg:
                                        2048 * qh + 512 * tg + 512],
                                in0=pb,
                                scalar1=t["bqkvs"][:, bcol + qh:bcol + qh + 1].opt(),
                                scalar2=None, op0=ALU.add), "ch1",
                                mark="at_pe%s%d_%d" % (proj, hg, gi))
                        pufree[par] = "at_pe%s%d_%d" % (proj, hg, gi)
                ctx.setmark("at_wcons", "ten")
            else:
                for tc in range(NTC):
                    par = tc % 2
                    pb = pU[:, 1024 + 512 * par:1280 + 512 * par]

                    @block.tensor
                    def _(te, hg=hg, tc=tc, pb=pb, par=par):
                        ctx.waitm(te, "at_wv%d" % hg)
                        if pufree[par] is not None:
                            ctx.waitm(te, pufree[par])
                        last = None
                        for fc in range(NFC):
                            last = te.matmul(
                                pb, ctx.t["xbig"][:, 2048 * fc + 128 * tc:
                                                  2048 * fc + 128 * tc + 128],
                                t["wstg2"][:, 256 * fc:256 * fc + 256],
                                start=(fc == 0), stop=(fc == NFC - 1))
                        ctx.inc(last, "ten", mark="at_vm%d_%d" % (hg, tc))

                    @block.vector
                    def _(v, hg=hg, tc=tc, pb=pb, h0=h0):
                        ctx.waitm(v, "at_vm%d_%d" % (hg, tc))
                        if hg >= 1 and tc == 0:
                            ctx.waitm(v, "at_scandone%d" % (hg - 1))
                            ctx.waitm(v, "at_scanmm%d" % (hg - 1))
                        ctx.inc(v.tensor_tensor(
                            out=t["vtok"][:, 256 * tc:256 * tc + 256], in0=pb,
                            in1=t["bvb"][:, h0 * 128:h0 * 128 + 256],
                            op=ALU.add), "ch1", mark="at_ve%d_%d" % (hg, tc))
                    pufree[tc % 2] = "at_ve%d_%d" % (hg, tc)
                ctx.setmark("at_wcons", "ten")

        for h2 in range(2):
            for tc in range(NTC):
                i2 = h2 * NTC + tc
                pb = pG[:, 1024 + 512 * (i2 % 2):1152 + 512 * (i2 % 2)]

                @block.tensor
                def _(te, hg=hg, h2=h2, tc=tc, i2=i2, pb=pb):
                    ctx.waitm(te, "at_pek%d_%d" % (hg, h2 * 4 + tc // 4))
                    if i2 < 2 and hg >= 1:
                        ctx.waitm(te, "at_oe%d_%d_%d" % (hg - 1, NTC - 1, i2))
                    if i2 >= 2:
                        ctx.waitm(te, "at_kte%d_%d" % (hg, i2 - 2))
                    ctx.inc(te.matmul(
                        pb, t["kt"][:, 2048 * h2 + 128 * tc:
                                    2048 * h2 + 128 * tc + 128],
                        t["ident"][:], start=True, stop=True), "ten",
                        mark="at_ktt%d_%d" % (hg, i2))

                @block.vector
                def _(v, hg=hg, h2=h2, tc=tc, i2=i2, pb=pb):
                    ctx.waitm(v, "at_ktt%d_%d" % (hg, i2))
                    if hg >= 1 and i2 == 0:
                        ctx.waitm(v, "at_scandone%d" % (hg - 1))
                        ctx.waitm(v, "at_scanmm%d" % (hg - 1))
                    ctx.inc(v.tensor_copy(
                        out=t["ktok"][:, 256 * tc + 128 * h2:
                                      256 * tc + 128 * h2 + 128],
                        in_=pb), "ch2", mark="at_kte%d_%d" % (hg, i2))

        for h2 in range(2):
            h = h0 + h2
            psl = pG[:, 512 * h2:512 * h2 + 128]
            for tc in range(NTC):
                buf = tc % 2

                @block.vector
                def _(v, hg=hg, h2=h2, h=h, tc=tc, buf=buf):
                    ctx.waitm(v, "at_kte%d_%d" % (hg, h2 * NTC + tc))
                    if tc >= 2:
                        ctx.waitm(v, "at_slm%d_%d_%d" % (hg, h2, tc - 2))
                    elif h2 == 1:
                        ctx.waitm(v, "at_slm%d_0_%d" % (hg, NTC - 2 + tc))
                    elif hg >= 1:
                        ctx.waitm(v, "at_slm%d_1_%d" % (hg - 1, NTC - 2 + tc))
                    ctx.inc(v.tensor_scalar(
                        out=t["ktmp"][:, 128 * buf:128 * buf + 128],
                        in0=t["ktok"][:, 256 * tc + 128 * h2:
                                      256 * tc + 128 * h2 + 128],
                        scalar1=t["khw"][:, 16 * tc + h:16 * tc + h + 1].opt(),
                        scalar2=None, op0=ALU.mult), "ch3",
                        mark="at_kh%d_%d_%d" % (hg, h2, tc))

                @block.tensor
                def _(te, hg=hg, h2=h2, tc=tc, buf=buf, psl=psl):
                    ctx.waitm(te, "at_kh%d_%d_%d" % (hg, h2, tc))
                    ctx.waitm(te, "at_ve%d_%d" % (hg, tc))
                    if tc == 0:
                        if hg == 0 and h2 == 0:
                            ctx.waitm(te, "at_msk2")
                        if hg >= 1:
                            ctx.waitm(te, "at_st%d_%d_%d" % (hg - 1, NTC - 1, h2))
                    ctx.inc(te.matmul(psl,
                                      t["ktmp"][:, 128 * buf:128 * buf + 128],
                                      t["vtok"][:, 256 * tc + 128 * h2:
                                                256 * tc + 128 * h2 + 128],
                                      start=(tc == 0), stop=(tc == NTC - 1)),
                            "ten", mark="at_slm%d_%d_%d" % (hg, h2, tc))

            @block.vector
            def _(v, hg=hg, h2=h2, psl=psl):
                ctx.waitm(v, "at_slm%d_%d_%d" % (hg, h2, NTC - 1))
                ctx.waitm(v, "at_msk2")
                if hg >= 1 and h2 == 0:
                    ctx.waitm(v, "at_snd%d" % (hg - 1))
                    ctx.waitm(v, "at_sbf%d_%d_1" % (hg - 1, NTC - 1))
                ctx.inc(v.tensor_scalar(out=t["ssend"][:, 128 * h2:128 * h2 + 128],
                                        in0=psl,
                                        scalar1=t["msb"][:, 0:1].opt(),
                                        scalar2=None, op0=ALU.mult), "ch4",
                        mark="at_ss%d_%d" % (hg, h2))

        @block.gpsimd
        def _(g, hg=hg):
            ctx.waitm(g, "at_ss%d_1" % hg)
            for h2 in range(2):
                ctx.inc(g.dma_start(
                    out=ctx.io["ccs_in"][256 * hg + 128 * h2:256 * hg + 128 * h2 + 128, :],
                    in_=t["ssend"][:, 128 * h2:128 * h2 + 128]), "dgp", 16)
            ctx.setmark("at_snd%d" % hg, "dgp")
            ctx.waitm(g, "at_snd%d" % hg)
            ctx.inc(g.collective_compute(
                "AllReduce", ALU.add,
                replica_groups=[[0, 1], [2, 3], [4, 5], [6, 7]],
                ins=[ctx.io["ccs_in"][256 * hg:256 * hg + 256, :]],
                outs=[ctx.io["ccs_out"][256 * hg:256 * hg + 256, :]]), "cc")
            ctx.wait(g, "cc")
            if hg >= 1:
                ctx.waitm(g, "at_si%d" % (hg - 1))
            for h2 in range(2):
                ctx.inc(g.dma_start(
                    out=t["srecv"][:, 128 * h2:128 * h2 + 128],
                    in_=ctx.io["ccs_out"][256 * hg + 128 * h2:256 * hg + 128 * h2 + 128, :]),
                    "dg", 16)
            ctx.setmark("at_sr%d" % hg, "dg")

        @block.vector
        def _(v, hg=hg):
            ctx.waitm(v, "at_sr%d" % hg)
            if hg >= 1:
                ctx.waitm(v, "at_sbf%d_%d_1" % (hg - 1, NTC - 1))
            ctx.inc(v.tensor_scalar(out=t["sfull"][:], in0=t["srecv"][:],
                                    scalar1=t["mrb"][:, 0:1].opt(),
                                    scalar2=None, op0=ALU.mult), "ch4",
                    mark="at_si%d" % hg)

        @block.scalar
        def _(s, hg=hg):
            ctx.waitm(s, "at_si%d" % hg)
            if hg >= 1:
                ctx.waitm(s, "at_mm23_%d_%d_1" % (hg - 1, NTC - 1))
            ctx.inc(s.copy(out=t["sbf"][:], in_=t["sfull"][:]), "scl",
                    mark="at_sbf%d_init" % hg)

        for tc in range(NTC):
            for h2 in range(2):
                h = h0 + h2
                buf = tc % 2
                pT = pG[:, 512 * h2:512 * h2 + 128]
                pO = pG[:, 1024 + 512 * h2:1152 + 512 * h2]
                pV = pU[:, 512 * h2:512 * h2 + 128]
                vtsl = t["vtok"][:, 256 * tc + 128 * h2:256 * tc + 128 * h2 + 128]
                qtsl = t["qt"][:, 2048 * h2 + 128 * tc:2048 * h2 + 128 * tc + 128]
                ktfsl = t["kt"][:, 2048 * h2 + 128 * tc:2048 * h2 + 128 * tc + 128]
                ktsl = t["ktok"][:, 256 * tc + 128 * h2:256 * tc + 128 * h2 + 128]
                ssl = slice(256 * h2 + 128 * buf, 256 * h2 + 128 * buf + 128)

                @block.tensor
                def _(te, hg=hg, tc=tc, h2=h2, pT=pT, ktfsl=ktfsl, qtsl=qtsl):
                    if tc == 0:
                        ctx.waitm(te, "at_peq%d_7" % hg)
                        ctx.waitm(te, "at_ss%d_%d" % (hg, h2))
                    else:
                        ctx.waitm(te, "at_st%d_%d_%d" % (hg, tc - 1, h2))
                    ctx.inc(te.matmul(pT, ktfsl, qtsl, start=True, stop=True),
                            "ten", mark="at_mm1_%d_%d_%d" % (hg, tc, h2))

                @block.vector
                def _(v, hg=hg, tc=tc, h2=h2, h=h, buf=buf, pT=pT, qtsl=qtsl,
                      ktsl=ktsl, ssl=ssl):
                    ctx.waitm(v, "at_mm1_%d_%d_%d" % (hg, tc, h2))
                    if tc >= 2:
                        ctx.waitm(v, "at_mm23_%d_%d_%d" % (hg, tc - 2, h2))
                        ctx.waitm(v, "at_mm4_%d_%d_%d" % (hg, tc - 2, h2))
                    elif hg >= 1:
                        ctx.waitm(v, "at_mm23_%d_%d_%d" % (hg - 1, NTC - 2 + tc, h2))
                        ctx.waitm(v, "at_mm4_%d_%d_%d" % (hg - 1, NTC - 2 + tc, h2))
                    ins = v.tensor_tensor(out=t["stsb"][:, ssl], in0=pT,
                                          in1=t["dt"][:, 128 * h:128 * h + 128],
                                          op=ALU.mult)
                    ctx.inc(ins, "ch5", mark="at_st%d_%d_%d" % (hg, tc, h2))
                    ctx.inc(v.tensor_tensor(out=t["qtb"][:, ssl], in0=qtsl,
                                            in1=t["qd"][:, 128 * h:128 * h + 128],
                                            op=ALU.mult),
                            "ch5", mark="at_qt%d_%d_%d" % (hg, tc, h2))
                    ctx.inc(v.tensor_scalar(out=t["ktmp2"][:, ssl], in0=ktsl,
                                            scalar1=t["kdv"][:, h:h + 1].opt(),
                                            scalar2=None, op0=ALU.mult),
                            "ch5", mark="at_km%d_%d_%d" % (hg, tc, h2))

                @block.tensor
                def _(te, hg=hg, tc=tc, h2=h2, buf=buf, pO=pO, vtsl=vtsl, ssl=ssl):
                    ctx.waitm(te, "at_qt%d_%d_%d" % (hg, tc, h2))
                    if tc >= 1:
                        ctx.waitm(te, "at_oe%d_%d_%d" % (hg, tc - 1, h2))
                        ctx.waitm(te, "at_sbf%d_%d_%d" % (hg, tc - 1, h2))
                    else:
                        ctx.waitm(te, "at_sbf%d_init" % hg)
                        ctx.waitm(te, "at_kte%d_%d" % (hg, 2 * NTC - 1))
                        if hg >= 1:
                            ctx.waitm(te, "at_oe%d_%d_%d" % (hg - 1, NTC - 1, h2))
                    te.matmul(pO, vtsl, t["stsb"][:, ssl], start=True, stop=False)
                    ctx.inc(te.matmul(pO, t["sbf"][:, 128 * h2:128 * h2 + 128],
                                      t["qtb"][:, ssl], start=False, stop=True),
                            "ten", mark="at_mm23_%d_%d_%d" % (hg, tc, h2))

                @block.tensor
                def _(te, hg=hg, tc=tc, h2=h2, buf=buf, pV=pV, vtsl=vtsl, ssl=ssl):
                    ctx.waitm(te, "at_km%d_%d_%d" % (hg, tc, h2))
                    if tc >= 1:
                        ctx.waitm(te, "at_sup%d_%d_%d" % (hg, tc - 1, h2))
                    elif hg >= 1:
                        ctx.waitm(te, "at_sup%d_%d_%d" % (hg - 1, NTC - 1, h2))
                    ctx.inc(te.matmul(pV, t["ktmp2"][:, ssl], vtsl,
                                      start=True, stop=True), "ten",
                            mark="at_mm4_%d_%d_%d" % (hg, tc, h2))

                @block.scalar
                def _(s, hg=hg, tc=tc, h2=h2, pO=pO, h=h):
                    ctx.waitm(s, "at_mm23_%d_%d_%d" % (hg, tc, h2))
                    ctx.inc(s.copy(out=ctx.t["att"][:, 2048 * h + 128 * tc:
                                                    2048 * h + 128 * tc + 128],
                                   in_=pO), "scl",
                            mark="at_oe%d_%d_%d" % (hg, tc, h2))

                @block.vector
                def _(v, hg=hg, tc=tc, h2=h2, h=h, pV=pV):
                    ctx.waitm(v, "at_mm4_%d_%d_%d" % (hg, tc, h2))
                    ctx.inc(v.scalar_tensor_tensor(
                        out=t["sfull"][:, 128 * h2:128 * h2 + 128],
                        in0=t["sfull"][:, 128 * h2:128 * h2 + 128],
                        scalar=t["dhv"][:, h:h + 1].opt(), in1=pV,
                        op0=ALU.mult, op1=ALU.add), "ch4",
                        mark="at_sup%d_%d_%d" % (hg, tc, h2))

                @block.scalar
                def _(s, hg=hg, tc=tc, h2=h2):
                    ctx.waitm(s, "at_sup%d_%d_%d" % (hg, tc, h2))
                    ctx.waitm(s, "at_mm23_%d_%d_%d" % (hg, tc, h2))
                    ctx.inc(s.copy(out=t["sbf"][:, 128 * h2:128 * h2 + 128],
                                   in_=t["sfull"][:, 128 * h2:128 * h2 + 128]),
                            "scl", mark="at_sbf%d_%d_%d" % (hg, tc, h2))

        ctx.setmark("at_scandone%d" % hg, "ch5")
        ctx.setmark("at_scanmm%d" % hg, "ten")


def emit_wout(ctx):
    block, t = ctx.block, ctx.t
    pG = ctx.ps["pG"]

    @block.gpsimd
    def _(g):
        ctx.inc(g.dma_start(
            out=ctx.t["xbig"][:].rearrange("p (fc o) -> p fc o", fc=NFC),
            in_=ctx.io["wout"].rearrange("(fc p) o -> p fc o", p=128),
            max_dma_last_dim=8192), "dgp", 16, mark="w3_w")

    bcast_dram_row(ctx, ctx.io["bout_r"], "bvb", "w3b")

    for tc in range(NTC):
        buf = tc % 2

        @block.sync
        def _(sy, tc=tc, buf=buf):
            if tc >= 2:
                ctx.waitm(sy, "w3_e%d" % ((tc - 2) * 4 + 3))
            ctx.inc(sy.dma_start(out=t["xst3"][:, 2048 * buf:2048 * buf + 2048],
                                 in_=ctx.io["xtok"][128 * tc:128 * tc + 128, :]),
                    "da" if buf == 0 else "db", 16, mark="w3_xd%d" % tc)

        for fg in range(4):
            gi = tc * 4 + fg
            pb = pG[:, 512 * (gi % 2):512 * (gi % 2) + 512]

            @block.tensor
            def _(te, tc=tc, fg=fg, gi=gi, pb=pb):
                if gi == 0:
                    ctx.waitm(te, "w3_w")
                if gi >= 2:
                    ctx.waitm(te, "w3_e%d" % (gi - 2))
                last = None
                for fc in range(NFC):
                    last = te.matmul(
                        pb, ctx.t["att"][:, 2048 * fc + 128 * tc:
                                         2048 * fc + 128 * tc + 128],
                        ctx.t["xbig"][:, 2048 * fc + 512 * fg:
                                      2048 * fc + 512 * fg + 512],
                        start=(fc == 0), stop=(fc == NFC - 1))
                ctx.inc(last, "ten", mark="w3_m%d" % gi)

            @block.vector
            def _(v, tc=tc, fg=fg, gi=gi, pb=pb, buf=buf):
                ctx.waitm(v, "w3_m%d" % gi)
                if fg == 0:
                    ctx.waitm(v, "w3_xd%d" % tc)
                    if tc == 0:
                        ctx.waitm(v, "bc_bvb")
                    if tc >= 2:
                        ctx.waitm(v, "w3_od%d" % (tc - 2))
                ins = v.tensor_tensor(out=t["ot3"][:, 2048 * buf + 512 * fg:
                                                   2048 * buf + 512 * fg + 512],
                                      in0=pb, in1=t["bvb"][:, 512 * fg:512 * fg + 512],
                                      op=ALU.add)
                ctx.fence(v, ins, "fv")
                ctx.inc(v.tensor_tensor(
                    out=t["ot3"][:, 2048 * buf + 512 * fg:2048 * buf + 512 * fg + 512],
                    in0=t["ot3"][:, 2048 * buf + 512 * fg:2048 * buf + 512 * fg + 512],
                    in1=t["xst3"][:, 2048 * buf + 512 * fg:2048 * buf + 512 * fg + 512],
                    op=ALU.add), "ch1", mark="w3_e%d" % gi)

        @block.gpsimd
        def _(g, tc=tc, buf=buf):
            ctx.waitm(g, "w3_e%d" % (tc * 4 + 3))
            ctx.inc(g.dma_start(out=ctx.io["out"][128 * tc:128 * tc + 128, :],
                                in_=t["ot3"][:, 2048 * buf:2048 * buf + 2048],
                                accum_op=ALU.add), "di" if buf == 0 else "dj",
                    16, mark="w3_od%d" % tc)


def declare_io(nc, io, debug=False):
    io["xtok"] = nc.dram_tensor("xtok", [T, D], F32, kind="ExternalInput").ap()
    io["xT"] = nc.dram_tensor("xT", [D, T], F32, kind="ExternalInput").ap()
    io["wqkv"] = nc.dram_tensor("wqkv", [D, 3 * D], F32, kind="ExternalInput").ap()
    io["wout"] = nc.dram_tensor("wout", [D, D], F32, kind="ExternalInput").ap()
    io["wgS"] = nc.dram_tensor("wgS", [NIC, 128, D], F32, kind="ExternalInput").ap()
    io["wuS"] = nc.dram_tensor("wuS", [NIC, 128, D], F32, kind="ExternalInput").ap()
    io["wdT"] = nc.dram_tensor("wdT", [FI, D], F32, kind="ExternalInput").ap()
    io["gsub_t"] = nc.dram_tensor("gsub_t", [128, 64], F32, kind="ExternalInput").ap()
    io["wmsl"] = nc.dram_tensor("wmsl", [3, 128, 16384], F32, kind="ExternalInput").ap()
    io["bqkv_t"] = nc.dram_tensor("bqkv_t", [128, 48], F32, kind="ExternalInput").ap()
    io["slopes_r"] = nc.dram_tensor("slopes_r", [1, 16], F32, kind="ExternalInput").ap()
    io["bqv"] = nc.dram_tensor("bqv", [1, D], F32, kind="ExternalInput").ap()
    io["bout_r"] = nc.dram_tensor("bout_r", [1, D], F32, kind="ExternalInput").ap()
    io["mrecv"] = nc.dram_tensor("mrecv", [1, 1], F32, kind="ExternalInput").ap()
    io["msend"] = nc.dram_tensor("msend", [1, 1], F32, kind="ExternalInput").ap()
    io["out"] = nc.dram_tensor("out", [T, D], F32, kind="ExternalOutput").ap()
    io["rgd"] = nc.dram_tensor("rgd", [FI, T], F32).ap()
    io["wdq"] = nc.dram_tensor("wdq", [FI, D], BF16).ap()
    io["ccw_in"] = nc.dram_tensor("ccw_in", [1, 8], F32).ap()
    io["ccs_in"] = nc.dram_tensor("ccs_in", [2048, 128], F32).ap()
    io["ccs_out"] = nc.dram_tensor("ccs_out", [2048, 128], F32).ap()
    io["ccw_out"] = nc.dram_tensor("ccw_out", [1, 8], F32).ap()
    if debug:
        io["dbg_xq"] = nc.dram_tensor("dbg_xq", [128, 16 * T], BF16,
                                      kind="ExternalOutput").ap()
        io["dbg_st"] = nc.dram_tensor("dbg_st", [128, 16 * 16], F32,
                                      kind="ExternalOutput").ap()
        io["dbg_rgd"] = nc.dram_tensor("dbg_rgd", [FI, T], F32,
                                       kind="ExternalOutput").ap()
        io["dbg_qscb"] = nc.dram_tensor("dbg_qscb", [128, T], F32,
                                        kind="ExternalOutput").ap()


def alloc_common(ctx, es):
    a = ctx.alloc
    a(es, "xbig", [128, 16 * T], BF16)
    a(es, "ones1f", [1, 128], F32); a(es, "onescf", [128, 1], F32)
    a(es, "identf", [128, 128], F32)
    a(es, "iota_p", [128, 1], F32)
    a(es, "c_v1", [128, 4], F32)
    a(es, "c_t16a", [128, 16], F32); a(es, "c_t16b", [128, 16], F32)
    a(es, "c_t16x", [16, 128], F32)
    a(es, "ident", [128, 128], BF16)
    a(es, "slr", [1, 16], F32); a(es, "slb", [128, 16], F32)
    a(es, "nslb", [128, 16], F32)
    a(es, "dhv", [128, 16], F32); a(es, "kdv", [128, 16], F32)
    a(es, "qdt", [128, 16], F32)
    a(es, "khw", [128, 256], F32)
    a(es, "bqkvs", [128, 48], F32); a(es, "gsub", [128, 64], F32)
    a(es, "bqvr", [1, D], F32); a(es, "boutr", [1, D], F32)
    a(es, "mrecvs", [1, 1], F32); a(es, "msends", [1, 1], F32)
    a(es, "swb", [128, 16], F32)
    a(es, "msum", [128, 16], F32); a(es, "sqsum", [128, 16], F32)
    a(es, "m_t", [128, 16], F32); a(es, "rstd_t", [128, 16], F32)
    a(es, "mx_t", [128, 16], F32); a(es, "sx_t", [128, 16], F32)
    a(es, "invsx_t", [128, 16], F32)
    a(es, "st_a", [128, 16], F32); a(es, "st_b", [128, 16], F32)
    a(es, "st_c", [128, 16], F32); a(es, "st_d", [128, 16], F32)
    a(es, "mxz_t", [128, 16], F32); a(es, "sq_t", [128, 16], F32)
    a(es, "rstdz_t", [128, 16], F32); a(es, "sz_t", [128, 16], F32)
    a(es, "qsc_t", [128, 16], F32); a(es, "invsd_t", [128, 16], F32)
    a(es, "cpool", [128, 2], F32)
    ctx.nc.const_aps.register(F32, LN_EPS, ctx.t["cpool"][:, 0:1])
    ctx.nc.const_aps.register(F32, SUB_EPS, ctx.t["cpool"][:, 1:2])


def build_full(debug=False, upto=99):
    nc = bass.Bass("TRN2", target_bir_lowering=False)
    es = ExitStack()
    io = {}
    declare_io(nc, io, debug)
    block = es.enter_context(nc.Block())
    ctx = Ctx(nc, block, es)
    ctx.io = io
    for s in ["dsy", "dgp", "dsc", "ten", "vec", "scl", "gps", "cc",
              "ch0", "ch1", "ch2", "ch3", "ch4", "ch5", "fv", "fs", "fg",
              "da", "db", "dc", "dd", "de", "df", "dg", "dh", "di", "dj"]:
        ctx.add_sem(s)
    for e in ENGS:
        ctx.add_sem("b_" + e)

    ctx.ps["pG"] = es.enter_context(nc.psum_tensor("pG", [128, 2048], F32))
    ctx.ps["pU"] = es.enter_context(nc.psum_tensor("pU", [128, 2048], F32))
    ctx.ps["pD"] = ctx.ps["pG"]
    ctx.ps["pA"] = ctx.ps["pG"]
    ctx.ps["pB"] = ctx.ps["pU"]

    alloc_common(ctx, es)
    a = ctx.alloc

    es_early = ExitStack()
    a(es_early, "c_qdrow", [1, 2048], F32)
    a(es_early, "ti32", [128, 2048], I32)

    with ExitStack() as es_c:
        a(es_c, "c_row", [128, 128], F32); a(es_c, "c_col", [128, 128], F32)
        a(es_c, "c_ige", [128, 128], F32); a(es_c, "c_diff", [128, 128], F32)
        a(es_c, "c_dte", [128, 256], F32)
        if upto >= 0:
            emit_const(ctx)
        ctx.barrier()

    with ExitStack() as es_w:
        with ExitStack() as es_wm:
            a(es_wm, "wm_stage", [128, 8192], F32)
            a(es_wm, "wm_acc", [128, 16], F32)
            a(es_wm, "wm_p", [128, 4], F32); a(es_wm, "wm_r", [1, 16], F32)
            if upto >= 1:
                emit_wmean(ctx)
            ctx.barrier()

        with ExitStack() as es_p0:
            a(es_p0, "rrow", [1, 2048], F32)
            a(es_p0, "xh", [128, 16 * 1024], F32)
            a(es_p0, "mxa", [128, 1024], F32)
            ctx.t["sbq"] = ctx.t["mxa"]
            a(es_p0, "mbf", [128, 2048], F32)
            a(es_p0, "rbf", [128, 2048], F32)
            if upto >= 2:
                emit_p0(ctx)
            ctx.barrier()
            if debug:
                @block.sync
                def _(sy):
                    ctx.inc(sy.dma_start(out=io["dbg_xq"], in_=ctx.t["xbig"][:]),
                            "dsy", 16)
                ctx.barrier()

    with ExitStack() as es_q:
        a(es_q, "qscb", [128, 2048], F32)
        with ExitStack() as es_gu:
            a(es_gu, "gsq", [128, 64], F32)
            a(es_gu, "wstg_g", [128, 4096], F32); a(es_gu, "wstg_u", [128, 4096], F32)
            a(es_gu, "wqg", [128, 4096], BF16); a(es_gu, "wqu", [128, 4096], BF16)
            a(es_gu, "mxz", [128, 2048], F32); a(es_gu, "sq", [128, 2048], F32)
            a(es_gu, "gu_t2", [128, 1024], F32)
            a(es_gu, "gu_r", [128, 1024], F32); a(es_gu, "gu_t3", [128, 1024], F32)
            a(es_gu, "rgt", [128, 2048], F32)
            a(es_gu, "wdstg", [128, 2048], F32)
            a(es_gu, "wdqb", [128, 2048], BF16)
            if upto >= 3:
                emit_gateup(ctx)
            ctx.barrier()

        with ExitStack() as es_dn:
            a(es_dn, "rstg", [128, 1024], F32)
            a(es_dn, "tiq", [128, 1024], I32)
            a(es_dn, "wdt2", [128, 2048], BF16)
            a(es_dn, "ot", [128, 8192], F32)
            if upto >= 4:
                emit_down(ctx)
            ctx.barrier()
        if debug:
            @block.sync
            def _(sy):
                ctx.inc(sy.dma_start(out=io["dbg_qscb"], in_=ctx.t["qscb"][:]),
                        "dsy", 16)
            ctx.barrier()
    es_early.close()

    with ExitStack() as es_xn:
        a(es_xn, "xst2", [128, 4096], F32)
        a(es_xn, "mbf2", [128, 2048], F32)
        a(es_xn, "rbf2", [128, 2048], F32)
        a(es_xn, "c_qdrow2", [1, 2048], F32)
        a(es_xn, "rrow2", [1, 2048], F32)
        if upto >= 5:
            emit_xn(ctx)
        ctx.barrier()

    with ExitStack() as es_late:
        a(es_late, "att", [128, 16 * T], BF16)
        a(es_late, "bqvr2", [1, 512], F32)
        a(es_late, "bvb", [128, 2048], BF16)
        a(es_late, "qd", [128, 2048], BF16)
        a(es_late, "dt", [128, 2048], BF16)
        with ExitStack() as es_d:
            a(es_d, "d_row", [128, 128], F32)
            a(es_d, "d_col", [128, 128], F32)
            a(es_d, "d_ige", [128, 128], F32)
            a(es_d, "d_diff", [128, 128], F32)
            a(es_d, "d_dte", [128, 256], F32)
            if upto >= 6:
                emit_decay(ctx)
            ctx.barrier()
        with ExitStack() as es_at:
            a(es_at, "wstg2", [128, 4096], BF16)
            a(es_at, "qt", [128, 4096], BF16)
            a(es_at, "kt", [128, 4096], BF16)
            a(es_at, "vtok", [128, 4096], BF16)
            a(es_at, "ktok", [128, 4096], BF16)
            a(es_at, "ssend", [128, 256], F32)
            ctx.t["srecv"] = ctx.t["ssend"]
            ctx.t["sfull"] = ctx.t["ssend"]
            a(es_at, "sbf", [128, 256], BF16)
            a(es_at, "stsb", [128, 512], BF16)
            a(es_at, "qtb", [128, 512], BF16)
            a(es_at, "ktmp2", [128, 512], BF16)
            ctx.t["ktmp"] = ctx.t["ktmp2"]
            a(es_at, "msb", [128, 1], F32)
            a(es_at, "mrb", [128, 1], F32)
            if upto >= 6:
                emit_attn(ctx)
            ctx.barrier()

        with ExitStack() as es_w3:
            a(es_w3, "xst3", [128, 4096], F32)
            a(es_w3, "ot3", [128, 4096], F32)
            if upto >= 7:
                emit_wout(ctx)
            ctx.barrier()

    if debug:
        @block.sync
        def _(sy):
            for i, n in enumerate(["m_t", "rstd_t", "mx_t", "sx_t", "invsx_t",
                                   "swb", "mxz_t", "sq_t", "rstdz_t", "sz_t",
                                   "qsc_t", "invsd_t"]):
                ctx.inc(sy.dma_start(out=io["dbg_st"][:, 16 * i:16 * i + 16],
                                     in_=ctx.t[n][:]), "dsy", 16)
            ctx.inc(sy.dma_start(out=io["dbg_rgd"], in_=io["rgd"]), "dsy", 16)

    @block.sync
    def _(sy):
        for s_ in ["dsy", "da", "db", "dc", "dd", "de", "df", "dg"]:
            ctx.wait(sy, s_)
    @block.vector
    def _(v):
        for s_ in ["dh", "di", "dj"]:
            ctx.wait(v, s_)
    @block.gpsimd
    def _(g):
        ctx.wait(g, "dgp")
    @block.scalar
    def _(s):
        ctx.wait(s, "dsc")

    es.close()
    return nc


def shard_inputs(L0_attn_x, L0_attn_Wqkv, L0_attn_bqkv, L0_attn_Wout,
                 L0_attn_bout, L0_attn_slopes, L0_ffn_w_gate, L0_ffn_w_up,
                 L0_ffn_w_down, L0_ffn_subln_weight):
    f32 = np.float32
    wgS = np.ascontiguousarray(np.asarray(L0_ffn_w_gate, f32).T
                               .reshape(16, 128, 64, 128).transpose(2, 1, 0, 3)
                               .reshape(64, 128, 2048))
    wuS = np.ascontiguousarray(np.asarray(L0_ffn_w_up, f32).T
                               .reshape(16, 128, 64, 128).transpose(2, 1, 0, 3)
                               .reshape(64, 128, 2048))
    wdT = np.ascontiguousarray(np.asarray(L0_ffn_w_down, f32).T)
    gsub_t = np.ascontiguousarray(np.asarray(L0_ffn_subln_weight, f32).reshape(64, 128).T)
    bqkv = np.asarray(L0_attn_bqkv, f32)
    bqkv_t = np.ascontiguousarray(bqkv.reshape(48, 128).T)
    slopes_r = np.ascontiguousarray(np.asarray(L0_attn_slopes, f32).reshape(1, 16))
    bqv = np.ascontiguousarray(bqkv[4096:].reshape(1, D))
    bout_r = np.ascontiguousarray(np.asarray(L0_attn_bout, f32).reshape(1, D))
    wg_f = np.asarray(L0_ffn_w_gate, f32).reshape(-1)
    wu_f = np.asarray(L0_ffn_w_up, f32).reshape(-1)
    wd_f = np.asarray(L0_ffn_w_down, f32).reshape(-1)
    x = np.asarray(L0_attn_x, f32)
    maps = []
    n8 = 2097152
    for c in range(8):
        b, hf = c // 2, c % 2
        xs = np.ascontiguousarray(x[b, 2048 * hf:2048 * hf + 2048, :])
        xTs = np.ascontiguousarray(xs.T)
        wmsl = np.ascontiguousarray(np.stack([
            wg_f[c * n8:(c + 1) * n8].reshape(128, 16384),
            wu_f[c * n8:(c + 1) * n8].reshape(128, 16384),
            wd_f[c * n8:(c + 1) * n8].reshape(128, 16384)]))
        maps.append({
            "xtok": xs, "xT": xTs, "wgS": wgS, "wuS": wuS, "wdT": wdT,
            "wqkv": np.ascontiguousarray(np.asarray(L0_attn_Wqkv, f32)),
            "wout": np.ascontiguousarray(np.asarray(L0_attn_Wout, f32)),
            "gsub_t": gsub_t, "wmsl": wmsl,
            "bqkv_t": bqkv_t, "slopes_r": slopes_r, "bqv": bqv,
            "bout_r": bout_r,
            "mrecv": np.array([[float(hf)]], dtype=f32),
            "msend": np.array([[float(1 - hf)]], dtype=f32),
        })
    return maps


_NC_CACHE = {}


def kernel(**inputs):
    if "nc" not in _NC_CACHE:
        _NC_CACHE["nc"] = build_full(debug=False)
    nc = _NC_CACHE["nc"]
    maps = shard_inputs(**inputs)
    res = run_bass_kernel_spmd(nc, maps, core_ids=list(range(8)))
    out = np.concatenate([np.asarray(res.results[c]["out"], dtype=np.float32)
                          .reshape(1, T, D) for c in range(8)])
    return out.reshape(4, 4096, D)


# version 04
# speedup vs baseline: 1.0068x; 1.0068x over previous
import numpy as np
from contextlib import ExitStack

import concourse.bass as bass
import concourse.mybir as mybir
from concourse.bass_utils import run_bass_kernel_spmd

F32 = mybir.dt.float32
BF16 = mybir.dt.bfloat16
I32 = mybir.dt.int32
AF = mybir.ActivationFunctionType
ALU = mybir.AluOpType
AX = mybir.AxisListType

D = 2048
T = 2048
HEADS = 16
HD = 128
FI = 8192
NFC = D // 128
NTC = T // 128
NIC = FI // 128
NHG = HEADS // 2
LN_EPS = 1e-5
SUB_EPS = 1e-6

ENGS = ["sync", "gpsimd", "vector", "scalar", "tensor"]


class Ctx:
    def __init__(self, nc, block, es):
        self.nc, self.block, self.es = nc, block, es
        self.sem, self.cnt = {}, {}
        self.mark = {}
        self.io = {}
        self.ps = {}
        self.t = {}

    def add_sem(self, name):
        self.sem[name] = self.es.enter_context(self.nc.semaphore(name))
        self.cnt[name] = 0

    def inc(self, instr, name, amt=1, mark=None):
        instr.then_inc(self.sem[name], amt)
        self.cnt[name] += amt
        if mark is not None:
            self.mark[mark] = (name, self.cnt[name])
        return self.cnt[name]

    def setmark(self, mark, name):
        self.mark[mark] = (name, self.cnt[name])

    def wait(self, eng, name, value=None):
        eng.wait_ge(self.sem[name], self.cnt[name] if value is None else value)

    def waitm(self, eng, mark):
        name, val = self.mark[mark]
        eng.wait_ge(self.sem[name], val)

    def fence(self, eng, instr, name):
        self.inc(instr, name)
        self.wait(eng, name)

    def barrier(self):
        vals = {}
        allsems = ["dsy", "dgp", "dsc", "ten", "vec", "scl", "gps", "cc",
                   "ch0", "ch1", "ch2", "ch3", "ch4", "ch5", "fv", "fs", "fg",
                   "da", "db", "dc", "dd", "de", "df", "dg", "dh", "di", "dj"]
        cur = {s: self.cnt[s] for s in allsems if s in self.cnt}
        for e in ENGS:
            self.cnt["b_" + e] += 1
            vals[e] = self.cnt["b_" + e]

        def mk(e):
            def _(eng):
                for s, v in cur.items():
                    if v:
                        eng.wait_ge(self.sem[s], v)
                eng.sem_inc(self.sem["b_" + e], 1)
                for o in ENGS:
                    if o != e:
                        eng.wait_ge(self.sem["b_" + o], vals[o])
            return _
        for e in ENGS:
            getattr(self.block, e)(mk(e))

    def alloc(self, es, name, shape, dtype):
        t = es.enter_context(self.nc.sbuf_tensor(name, shape, dtype))
        self.t[name] = t
        return t



def nrecip(ctx, v, out_ap, in_ap, tmp_ap):
    ins = v.reciprocal(out=out_ap, in_=in_ap)
    ctx.fence(v, ins, "fv")
    ins = v.tensor_tensor(out=tmp_ap, in0=out_ap, in1=in_ap, op=ALU.mult)
    ctx.fence(v, ins, "fv")
    ins = v.tensor_scalar(out=tmp_ap, in0=tmp_ap, scalar1=-1.0, scalar2=2.0,
                          op0=ALU.mult, op1=ALU.add)
    ctx.fence(v, ins, "fv")
    ins = v.tensor_tensor(out=out_ap, in0=out_ap, in1=tmp_ap, op=ALU.mult)
    ctx.fence(v, ins, "fv")



def bcast_dram_row(ctx, io_ap, dstname, tag):
    block, t = ctx.block, ctx.t
    for k in range(4):
        @block.sync
        def _(sy, k=k):
            if k > 0:
                ctx.waitm(sy, "bdm_%s%d" % (tag, k - 1))
            ctx.inc(sy.dma_start(out=t["bqvr2"][0:1, 0:512],
                                 in_=io_ap[0:1, 512 * k:512 * k + 512]),
                    "dc", 16, mark="bdr_%s%d" % (tag, k))

        @block.tensor
        def _(te, k=k):
            ctx.waitm(te, "bdr_%s%d" % (tag, k))
            if k == 0 and "bc_pBfree" in ctx.mark:
                ctx.waitm(te, "bc_pBfree")
            if k > 0:
                ctx.waitm(te, "bde_%s%d" % (tag, k - 1))
            ctx.inc(te.matmul(ctx.ps["pB"][:, 0:512], t["ones1f"][:],
                              t["bqvr2"][0:1, 0:512], start=True, stop=True),
                    "ten", mark="bdm_%s%d" % (tag, k))

        @block.scalar
        def _(s, k=k):
            ctx.waitm(s, "bdm_%s%d" % (tag, k))
            ctx.inc(s.copy(out=t[dstname][:, 512 * k:512 * k + 512],
                           in_=ctx.ps["pB"][:, 0:512]), "scl",
                    mark="bde_%s%d" % (tag, k))
    ctx.setmark("bc_%s" % dstname, "scl")
    ctx.setmark("bc_pBfree", "scl")


def emit_const(ctx):
    block, t = ctx.block, ctx.t

    @block.sync
    def _(sy):
        for name, dst in [("bqkv_t", t["bqkvs"]), ("gsub_t", t["gsub"]),
                          ("slopes_r", t["slr"]), ("mrecv", t["mrecvs"]),
                          ("msend", t["msends"])]:
            ctx.inc(sy.dma_start(out=dst[:], in_=ctx.io[name]), "dsy", 16)
        ctx.setmark("c_loads", "dsy")

    @block.gpsimd
    def _(g):
        ctx.inc(g.iota(out=t["iota_p"][:], pattern=[[0, 1]], base=0,
                       channel_multiplier=1, allow_small_or_imprecise_dtypes=True),
                "gps")
        ctx.inc(g.iota(out=t["c_row"][:], pattern=[[1, 128]], base=0,
                       channel_multiplier=0, allow_small_or_imprecise_dtypes=True),
                "gps")
        ctx.inc(g.iota(out=t["c_col"][:], pattern=[[0, 128]], base=0,
                       channel_multiplier=1,
                       allow_small_or_imprecise_dtypes=True), "gps", mark="c_iota")

    @block.vector
    def _(v):
        ctx.inc(v.memset(t["ones1f"][:], 1.0), "vec")
        ctx.inc(v.memset(t["onescf"][:], 1.0), "vec")
        ctx.inc(v.memset(t["cpool"][:, 0:1], LN_EPS), "vec")
        ctx.inc(v.memset(t["cpool"][:, 1:2], SUB_EPS), "vec")
        ctx.waitm(v, "c_iota")
        ctx.inc(v.tensor_tensor(out=t["identf"][:], in0=t["c_row"][:],
                                in1=t["c_col"][:], op=ALU.is_equal), "vec")
        ctx.inc(v.tensor_tensor(out=t["ident"][:], in0=t["c_row"][:],
                                in1=t["c_col"][:], op=ALU.is_equal), "vec",
                mark="c_base")

    @block.tensor
    def _(te):
        ctx.waitm(te, "c_loads")
        ctx.waitm(te, "c_base")
        ctx.inc(te.matmul(ctx.ps["pA"][:, 0:16], t["ones1f"][:], t["slr"][:],
                          start=True, stop=True), "ten", mark="c_slmm")

    @block.scalar
    def _(s):
        ctx.waitm(s, "c_slmm")
        ctx.fence(s, s.copy(out=t["slb"][:], in_=ctx.ps["pA"][:, 0:16]), "fs")
        ctx.inc(s.activation(out=t["dhv"][:], in_=t["slb"][:], func=AF.Exp,
                             scale=-128.0), "scl", mark="c_slb")

    @block.vector
    def _(v):
        ctx.waitm(v, "c_slb")
        for tc in range(NTC):
            ctx.inc(v.tensor_scalar(out=t["st_a"][:, tc:tc + 1], in0=t["iota_p"][:],
                                    scalar1=float(128 * tc - 2048), scalar2=None,
                                    op0=ALU.add), "vec")
        ins = v.tensor_scalar(out=t["st_b"][:, 0:1], in0=t["iota_p"][:],
                              scalar1=-128.0, scalar2=None, op0=ALU.add)
        ctx.fence(v, ins, "fv")
        ctx.inc(v.tensor_scalar(out=t["c_t16a"][:], in0=t["slb"][:],
                                scalar1=t["st_b"][:, 0:1].opt(), scalar2=None,
                                op0=ALU.mult), "vec")
        ctx.inc(v.tensor_scalar(out=t["c_t16b"][:], in0=t["slb"][:],
                                scalar1=t["iota_p"][:, 0:1].opt(), scalar2=-1.0,
                                op0=ALU.mult, op1=ALU.mult), "vec")
        for tc in range(NTC):
            ctx.inc(v.tensor_scalar(out=t["khw"][:, 16 * tc:16 * tc + 16],
                                    in0=t["slb"][:],
                                    scalar1=t["st_a"][:, tc:tc + 1].opt(),
                                    scalar2=None, op0=ALU.mult), "vec")
        ctx.inc(v.tensor_scalar(out=t["nslb"][:], in0=t["slb"][:], scalar1=-1.0,
                                scalar2=None, op0=ALU.mult), "vec", mark="c_pre")

    @block.scalar
    def _(s):
        ctx.waitm(s, "c_pre")
        ctx.inc(s.activation(out=t["kdv"][:], in_=t["c_t16a"][:], func=AF.Exp),
                "scl")
        ctx.inc(s.activation(out=t["qdt"][:], in_=t["c_t16b"][:], func=AF.Exp),
                "scl")
        ctx.inc(s.activation(out=t["khw"][:], in_=t["khw"][:], func=AF.Exp),
                "scl", mark="c_exps")



def bcast_row(ctx, rowname, rowoff, width, dstname, wait_mark=None, dtype_note=""):
    block, t = ctx.block, ctx.t
    nk = width // 512
    for k in range(nk):
        @block.tensor
        def _(te, k=k):
            if k == 0 and wait_mark is not None:
                ctx.waitm(te, wait_mark)
            if k == 0 and "bc_pBfree" in ctx.mark:
                ctx.waitm(te, "bc_pBfree")
            if k > 0:
                ctx.waitm(te, "bc_%s%d" % (dstname, k - 1))
            ctx.inc(te.matmul(ctx.ps["pB"][:, 0:512], t["ones1f"][:],
                              t[rowname][0:1, rowoff + 512 * k:rowoff + 512 * k + 512],
                              start=True, stop=True), "ten",
                    mark="bcm_%s%d" % (dstname, k))

        @block.scalar
        def _(s, k=k):
            ctx.waitm(s, "bcm_%s%d" % (dstname, k))
            ctx.inc(s.copy(out=t[dstname][:, 512 * k:512 * k + 512],
                           in_=ctx.ps["pB"][:, 0:512]), "scl",
                    mark="bc_%s%d" % (dstname, k))
    ctx.setmark("bc_%s" % dstname, "scl")
    ctx.setmark("bc_pBfree", "scl")


def row_from_cols(ctx, src_ap, ncols, rowname, rowoff, tag):
    block, t = ctx.block, ctx.t

    @block.tensor
    def _(te):
        ctx.wait(te, "vec")
        ctx.wait(te, "scl")
        ctx.inc(te.transpose(out=ctx.ps["pA"][0:ncols, 0:128], in_=src_ap,
                             identity=t["identf"][:]), "ten", mark="rf_t" + tag)

    @block.scalar
    def _(s):
        ctx.waitm(s, "rf_t" + tag)
        ctx.wait(s, "dg")
        ctx.inc(s.copy(out=t["c_t16x"][0:ncols, :], in_=ctx.ps["pA"][0:ncols, 0:128]),
                "scl", mark="rf_c" + tag)

    @block.gpsimd
    def _(g):
        ctx.waitm(g, "rf_c" + tag)
        ctx.inc(g.dma_start(out=t[rowname][0:1, rowoff:rowoff + 128 * ncols],
                            in_=t["c_t16x"][0:ncols, :]), "dg", 16, mark="rf_d" + tag)


def emit_wmean(ctx):
    block, t = ctx.block, ctx.t
    stg, acc = t["wm_stage"], t["wm_acc"]

    for idx in range(12):
        w, k, buf = idx // 4, idx % 4, idx % 2

        @block.sync
        def _(sy, idx=idx, w=w, k=k, buf=buf):
            if idx >= 2:
                ctx.waitm(sy, "wm_a%d" % (idx - 2))
            ctx.inc(sy.dma_start(out=stg[:, 4096 * buf:4096 * buf + 4096],
                                 in_=ctx.io["wmsl"][w, :, 4096 * k:4096 * k + 4096]),
                    "da" if buf == 0 else "db", 16, mark="wm_d%d" % idx)

        @block.scalar
        def _(s, idx=idx, buf=buf):
            ctx.waitm(s, "wm_d%d" % idx)
            sl = stg[:, 4096 * buf:4096 * buf + 4096]
            ctx.inc(s.activation(out=sl, in_=sl, func=AF.Abs,
                                 accum_out=acc[:, idx:idx + 1]),
                    "scl", mark="wm_a%d" % idx)

    @block.vector
    def _(v):
        ctx.waitm(v, "wm_a11")
        for w in range(3):
            ctx.inc(v.tensor_reduce(out=t["wm_p"][:, w:w + 1], in_=acc[:, 4 * w:4 * w + 4],
                                    axis=AX.X, op=ALU.add), "vec")
        ctx.inc(v.memset(t["wm_p"][:, 3:4], 0.0), "vec", mark="wm_p")

    @block.tensor
    def _(te):
        ctx.waitm(te, "wm_p")
        ctx.inc(te.matmul(ctx.ps["pA"][0:1, 0:4], t["onescf"][:], t["wm_p"][:, 0:4],
                          start=True, stop=True), "ten", mark="wm_mm")

    @block.scalar
    def _(s):
        ctx.waitm(s, "wm_mm")
        ctx.fence(s, s.copy(out=t["wm_r"][0:1, 0:4], in_=ctx.ps["pA"][0:1, 0:4]),
                  "fs")
        ctx.inc(s.memzero(t["wm_r"][0:1, 4:8]), "scl", mark="wm_r0")

    @block.gpsimd
    def _(g):
        ctx.waitm(g, "wm_r0")
        ctx.inc(g.dma_start(out=ctx.io["ccw_in"], in_=t["wm_r"][0:1, 0:8]),
                "dgp", 16)
        ctx.wait(g, "dgp")
        ctx.inc(g.collective_compute(
            "AllReduce", ALU.add, replica_groups=[list(range(8))],
            ins=[ctx.io["ccw_in"]], outs=[ctx.io["ccw_out"]]), "cc")
        ctx.wait(g, "cc")
        ctx.inc(g.dma_start(out=t["wm_r"][0:1, 0:8], in_=ctx.io["ccw_out"]),
                "dgp", 16, mark="wm_rb")

    @block.vector
    def _(v):
        ctx.waitm(v, "wm_rb")
        ins = v.tensor_scalar(out=t["wm_r"][0:1, 4:7], in0=t["wm_r"][0:1, 0:3],
                              scalar1=1.0 / 16777216.0, scalar2=1e-5, op0=ALU.mult,
                              op1=ALU.max)
        ctx.fence(v, ins, "fv")
        ctx.inc(v.memset(t["wm_r"][0:1, 7:8], 1.0), "vec")
        ctx.inc(v.memset(t["wm_r"][0:1, 11:16], 1.0), "vec")
        nrecip(ctx, v, t["wm_r"][0:1, 8:11], t["wm_r"][0:1, 4:7],
               t["wm_r"][0:1, 12:15])
        ctx.inc(v.memset(t["wm_r"][0:1, 11:16], 1.0), "vec", mark="wm_sc")

    @block.tensor
    def _(te):
        ctx.waitm(te, "wm_sc")
        ctx.inc(te.matmul(ctx.ps["pA"][:, 0:16], t["ones1f"][:], t["wm_r"][0:1, 0:16],
                          start=True, stop=True), "ten", mark="wm_bm")

    @block.scalar
    def _(s):
        ctx.waitm(s, "wm_bm")
        ctx.inc(s.copy(out=t["swb"][:], in_=ctx.ps["pA"][:, 0:16]), "scl",
                mark="wm_done")


def emit_p0(ctx):
    block, t = ctx.block, ctx.t
    xstat = t["xh"]

    for tc in range(NTC):
        buf = tc % 2

        @block.sync
        def _(sy, tc=tc, buf=buf):
            if tc >= 2:
                ctx.waitm(sy, "p0_s%d" % (tc - 2))
            ctx.inc(sy.dma_start(out=xstat[:, 2048 * buf:2048 * buf + 2048],
                                 in_=ctx.io["xtok"][128 * tc:128 * tc + 128, :]),
                    "da" if buf == 0 else "db", 16, mark="p0_d%d" % tc)

        @block.scalar
        def _(s, tc=tc, buf=buf):
            ctx.waitm(s, "p0_d%d" % tc)
            sl = xstat[:, 2048 * buf:2048 * buf + 2048]
            ins = s.activation(out=sl, in_=sl, func=AF.Copy, bias=0.0,
                               accum_out=t["msum"][:, tc:tc + 1])
            ctx.fence(s, ins, "fs")
            ctx.inc(s.activation(out=sl, in_=sl, func=AF.Square,
                                 accum_out=t["sqsum"][:, tc:tc + 1]),
                    "scl", mark="p0_s%d" % tc)

    @block.vector
    def _(v):
        ctx.waitm(v, "p0_s%d" % (NTC - 1))
        ctx.inc(v.tensor_scalar(out=t["m_t"][:], in0=t["msum"][:],
                                scalar1=1.0 / 2048.0, scalar2=None, op0=ALU.mult),
                "vec")
        ins = v.tensor_scalar(out=t["st_a"][:], in0=t["sqsum"][:],
                              scalar1=1.0 / 2048.0, scalar2=None, op0=ALU.mult)
        ctx.fence(v, ins, "fv")
        ins = v.tensor_tensor(out=t["st_b"][:], in0=t["m_t"][:], in1=t["m_t"][:],
                              op=ALU.mult)
        ctx.fence(v, ins, "fv")
        ctx.inc(v.tensor_tensor(out=t["st_a"][:], in0=t["st_a"][:], in1=t["st_b"][:],
                                op=ALU.subtract), "vec", mark="p0_var")

    @block.scalar
    def _(s):
        ctx.waitm(s, "p0_var")
        ctx.inc(s.activation(out=t["st_b"][:], in_=t["st_a"][:], func=AF.Sqrt,
                             bias=LN_EPS), "scl", mark="p0_sd")

    @block.vector
    def _(v):
        ctx.waitm(v, "p0_sd")
        nrecip(ctx, v, t["rstd_t"][:], t["st_b"][:], t["st_c"][:])
        ctx.inc(v.tensor_copy(out=t["st_c"][:], in_=t["rstd_t"][:]), "vec",
                mark="p0_rstd")

    row_from_cols(ctx, t["m_t"][:], 16, "c_qdrow", 0, "p0m")
    row_from_cols(ctx, t["rstd_t"][:], 16, "rrow", 0, "p0r")
    bcast_row(ctx, "c_qdrow", 0, 2048, "mbf", wait_mark="rf_dp0m")
    bcast_row(ctx, "rrow", 0, 2048, "rbf", wait_mark="rf_dp0r")

    for hf in range(2):
        hsl = slice(1024 * hf, 1024 * hf + 1024)

        @block.sync
        def _(sy, hf=hf):
            if hf == 1:
                ctx.wait(sy, "gps")
                ctx.wait(sy, "vec")
            else:
                ctx.waitm(sy, "bc_rbf")
            ctx.inc(sy.dma_start(
                out=t["xh"][:].rearrange("p (fc t) -> p fc t", fc=NFC),
                in_=ctx.io["xT"].rearrange("(fc p) t -> p fc t", p=128)[:, :, hsl]),
                "dc", 16, mark="p0_xh%d" % hf)

        @block.vector
        def _(v, hf=hf):
            if hf == 1:
                ctx.wait(v, "ten")
                ctx.waitm(v, "p0_q0_%d" % (NFC - 1))
            ctx.inc(v.memset(t["mxa"][:], 0.0), "vec", mark="p0_mx0_%d" % hf)

        for fc in range(NFC):
            @block.vector
            def _(v, hf=hf, fc=fc):
                if fc == 0:
                    ctx.waitm(v, "p0_xh%d" % hf)
                xsl = t["xh"][:, 1024 * fc:1024 * fc + 1024]
                ins = v.tensor_tensor(out=xsl, in0=xsl, in1=t["mbf"][:, hsl],
                                      op=ALU.subtract)
                ctx.fence(v, ins, "fv")
                ins = v.tensor_tensor(out=xsl, in0=xsl, in1=t["rbf"][:, hsl],
                                      op=ALU.mult)
                ctx.fence(v, ins, "fv")
                ins = v.scalar_tensor_tensor(out=t["mxa"][:], in0=xsl, scalar=1.0,
                                             in1=t["mxa"][:], op0=ALU.mult,
                                             op1=ALU.max)
                ctx.fence(v, ins, "fv")
                ctx.inc(v.scalar_tensor_tensor(out=t["mxa"][:], in0=xsl, scalar=-1.0,
                                               in1=t["mxa"][:], op0=ALU.mult,
                                               op1=ALU.max),
                        "vec", mark="p0_mx%d_%d" % (hf, fc))

        for tp in range(8):
            pbuf = "pA" if tp % 2 == 0 else "pB"

            @block.tensor
            def _(te, hf=hf, tp=tp, pbuf=pbuf):
                if tp == 0:
                    ctx.waitm(te, "p0_mx%d_%d" % (hf, NFC - 1))
                if tp >= 2:
                    ctx.waitm(te, "p0_red%d_%d" % (hf, tp - 2))
                ctx.inc(te.transpose(out=ctx.ps[pbuf][:, 0:128],
                                     in_=t["mxa"][:, 128 * tp:128 * tp + 128],
                                     identity=t["identf"][:]), "ten",
                        mark="p0_tp%d_%d" % (hf, tp))

            @block.vector
            def _(v, hf=hf, tp=tp, pbuf=pbuf):
                ctx.waitm(v, "p0_tp%d_%d" % (hf, tp))
                ctx.inc(v.tensor_reduce(out=t["mx_t"][:, 8 * hf + tp:8 * hf + tp + 1],
                                        in_=ctx.ps[pbuf][:, 0:128],
                                        axis=AX.X, op=ALU.max), "vec",
                        mark="p0_red%d_%d" % (hf, tp))

        c8 = slice(8 * hf, 8 * hf + 8)

        @block.vector
        def _(v, hf=hf, c8=c8):
            ctx.waitm(v, "p0_red%d_7" % hf)
            ins = v.tensor_scalar(out=t["st_a"][:, c8], in0=t["mx_t"][:, c8],
                                  scalar1=1e-5, scalar2=None, op0=ALU.max)
            ctx.fence(v, ins, "fv")
            nrecip(ctx, v, t["st_b"][:, c8], t["st_a"][:, c8], t["st_d"][:, c8])
            ins = v.tensor_scalar(out=t["sx_t"][:, c8], in0=t["st_b"][:, c8],
                                  scalar1=127.0, scalar2=None, op0=ALU.mult)
            ctx.fence(v, ins, "fv")
            nrecip(ctx, v, t["invsx_t"][:, c8], t["sx_t"][:, c8], t["st_d"][:, c8])
            ctx.inc(v.tensor_copy(out=t["st_d"][:, c8], in_=t["invsx_t"][:, c8]),
                    "vec", mark="p0_sx%d" % hf)

        row_from_cols(ctx, t["sx_t"][:, c8], 8, "rrow", 0, "p0x%d" % hf)
        bcast_row(ctx, "rrow", 0, 1024, "sbq", wait_mark="rf_dp0x%d" % hf)

        for fc in range(NFC):
            b2 = fc % 2

            @block.vector
            def _(v, hf=hf, fc=fc, b2=b2):
                if fc == 0:
                    ctx.waitm(v, "bc_sbq1")
                if fc >= 2:
                    ctx.waitm(v, "p0_cl%d_%d" % (hf, fc - 2))
                ctx.inc(v.tensor_tensor(out=t["ti32"][:, 1024 * b2:1024 * b2 + 1024],
                                        in0=t["xh"][:, 1024 * fc:1024 * fc + 1024],
                                        in1=t["sbq"][:, 0:1024], op=ALU.mult),
                        "vec", mark="p0_q%d_%d" % (hf, fc))

            @block.gpsimd
            def _(g, hf=hf, fc=fc, b2=b2):
                ctx.waitm(g, "p0_q%d_%d" % (hf, fc))
                dst = ctx.t["xbig"][:, 2048 * fc + 1024 * hf:
                                    2048 * fc + 1024 * hf + 1024]
                ctx.inc(g.tensor_scalar(out=dst,
                                        in0=t["ti32"][:, 1024 * b2:1024 * b2 + 1024],
                                        scalar1=127.0, scalar2=-128.0,
                                        op0=ALU.min, op1=ALU.max), "gps",
                        mark="p0_cl%d_%d" % (hf, fc))


def emit_gateup(ctx):
    block, t = ctx.block, ctx.t
    swb = t["swb"]

    @block.vector
    def _(v):
        ctx.inc(v.memset(t["mxz"][:], 0.0), "vec")
        ctx.inc(v.memset(t["sq"][:], 0.0), "vec")
        ctx.waitm(v, "c_loads")
        ctx.inc(v.tensor_tensor(out=t["gsq"][:], in0=t["gsub"][:], in1=t["gsub"][:],
                                op=ALU.mult), "vec", mark="gu_z")

    for ic in range(NIC):
        buf = ic % 2

        @block.sync
        def _(sy, ic=ic, buf=buf):
            if ic >= 1:
                ctx.waitm(sy, "gu_rg%d_1" % (ic - 1))
                ctx.inc(sy.dma_start(out=ctx.io["rgd"][128 * (ic - 1):128 * ic, :],
                                     in_=t["rgt"][:]), "de", 16,
                        mark="gu_rgd%d" % (ic - 1))
            if ic >= 2:
                ctx.waitm(sy, "gu_q%d" % (ic - 2))
            for wio, stg in [("wgS", t["wstg_g"]), ("wuS", t["wstg_u"])]:
                ctx.inc(sy.dma_start(out=stg[:, 2048 * (ic % 2):2048 * (ic % 2) + 2048],
                                     in_=ctx.io[wio][ic]), "dd", 16)
            ctx.setmark("gu_w%d" % ic, "dd")
            if ic >= 1:
                ctx.waitm(sy, "gu_wd1_%d" % (ic - 1))
            ctx.inc(sy.dma_start(out=t["wdstg"][:],
                                 in_=ctx.io["wdT"][128 * ic:128 * ic + 128, :]),
                    "df", 16, mark="gu_wdd%d" % ic)

        @block.vector
        def _(v, ic=ic, buf=buf):
            ctx.waitm(v, "gu_w%d" % ic)
            if ic == 0:
                ctx.waitm(v, "gu_z")
            if ic >= 2:
                ctx.waitm(v, "gu_mm%d_1" % (ic - 2))
            if ic >= 1:
                ctx.waitm(v, "gu_wd2_%d" % (ic - 1))
            for wi, (stg, wq) in enumerate([(t["wstg_g"], t["wqg"]),
                                            (t["wstg_u"], t["wqu"])]):
                ins = v.tensor_scalar(out=t["ti32"][:, 0:2048],
                                      in0=stg[:, 2048 * buf:2048 * buf + 2048],
                                      scalar1=swb[:, 8 + wi:9 + wi].opt(),
                                      scalar2=None, op0=ALU.mult)
                ctx.fence(v, ins, "fv")
                ins = v.tensor_scalar(out=wq[:, 2048 * buf:2048 * buf + 2048],
                                      in0=t["ti32"][:, 0:2048],
                                      scalar1=1.0, scalar2=-1.0,
                                      op0=ALU.min, op1=ALU.max)
                if wi == 0:
                    ctx.fence(v, ins, "fv")
                else:
                    ctx.inc(ins, "ch1")
            ctx.setmark("gu_q%d" % ic, "ch1")

        for half in range(2):
            e = 2 * ic + half

            @block.tensor
            def _(te, ic=ic, buf=buf, half=half):
                if half == 0:
                    ctx.waitm(te, "gu_q%d" % ic)
                if ic >= 1:
                    ctx.waitm(te, "gu_rel%d_%d" % (ic - 1, half))
                    ctx.waitm(te, "gu_ps%d_%d" % (ic - 1, half))
                last = None
                for fc in range(NFC):
                    for gu, wq in [(0, t["wqg"]), (1, t["wqu"])]:
                        lhsT = wq[:, 2048 * buf + 128 * fc:2048 * buf + 128 * fc + 128]
                        for tg in (2 * half, 2 * half + 1):
                            ps = ctx.ps["pG" if gu == 0 else "pU"]
                            last = te.matmul(
                                ps[:, 512 * tg:512 * tg + 512], lhsT,
                                ctx.t["xbig"][:, 2048 * fc + 512 * tg:
                                              2048 * fc + 512 * tg + 512],
                                start=(fc == 0), stop=(fc == NFC - 1))
                ctx.inc(last, "ten", mark="gu_mm%d_%d" % (ic, half))

            @block.scalar
            def _(s, ic=ic, half=half, e=e):
                ctx.waitm(s, "gu_mm%d_%d" % (ic, half))
                if e >= 1:
                    ctx.waitm(s, "gu_ps%d_%d" % ((e - 1) // 2, (e - 1) % 2))
                for k, tg in enumerate((2 * half, 2 * half + 1)):
                    sl = t["gu_t2"][:, 512 * k:512 * k + 512]
                    ctx.inc(s.activation(out=sl,
                                         in_=ctx.ps["pG"][:, 512 * tg:512 * tg + 512],
                                         func=AF.Relu), "ch2")
                ctx.setmark("gu_rel%d_%d" % (ic, half), "ch2")

            @block.gpsimd
            def _(g, ic=ic, half=half):
                ctx.waitm(g, "gu_rel%d_%d" % (ic, half))
                for k in range(2):
                    sl = t["gu_t2"][:, 512 * k:512 * k + 512]
                    ctx.inc(g.tensor_tensor(out=sl, in0=sl, in1=sl, op=ALU.mult),
                            "gps")
                ctx.setmark("gu_sq%d_%d" % (ic, half), "gps")

            @block.vector
            def _(v, ic=ic, half=half):
                ctx.waitm(v, "gu_sq%d_%d" % (ic, half))
                e_ = 2 * ic + half
                if e_ >= 1:
                    ctx.waitm(v, "gu_rg%d_%d" % ((e_ - 1) // 2, (e_ - 1) % 2))
                for k, tg in enumerate((2 * half, 2 * half + 1)):
                    ctx.inc(v.tensor_tensor(out=t["gu_r"][:, 512 * k:512 * k + 512],
                                            in0=t["gu_t2"][:, 512 * k:512 * k + 512],
                                            in1=ctx.ps["pU"][:, 512 * tg:512 * tg + 512],
                                            op=ALU.mult), "ch2")
                ctx.setmark("gu_ps%d_%d" % (ic, half), "ch2")

            @block.gpsimd
            def _(g, ic=ic, half=half, e=e):
                ctx.waitm(g, "gu_ps%d_%d" % (ic, half))
                if e >= 1:
                    ctx.waitm(g, "gu_mxq%d_%d" % ((e - 1) // 2, (e - 1) % 2))
                ctx.inc(g.tensor_tensor(out=t["gu_t3"][:], in0=t["gu_r"][:],
                                        in1=t["gu_r"][:], op=ALU.mult), "ch5")
                ctx.setmark("gu_t3_%d_%d" % (ic, half), "ch5")

            @block.vector
            def _(v, ic=ic, half=half):
                ctx.waitm(v, "gu_ps%d_%d" % (ic, half))
                ctx.waitm(v, "gu_t3_%d_%d" % (ic, half))
                if ic >= 1 and half == 0:
                    ctx.waitm(v, "gu_rgd%d" % (ic - 1))
                hsl = slice(1024 * half, 1024 * half + 1024)
                ctx.inc(v.tensor_scalar(out=t["rgt"][:, hsl], in0=t["gu_r"][:],
                                        scalar1=t["gsub"][:, ic:ic + 1].opt(),
                                        scalar2=None, op0=ALU.mult), "ch3",
                        mark="gu_rg%d_%d" % (ic, half))
                ctx.inc(v.scalar_tensor_tensor(out=t["mxz"][:, hsl],
                                               in0=t["gu_t3"][:],
                                               scalar=t["gsq"][:, ic:ic + 1].opt(),
                                               in1=t["mxz"][:, hsl],
                                               op0=ALU.mult, op1=ALU.max), "vec",
                        mark="gu_mxq%d_%d" % (ic, half))

            @block.gpsimd
            def _(g, ic=ic, half=half):
                hsl = slice(1024 * half, 1024 * half + 1024)
                ctx.waitm(g, "gu_t3_%d_%d" % (ic, half))
                ctx.inc(g.tensor_tensor(out=t["sq"][:, hsl], in0=t["sq"][:, hsl],
                                        in1=t["gu_t3"][:], op=ALU.add), "gps",
                        mark="gu_sqa%d_%d" % (ic, half))

        @block.scalar
        def _(s, ic=ic):
            ctx.waitm(s, "gu_q%d" % ic)
            ctx.waitm(s, "gu_wdd%d" % ic)
            if ic >= 1:
                ctx.waitm(s, "gu_wd2_%d" % (ic - 1))
            ctx.inc(s.activation(out=t["ti32"][:, 0:2048], in_=t["wdstg"][:],
                                 func=AF.Copy, bias=0.0,
                                 scale=swb[:, 10:11].opt()), "ch4",
                    mark="gu_wd1_%d" % ic)

        @block.gpsimd
        def _(g, ic=ic):
            ctx.waitm(g, "gu_wd1_%d" % ic)
            if ic >= 1:
                ctx.waitm(g, "gu_wdq%d" % (ic - 1))
            ins = g.tensor_scalar(out=t["wdqb"][:], in0=t["ti32"][:, 0:2048],
                                  scalar1=1.0, scalar2=-1.0,
                                  op0=ALU.min, op1=ALU.max)
            ctx.inc(ins, "ch5", mark="gu_wd2_%d" % ic)
            ctx.waitm(g, "gu_wd2_%d" % ic)
            ctx.inc(g.dma_start(out=ctx.io["wdq"][128 * ic:128 * ic + 128, :],
                                in_=t["wdqb"][:]), "dgp", 16, mark="gu_wdq%d" % ic)

    @block.sync
    def _(sy):
        ctx.waitm(sy, "gu_rg%d_1" % (NIC - 1))
        ctx.inc(sy.dma_start(out=ctx.io["rgd"][128 * (NIC - 1):128 * NIC, :],
                             in_=t["rgt"][:]), "de", 16, mark="gu_rgd%d" % (NIC - 1))

    for src_, dst, op in [("mxz", "mxz_t", ALU.max), ("sq", "sq_t", ALU.add)]:
        for tp in range(NTC):
            pbuf = "pA" if tp % 2 == 0 else "pB"

            @block.tensor
            def _(te, src_=src_, tp=tp, pbuf=pbuf):
                if tp == 0 and src_ == "mxz":
                    ctx.waitm(te, "gu_mxq%d_1" % (NIC - 1))
                    ctx.wait(te, "vec")
                if tp == 0 and src_ == "sq":
                    ctx.waitm(te, "ep_mxz_%d" % (NTC - 1))
                    ctx.waitm(te, "gu_sqa%d_1" % (NIC - 1))
                if tp >= 2:
                    ctx.waitm(te, "ep_%s_%d" % (src_, tp - 2))
                ctx.inc(te.transpose(out=ctx.ps[pbuf][:, 0:128],
                                     in_=t[src_][:, 128 * tp:128 * tp + 128],
                                     identity=t["identf"][:]), "ten",
                        mark="ept_%s_%d" % (src_, tp))

            @block.vector
            def _(v, src_=src_, dst=dst, tp=tp, op=op, pbuf=pbuf):
                ctx.waitm(v, "ept_%s_%d" % (src_, tp))
                ctx.inc(v.tensor_reduce(out=t[dst][:, tp:tp + 1],
                                        in_=ctx.ps[pbuf][:, 0:128],
                                        axis=AX.X, op=op), "vec",
                        mark="ep_%s_%d" % (src_, tp))

    @block.scalar
    def _(s):
        ctx.waitm(s, "ep_mxz_%d" % (NTC - 1))
        ctx.wait(s, "vec")
        ctx.inc(s.activation(out=t["mxz_t"][:], in_=t["mxz_t"][:], func=AF.Sqrt),
                "scl", mark="gu_mxs")

    @block.vector
    def _(v):
        ctx.waitm(v, "ep_sq_%d" % (NTC - 1))
        ctx.waitm(v, "gu_mxs")
        k1, v1 = t["st_a"], t["st_b"]
        ins = v.tensor_tensor(out=k1[:], in0=t["invsx_t"][:], in1=t["invsx_t"][:],
                              op=ALU.mult)
        ctx.fence(v, ins, "fv")
        ins = v.tensor_tensor(out=k1[:], in0=k1[:], in1=t["invsx_t"][:], op=ALU.mult)
        ctx.fence(v, ins, "fv")
        ins = v.tensor_scalar(out=k1[:], in0=k1[:], scalar1=swb[:, 4:5].opt(),
                              scalar2=swb[:, 4:5].opt(), op0=ALU.mult, op1=ALU.mult)
        ctx.fence(v, ins, "fv")
        ins = v.tensor_scalar(out=k1[:], in0=k1[:], scalar1=swb[:, 5:6].opt(),
                              scalar2=None, op0=ALU.mult)
        ctx.fence(v, ins, "fv")
        ins = v.tensor_tensor(out=v1[:], in0=k1[:], in1=k1[:], op=ALU.mult)
        ctx.fence(v, ins, "fv")
        ins = v.tensor_tensor(out=v1[:], in0=v1[:], in1=t["sq_t"][:], op=ALU.mult)
        ctx.fence(v, ins, "fv")
        ctx.inc(v.tensor_scalar(out=v1[:], in0=v1[:], scalar1=1.0 / 8192.0,
                                scalar2=None, op0=ALU.mult), "vec", mark="gu_var")

    @block.scalar
    def _(s):
        ctx.waitm(s, "gu_var")
        ctx.inc(s.activation(out=t["st_c"][:], in_=t["st_b"][:], func=AF.Sqrt,
                             bias=SUB_EPS), "scl", mark="gu_sd")

    @block.vector
    def _(v):
        ctx.waitm(v, "gu_sd")
        nrecip(ctx, v, t["rstdz_t"][:], t["st_c"][:], t["st_d"][:])
        ins = v.tensor_tensor(out=t["st_b"][:], in0=t["rstdz_t"][:], in1=t["st_a"][:],
                              op=ALU.mult)
        ctx.fence(v, ins, "fv")
        ins = v.tensor_tensor(out=t["st_c"][:], in0=t["st_b"][:], in1=t["mxz_t"][:],
                              op=ALU.mult)
        ctx.fence(v, ins, "fv")
        ins = v.tensor_scalar(out=t["st_c"][:], in0=t["st_c"][:], scalar1=1e-5,
                              scalar2=None, op0=ALU.max)
        ctx.fence(v, ins, "fv")
        ins = v.reciprocal(out=t["st_d"][:], in_=t["st_c"][:])
        ctx.fence(v, ins, "fv")
        ins = v.tensor_scalar(out=t["sz_t"][:], in0=t["st_d"][:], scalar1=127.0,
                              scalar2=None, op0=ALU.mult)
        ctx.fence(v, ins, "fv")
        ins = v.tensor_tensor(out=t["qsc_t"][:], in0=t["st_b"][:], in1=t["sz_t"][:],
                              op=ALU.mult)
        ctx.fence(v, ins, "fv")
        ins = v.reciprocal(out=t["st_d"][:], in_=t["sz_t"][:])
        ctx.fence(v, ins, "fv")
        ctx.inc(v.tensor_scalar(out=t["invsd_t"][:], in0=t["st_d"][:],
                                scalar1=swb[:, 6:7].opt(), scalar2=None,
                                op0=ALU.mult), "vec", mark="gu_tok")

    row_from_cols(ctx, t["qsc_t"][:], 16, "c_qdrow", 0, "guq")
    bcast_row(ctx, "c_qdrow", 0, 2048, "qscb", wait_mark="rf_dguq")


def emit_down(ctx):
    block, t = ctx.block, ctx.t
    for q in range(4):
        qsl = slice(512 * q, 512 * q + 512)

        for i in range(NIC):
            buf = i % 2

            @block.sync
            def _(sy, q=q, i=i, buf=buf):
                if i >= 2:
                    ctx.waitm(sy, "dn_q%d_%d" % (q, i - 2))
                elif q >= 1:
                    ctx.waitm(sy, "dn_q%d_%d" % (q - 1, NIC - 2 + i))
                ctx.inc(sy.dma_start(out=t["rstg"][:, 512 * buf:512 * buf + 512],
                                     in_=ctx.io["rgd"][128 * i:128 * i + 128, qsl]),
                        "da" if buf == 0 else "db", 16, mark="dn_d%d_%d" % (q, i))

            @block.vector
            def _(v, q=q, i=i, buf=buf):
                ctx.waitm(v, "dn_d%d_%d" % (q, i))
                if i >= 2:
                    ctx.waitm(v, "dn_c%d_%d" % (q, i - 2))
                elif q >= 1:
                    ctx.waitm(v, "dn_c%d_%d" % (q - 1, NIC - 2 + i))
                ctx.inc(v.tensor_tensor(out=t["tiq"][:, 512 * buf:512 * buf + 512],
                                        in0=t["rstg"][:, 512 * buf:512 * buf + 512],
                                        in1=t["qscb"][:, qsl], op=ALU.mult),
                        "ch4", mark="dn_q%d_%d" % (q, i))

            @block.gpsimd
            def _(g, q=q, i=i, buf=buf):
                ctx.waitm(g, "dn_q%d_%d" % (q, i))
                if q >= 1 and i == 0:
                    ctx.waitm(g, "dn_mm%d_%d" % (q - 1, 2 * NIC - 1))
                ctx.inc(g.tensor_scalar(out=ctx.t["xbig"][:, 512 * i:512 * i + 512],
                                        in0=t["tiq"][:, 512 * buf:512 * buf + 512],
                                        scalar1=127.0, scalar2=-128.0,
                                        op0=ALU.min, op1=ALU.max), "ch0",
                        mark="dn_c%d_%d" % (q, i))

        for fgp in range(2):
            for i in range(NIC):
                wbuf = i % 2
                idx = fgp * NIC + i

                @block.scalar
                def _(s, q=q, fgp=fgp, i=i, wbuf=wbuf, idx=idx):
                    if idx >= 2:
                        ctx.waitm(s, "dn_mm%d_%d" % (q, idx - 2))
                    elif q >= 1:
                        ctx.waitm(s, "dn_mm%d_%d" % (q - 1, 2 * NIC - 2 + idx))
                    ctx.inc(s.dma_start(
                        out=t["wdt2"][:, 1024 * wbuf:1024 * wbuf + 1024],
                        in_=ctx.io["wdq"][128 * i:128 * i + 128,
                                          1024 * fgp:1024 * fgp + 1024]),
                        "dsc" if wbuf == 0 else "df", 16,
                        mark="dn_wt%d_%d" % (q, idx))

                @block.tensor
                def _(te, q=q, fgp=fgp, i=i, wbuf=wbuf, idx=idx):
                    ctx.waitm(te, "dn_wt%d_%d" % (q, idx))
                    if fgp == 0:
                        ctx.waitm(te, "dn_c%d_%d" % (q, i))
                    if i == 0:
                        if fgp == 1:
                            ctx.waitm(te, "dn_ev%d_0" % q)
                        elif q >= 1:
                            ctx.waitm(te, "dn_ev%d_1" % (q - 1))
                    last = None
                    for fg2 in range(2):
                        ps = ctx.ps["pG" if fg2 == 0 else "pU"]
                        for tc in range(4):
                            last = te.matmul(
                                ps[:, 512 * tc:512 * tc + 512],
                                ctx.t["xbig"][:, 512 * i + 128 * tc:
                                              512 * i + 128 * tc + 128],
                                t["wdt2"][:, 1024 * wbuf + 512 * fg2:
                                          1024 * wbuf + 512 * fg2 + 512],
                                start=(i == 0), stop=(i == NIC - 1))
                    ctx.inc(last, "ten", mark="dn_mm%d_%d" % (q, idx))

            @block.vector
            def _(v, q=q, fgp=fgp):
                ctx.waitm(v, "dn_mm%d_%d" % (q, fgp * NIC + NIC - 1))
                if q >= 1 and fgp == 0:
                    ctx.waitm(v, "dn_od%d" % (q - 1))
                for fg2 in range(2):
                    ps = ctx.ps["pG" if fg2 == 0 else "pU"]
                    fg = 2 * fgp + fg2
                    for tc in range(4):
                        ctx.inc(v.tensor_scalar(
                            out=t["ot"][:, 2048 * tc + 512 * fg:
                                        2048 * tc + 512 * fg + 512],
                            in0=ps[:, 512 * tc:512 * tc + 512],
                            scalar1=t["invsd_t"][:, 4 * q + tc:4 * q + tc + 1].opt(),
                            scalar2=None, op0=ALU.mult), "ch1")
                ctx.setmark("dn_ev%d_%d" % (q, fgp), "ch1")

        @block.sync
        def _(sy, q=q):
            ctx.waitm(sy, "dn_ev%d_1" % q)
            for tc in range(4):
                ctx.inc(sy.dma_start(
                    out=ctx.io["out"][512 * q + 128 * tc:512 * q + 128 * tc + 128, :],
                    in_=t["ot"][:, 2048 * tc:2048 * tc + 2048]), "de", 16)
            ctx.setmark("dn_od%d" % q, "de")


def emit_xn(ctx):
    block, t = ctx.block, ctx.t

    row_from_cols(ctx, t["m_t"][:], 16, "c_qdrow2", 0, "x2m")
    row_from_cols(ctx, t["rstd_t"][:], 16, "rrow2", 0, "x2r")
    bcast_row(ctx, "c_qdrow2", 0, 2048, "mbf2", wait_mark="rf_dx2m")
    bcast_row(ctx, "rrow2", 0, 2048, "rbf2", wait_mark="rf_dx2r")

    for fc in range(NFC):
        buf = fc % 2

        @block.sync
        def _(sy, fc=fc, buf=buf):
            if fc >= 2:
                ctx.waitm(sy, "x2_%d" % (fc - 2))
            ctx.inc(sy.dma_start(out=t["xst2"][:, 2048 * buf:2048 * buf + 2048],
                                 in_=ctx.io["xT"][128 * fc:128 * fc + 128, :]),
                    "da" if buf == 0 else "db", 16, mark="x2d_%d" % fc)

        @block.vector
        def _(v, fc=fc, buf=buf):
            ctx.waitm(v, "x2d_%d" % fc)
            if fc == 0:
                ctx.waitm(v, "bc_rbf2")
            sl = t["xst2"][:, 2048 * buf:2048 * buf + 2048]
            ins = v.tensor_tensor(out=sl, in0=sl, in1=t["mbf2"][:], op=ALU.subtract)
            ctx.fence(v, ins, "fv")
            ctx.inc(v.tensor_tensor(out=ctx.t["xbig"][:, 2048 * fc:2048 * fc + 2048],
                                    in0=sl, in1=t["rbf2"][:], op=ALU.mult),
                    "ch0", mark="x2_%d" % fc)



def emit_decay(ctx):
    block, t = ctx.block, ctx.t

    @block.gpsimd
    def _(g):
        ctx.inc(g.iota(out=t["d_row"][:], pattern=[[1, 128]], base=0,
                       channel_multiplier=0, allow_small_or_imprecise_dtypes=True),
                "gps")
        ctx.inc(g.iota(out=t["d_col"][:], pattern=[[0, 128]], base=0,
                       channel_multiplier=1,
                       allow_small_or_imprecise_dtypes=True), "gps", mark="d_iota")

    @block.vector
    def _(v):
        ctx.waitm(v, "d_iota")
        ctx.inc(v.tensor_tensor(out=t["d_ige"][:], in0=t["d_row"][:],
                                in1=t["d_col"][:], op=ALU.is_ge), "vec")
        ins = v.tensor_tensor(out=t["d_diff"][:], in0=t["d_row"][:],
                              in1=t["d_col"][:], op=ALU.subtract)
        ctx.fence(v, ins, "fv")
        ctx.inc(v.tensor_scalar(out=t["d_diff"][:], in0=t["d_diff"][:],
                                scalar1=0.0, scalar2=None, op0=ALU.max),
                "vec", mark="d_base")

    for h in range(HEADS):
        @block.scalar
        def _(s, h=h):
            if h == 0:
                ctx.waitm(s, "d_base")
            if h >= 2:
                ctx.waitm(s, "d_dtv%d" % (h - 2))
            ctx.inc(s.activation(out=t["d_dte"][:, 128 * (h % 2):128 * (h % 2) + 128],
                                 in_=t["d_diff"][:], func=AF.Exp,
                                 scale=t["nslb"][:, h:h + 1].opt()),
                    "scl", mark="d_dts%d" % h)

        @block.vector
        def _(v, h=h):
            ctx.waitm(v, "d_dts%d" % h)
            ctx.inc(v.tensor_tensor(out=t["dt"][:, 128 * h:128 * h + 128],
                                    in0=t["d_dte"][:, 128 * (h % 2):128 * (h % 2) + 128],
                                    in1=t["d_ige"][:], op=ALU.mult),
                    "vec", mark="d_dtv%d" % h)

    for h in range(HEADS):
        @block.scalar
        def _(s, h=h):
            if h == 0:
                ctx.waitm(s, "d_base")
            ctx.inc(s.activation(out=t["qd"][:, 128 * h:128 * h + 128],
                                 in_=t["d_row"][:], func=AF.Exp,
                                 scale=t["nslb"][:, h:h + 1].opt()),
                    "scl", mark="d_qb%d" % h)


def emit_attn(ctx):
    block, t = ctx.block, ctx.t
    pG, pU = ctx.ps["pG"], ctx.ps["pU"]
    pufree = [None, None]

    @block.tensor
    def _(te):
        ctx.wait(te, "vec")
        ctx.wait(te, "scl")
        te.matmul(pG[:, 0:1], t["ones1f"][:], t["msends"][0:1, 0:1],
                  start=True, stop=True)
        ctx.inc(te.matmul(pG[:, 512:513], t["ones1f"][:], t["mrecvs"][0:1, 0:1],
                          start=True, stop=True), "ten", mark="at_msk")

    @block.scalar
    def _(s):
        ctx.waitm(s, "at_msk")
        s.copy(out=t["msb"][:, 0:1], in_=pG[:, 0:1])
        ctx.inc(s.copy(out=t["mrb"][:, 0:1], in_=pG[:, 512:513]), "scl",
                mark="at_msk2")

    bcast_dram_row(ctx, ctx.io["bqv"], "bvb", "atb")

    for hg in range(NHG):
        h0 = 2 * hg
        def emit_proj(proj, colbase, hg=hg, h0=h0):
            @block.gpsimd
            def _(g, hg=hg, proj=proj, colbase=colbase):
                if not (hg == 0 and proj == "k"):
                    ctx.waitm(g, "at_wcons")
                ctx.inc(g.dma_start(
                    out=t["wstg2"][:].rearrange("p (fc o) -> p fc o", fc=NFC),
                    in_=ctx.io["wqkv"].rearrange("(fc p) o -> p fc o", p=128)
                    [:, :, colbase:colbase + 256]), "dh", 16,
                    mark="at_w%s%d" % (proj, hg))

            if proj in ("q", "k"):
                dst = t["qt"] if proj == "q" else t["kt"]
                bcol = (0 if proj == "q" else 16) + h0
                for qh in range(2):
                    for tg in range(4):
                        gi = qh * 4 + tg
                        par = gi % 2
                        pb = pU[:, 1024 + 512 * par:1536 + 512 * par]

                        @block.tensor
                        def _(te, hg=hg, proj=proj, qh=qh, tg=tg, gi=gi, pb=pb,
                              par=par):
                            ctx.waitm(te, "at_w%s%d" % (proj, hg))
                            if pufree[par] is not None:
                                ctx.waitm(te, pufree[par])
                            last = None
                            for fc in range(NFC):
                                last = te.matmul(
                                    pb, t["wstg2"][:, 256 * fc + 128 * qh:
                                                   256 * fc + 128 * qh + 128],
                                    ctx.t["xbig"][:, 2048 * fc + 512 * tg:
                                                  2048 * fc + 512 * tg + 512],
                                    start=(fc == 0), stop=(fc == NFC - 1))
                            ctx.inc(last, "ten", mark="at_pm%s%d_%d" % (proj, hg, gi))

                        @block.vector
                        def _(v, hg=hg, proj=proj, qh=qh, tg=tg, gi=gi, pb=pb,
                              dst=dst, bcol=bcol):
                            ctx.waitm(v, "at_pm%s%d_%d" % (proj, hg, gi))
                            if hg >= 1 and gi == 0 and proj == "q":
                                ctx.waitm(v, "at_scandone%d" % (hg - 1))
                                ctx.waitm(v, "at_scanmm%d" % (hg - 1))
                            ctx.inc(v.tensor_scalar(
                                out=dst[:, 2048 * qh + 512 * tg:
                                        2048 * qh + 512 * tg + 512],
                                in0=pb,
                                scalar1=t["bqkvs"][:, bcol + qh:bcol + qh + 1].opt(),
                                scalar2=None, op0=ALU.add), "ch1",
                                mark="at_pe%s%d_%d" % (proj, hg, gi))
                        pufree[par] = "at_pe%s%d_%d" % (proj, hg, gi)
                ctx.setmark("at_wcons", "ten")
            else:
                for tc in range(NTC):
                    par = tc % 2
                    pb = pU[:, 1024 + 512 * par:1280 + 512 * par]

                    @block.tensor
                    def _(te, hg=hg, tc=tc, pb=pb, par=par):
                        ctx.waitm(te, "at_wv%d" % hg)
                        if pufree[par] is not None:
                            ctx.waitm(te, pufree[par])
                        last = None
                        for fc in range(NFC):
                            last = te.matmul(
                                pb, ctx.t["xbig"][:, 2048 * fc + 128 * tc:
                                                  2048 * fc + 128 * tc + 128],
                                t["wstg2"][:, 256 * fc:256 * fc + 256],
                                start=(fc == 0), stop=(fc == NFC - 1))
                        ctx.inc(last, "ten", mark="at_vm%d_%d" % (hg, tc))

                    @block.vector
                    def _(v, hg=hg, tc=tc, pb=pb, h0=h0):
                        ctx.waitm(v, "at_vm%d_%d" % (hg, tc))
                        if hg >= 1 and tc == 0:
                            ctx.waitm(v, "at_scandone%d" % (hg - 1))
                            ctx.waitm(v, "at_scanmm%d" % (hg - 1))
                        ctx.inc(v.tensor_tensor(
                            out=t["vtok"][:, 256 * tc:256 * tc + 256], in0=pb,
                            in1=t["bvb"][:, h0 * 128:h0 * 128 + 256],
                            op=ALU.add), "ch1", mark="at_ve%d_%d" % (hg, tc))
                    pufree[tc % 2] = "at_ve%d_%d" % (hg, tc)
                ctx.setmark("at_wcons", "ten")


        emit_proj("k", D + h0 * 128)
        emit_proj("v", 2 * D + h0 * 128)
        for h2 in range(2):
            for tc in range(NTC):
                i2 = h2 * NTC + tc
                pb = pG[:, 1024 + 512 * (i2 % 2):1152 + 512 * (i2 % 2)]

                @block.tensor
                def _(te, hg=hg, h2=h2, tc=tc, i2=i2, pb=pb):
                    ctx.waitm(te, "at_pek%d_%d" % (hg, h2 * 4 + tc // 4))
                    if i2 < 2 and hg >= 1:
                        ctx.waitm(te, "at_oe%d_%d_%d" % (hg - 1, NTC - 1, i2))
                    if i2 >= 2:
                        ctx.waitm(te, "at_kte%d_%d" % (hg, i2 - 2))
                    ctx.inc(te.matmul(
                        pb, t["kt"][:, 2048 * h2 + 128 * tc:
                                    2048 * h2 + 128 * tc + 128],
                        t["ident"][:], start=True, stop=True), "ten",
                        mark="at_ktt%d_%d" % (hg, i2))

                @block.vector
                def _(v, hg=hg, h2=h2, tc=tc, i2=i2, pb=pb):
                    ctx.waitm(v, "at_ktt%d_%d" % (hg, i2))
                    if hg >= 1 and i2 == 0:
                        ctx.waitm(v, "at_scandone%d" % (hg - 1))
                        ctx.waitm(v, "at_scanmm%d" % (hg - 1))
                    ctx.inc(v.tensor_copy(
                        out=t["ktok"][:, 256 * tc + 128 * h2:
                                      256 * tc + 128 * h2 + 128],
                        in_=pb), "ch2", mark="at_kte%d_%d" % (hg, i2))

        for h2 in range(2):
            h = h0 + h2
            psl = pG[:, 512 * h2:512 * h2 + 128]
            for tc in range(NTC):
                buf = tc % 2

                @block.vector
                def _(v, hg=hg, h2=h2, h=h, tc=tc, buf=buf):
                    ctx.waitm(v, "at_kte%d_%d" % (hg, h2 * NTC + tc))
                    if tc >= 2:
                        ctx.waitm(v, "at_slm%d_%d_%d" % (hg, h2, tc - 2))
                    elif h2 == 1:
                        ctx.waitm(v, "at_slm%d_0_%d" % (hg, NTC - 2 + tc))
                    elif hg >= 1:
                        ctx.waitm(v, "at_slm%d_1_%d" % (hg - 1, NTC - 2 + tc))
                    ctx.inc(v.tensor_scalar(
                        out=t["ktmp"][:, 128 * buf:128 * buf + 128],
                        in0=t["ktok"][:, 256 * tc + 128 * h2:
                                      256 * tc + 128 * h2 + 128],
                        scalar1=t["khw"][:, 16 * tc + h:16 * tc + h + 1].opt(),
                        scalar2=None, op0=ALU.mult), "ch3",
                        mark="at_kh%d_%d_%d" % (hg, h2, tc))

                @block.tensor
                def _(te, hg=hg, h2=h2, tc=tc, buf=buf, psl=psl):
                    ctx.waitm(te, "at_kh%d_%d_%d" % (hg, h2, tc))
                    ctx.waitm(te, "at_ve%d_%d" % (hg, tc))
                    if tc == 0:
                        if hg == 0 and h2 == 0:
                            ctx.waitm(te, "at_msk2")
                        if hg >= 1:
                            ctx.waitm(te, "at_st%d_%d_%d" % (hg - 1, NTC - 1, h2))
                    ctx.inc(te.matmul(psl,
                                      t["ktmp"][:, 128 * buf:128 * buf + 128],
                                      t["vtok"][:, 256 * tc + 128 * h2:
                                                256 * tc + 128 * h2 + 128],
                                      start=(tc == 0), stop=(tc == NTC - 1)),
                            "ten", mark="at_slm%d_%d_%d" % (hg, h2, tc))

            @block.vector
            def _(v, hg=hg, h2=h2, psl=psl):
                ctx.waitm(v, "at_slm%d_%d_%d" % (hg, h2, NTC - 1))
                ctx.waitm(v, "at_msk2")
                if hg >= 1 and h2 == 0:
                    ctx.waitm(v, "at_snd%d" % (hg - 1))
                    ctx.waitm(v, "at_sbf%d_%d_1" % (hg - 1, NTC - 1))
                ctx.inc(v.tensor_scalar(out=t["ssend"][:, 128 * h2:128 * h2 + 128],
                                        in0=psl,
                                        scalar1=t["msb"][:, 0:1].opt(),
                                        scalar2=None, op0=ALU.mult), "ch4",
                        mark="at_ss%d_%d" % (hg, h2))

        @block.gpsimd
        def _(g, hg=hg):
            ctx.waitm(g, "at_ss%d_1" % hg)
            for h2 in range(2):
                ctx.inc(g.dma_start(
                    out=ctx.io["ccs_in"][256 * hg + 128 * h2:256 * hg + 128 * h2 + 128, :],
                    in_=t["ssend"][:, 128 * h2:128 * h2 + 128]), "dgp", 16)
            ctx.setmark("at_snd%d" % hg, "dgp")
            ctx.waitm(g, "at_snd%d" % hg)
            ctx.inc(g.collective_compute(
                "AllReduce", ALU.add,
                replica_groups=[[0, 1], [2, 3], [4, 5], [6, 7]],
                ins=[ctx.io["ccs_in"][256 * hg:256 * hg + 256, :]],
                outs=[ctx.io["ccs_out"][256 * hg:256 * hg + 256, :]]), "cc")
            ctx.wait(g, "cc")
            if hg >= 1:
                ctx.waitm(g, "at_si%d" % (hg - 1))
            for h2 in range(2):
                ctx.inc(g.dma_start(
                    out=t["srecv"][:, 128 * h2:128 * h2 + 128],
                    in_=ctx.io["ccs_out"][256 * hg + 128 * h2:256 * hg + 128 * h2 + 128, :]),
                    "dg", 16)
            ctx.setmark("at_sr%d" % hg, "dg")

        emit_proj("q", h0 * 128)

        @block.vector
        def _(v, hg=hg):
            ctx.waitm(v, "at_sr%d" % hg)
            if hg >= 1:
                ctx.waitm(v, "at_sbf%d_%d_1" % (hg - 1, NTC - 1))
            ctx.inc(v.tensor_scalar(out=t["sfull"][:], in0=t["srecv"][:],
                                    scalar1=t["mrb"][:, 0:1].opt(),
                                    scalar2=None, op0=ALU.mult), "ch4",
                    mark="at_si%d" % hg)

        @block.scalar
        def _(s, hg=hg):
            ctx.waitm(s, "at_si%d" % hg)
            if hg >= 1:
                ctx.waitm(s, "at_mm23_%d_%d_1" % (hg - 1, NTC - 1))
            ctx.inc(s.copy(out=t["sbf"][:], in_=t["sfull"][:]), "scl",
                    mark="at_sbf%d_init" % hg)

        for tc in range(NTC):
            for h2 in range(2):
                h = h0 + h2
                buf = tc % 2
                pT = pG[:, 512 * h2:512 * h2 + 128]
                pO = pG[:, 1024 + 512 * h2:1152 + 512 * h2]
                pV = pU[:, 512 * h2:512 * h2 + 128]
                vtsl = t["vtok"][:, 256 * tc + 128 * h2:256 * tc + 128 * h2 + 128]
                qtsl = t["qt"][:, 2048 * h2 + 128 * tc:2048 * h2 + 128 * tc + 128]
                ktfsl = t["kt"][:, 2048 * h2 + 128 * tc:2048 * h2 + 128 * tc + 128]
                ktsl = t["ktok"][:, 256 * tc + 128 * h2:256 * tc + 128 * h2 + 128]
                ssl = slice(256 * h2 + 128 * buf, 256 * h2 + 128 * buf + 128)

                @block.tensor
                def _(te, hg=hg, tc=tc, h2=h2, pT=pT, ktfsl=ktfsl, qtsl=qtsl):
                    if tc == 0:
                        ctx.waitm(te, "at_peq%d_7" % hg)
                        ctx.waitm(te, "at_ss%d_%d" % (hg, h2))
                    else:
                        ctx.waitm(te, "at_st%d_%d_%d" % (hg, tc - 1, h2))
                    ctx.inc(te.matmul(pT, ktfsl, qtsl, start=True, stop=True),
                            "ten", mark="at_mm1_%d_%d_%d" % (hg, tc, h2))

                @block.vector
                def _(v, hg=hg, tc=tc, h2=h2, h=h, buf=buf, pT=pT, qtsl=qtsl,
                      ktsl=ktsl, ssl=ssl):
                    ctx.waitm(v, "at_mm1_%d_%d_%d" % (hg, tc, h2))
                    if tc >= 2:
                        ctx.waitm(v, "at_mm23_%d_%d_%d" % (hg, tc - 2, h2))
                        ctx.waitm(v, "at_mm4_%d_%d_%d" % (hg, tc - 2, h2))
                    elif hg >= 1:
                        ctx.waitm(v, "at_mm23_%d_%d_%d" % (hg - 1, NTC - 2 + tc, h2))
                        ctx.waitm(v, "at_mm4_%d_%d_%d" % (hg - 1, NTC - 2 + tc, h2))
                    ins = v.tensor_tensor(out=t["stsb"][:, ssl], in0=pT,
                                          in1=t["dt"][:, 128 * h:128 * h + 128],
                                          op=ALU.mult)
                    ctx.inc(ins, "ch5", mark="at_st%d_%d_%d" % (hg, tc, h2))
                    ctx.inc(v.tensor_tensor(out=t["qtb"][:, ssl], in0=qtsl,
                                            in1=t["qd"][:, 128 * h:128 * h + 128],
                                            op=ALU.mult),
                            "ch5", mark="at_qt%d_%d_%d" % (hg, tc, h2))
                    ctx.inc(v.tensor_scalar(out=t["ktmp2"][:, ssl], in0=ktsl,
                                            scalar1=t["kdv"][:, h:h + 1].opt(),
                                            scalar2=None, op0=ALU.mult),
                            "ch5", mark="at_km%d_%d_%d" % (hg, tc, h2))

                @block.tensor
                def _(te, hg=hg, tc=tc, h2=h2, buf=buf, pO=pO, vtsl=vtsl, ssl=ssl):
                    ctx.waitm(te, "at_qt%d_%d_%d" % (hg, tc, h2))
                    if tc >= 1:
                        ctx.waitm(te, "at_oe%d_%d_%d" % (hg, tc - 1, h2))
                        ctx.waitm(te, "at_sbf%d_%d_%d" % (hg, tc - 1, h2))
                    else:
                        ctx.waitm(te, "at_sbf%d_init" % hg)
                        ctx.waitm(te, "at_kte%d_%d" % (hg, 2 * NTC - 1))
                        if hg >= 1:
                            ctx.waitm(te, "at_oe%d_%d_%d" % (hg - 1, NTC - 1, h2))
                    te.matmul(pO, vtsl, t["stsb"][:, ssl], start=True, stop=False)
                    ctx.inc(te.matmul(pO, t["sbf"][:, 128 * h2:128 * h2 + 128],
                                      t["qtb"][:, ssl], start=False, stop=True),
                            "ten", mark="at_mm23_%d_%d_%d" % (hg, tc, h2))

                @block.tensor
                def _(te, hg=hg, tc=tc, h2=h2, buf=buf, pV=pV, vtsl=vtsl, ssl=ssl):
                    ctx.waitm(te, "at_km%d_%d_%d" % (hg, tc, h2))
                    if tc >= 1:
                        ctx.waitm(te, "at_sup%d_%d_%d" % (hg, tc - 1, h2))
                    elif hg >= 1:
                        ctx.waitm(te, "at_sup%d_%d_%d" % (hg - 1, NTC - 1, h2))
                    ctx.inc(te.matmul(pV, t["ktmp2"][:, ssl], vtsl,
                                      start=True, stop=True), "ten",
                            mark="at_mm4_%d_%d_%d" % (hg, tc, h2))

                @block.scalar
                def _(s, hg=hg, tc=tc, h2=h2, pO=pO, h=h):
                    ctx.waitm(s, "at_mm23_%d_%d_%d" % (hg, tc, h2))
                    ctx.inc(s.copy(out=ctx.t["att"][:, 2048 * h + 128 * tc:
                                                    2048 * h + 128 * tc + 128],
                                   in_=pO), "scl",
                            mark="at_oe%d_%d_%d" % (hg, tc, h2))

                @block.vector
                def _(v, hg=hg, tc=tc, h2=h2, h=h, pV=pV):
                    ctx.waitm(v, "at_mm4_%d_%d_%d" % (hg, tc, h2))
                    ctx.inc(v.scalar_tensor_tensor(
                        out=t["sfull"][:, 128 * h2:128 * h2 + 128],
                        in0=t["sfull"][:, 128 * h2:128 * h2 + 128],
                        scalar=t["dhv"][:, h:h + 1].opt(), in1=pV,
                        op0=ALU.mult, op1=ALU.add), "ch4",
                        mark="at_sup%d_%d_%d" % (hg, tc, h2))

                @block.scalar
                def _(s, hg=hg, tc=tc, h2=h2):
                    ctx.waitm(s, "at_sup%d_%d_%d" % (hg, tc, h2))
                    ctx.waitm(s, "at_mm23_%d_%d_%d" % (hg, tc, h2))
                    ctx.inc(s.copy(out=t["sbf"][:, 128 * h2:128 * h2 + 128],
                                   in_=t["sfull"][:, 128 * h2:128 * h2 + 128]),
                            "scl", mark="at_sbf%d_%d_%d" % (hg, tc, h2))

        ctx.setmark("at_scandone%d" % hg, "ch5")
        ctx.setmark("at_scanmm%d" % hg, "ten")


def emit_wout(ctx):
    block, t = ctx.block, ctx.t
    pG = ctx.ps["pG"]

    @block.gpsimd
    def _(g):
        ctx.inc(g.dma_start(
            out=ctx.t["xbig"][:].rearrange("p (fc o) -> p fc o", fc=NFC),
            in_=ctx.io["wout"].rearrange("(fc p) o -> p fc o", p=128),
            max_dma_last_dim=8192), "dgp", 16, mark="w3_w")

    bcast_dram_row(ctx, ctx.io["bout_r"], "bvb", "w3b")

    for tc in range(NTC):
        buf = tc % 2

        @block.sync
        def _(sy, tc=tc, buf=buf):
            if tc >= 2:
                ctx.waitm(sy, "w3_e%d" % ((tc - 2) * 4 + 3))
            ctx.inc(sy.dma_start(out=t["xst3"][:, 2048 * buf:2048 * buf + 2048],
                                 in_=ctx.io["xtok"][128 * tc:128 * tc + 128, :]),
                    "da" if buf == 0 else "db", 16, mark="w3_xd%d" % tc)

        for fg in range(4):
            gi = tc * 4 + fg
            pb = pG[:, 512 * (gi % 2):512 * (gi % 2) + 512]

            @block.tensor
            def _(te, tc=tc, fg=fg, gi=gi, pb=pb):
                if gi == 0:
                    ctx.waitm(te, "w3_w")
                if gi >= 2:
                    ctx.waitm(te, "w3_e%d" % (gi - 2))
                last = None
                for fc in range(NFC):
                    last = te.matmul(
                        pb, ctx.t["att"][:, 2048 * fc + 128 * tc:
                                         2048 * fc + 128 * tc + 128],
                        ctx.t["xbig"][:, 2048 * fc + 512 * fg:
                                      2048 * fc + 512 * fg + 512],
                        start=(fc == 0), stop=(fc == NFC - 1))
                ctx.inc(last, "ten", mark="w3_m%d" % gi)

            @block.vector
            def _(v, tc=tc, fg=fg, gi=gi, pb=pb, buf=buf):
                ctx.waitm(v, "w3_m%d" % gi)
                if fg == 0:
                    ctx.waitm(v, "w3_xd%d" % tc)
                    if tc == 0:
                        ctx.waitm(v, "bc_bvb")
                    if tc >= 2:
                        ctx.waitm(v, "w3_od%d" % (tc - 2))
                ins = v.tensor_tensor(out=t["ot3"][:, 2048 * buf + 512 * fg:
                                                   2048 * buf + 512 * fg + 512],
                                      in0=pb, in1=t["bvb"][:, 512 * fg:512 * fg + 512],
                                      op=ALU.add)
                ctx.fence(v, ins, "fv")
                ctx.inc(v.tensor_tensor(
                    out=t["ot3"][:, 2048 * buf + 512 * fg:2048 * buf + 512 * fg + 512],
                    in0=t["ot3"][:, 2048 * buf + 512 * fg:2048 * buf + 512 * fg + 512],
                    in1=t["xst3"][:, 2048 * buf + 512 * fg:2048 * buf + 512 * fg + 512],
                    op=ALU.add), "ch1", mark="w3_e%d" % gi)

        @block.gpsimd
        def _(g, tc=tc, buf=buf):
            ctx.waitm(g, "w3_e%d" % (tc * 4 + 3))
            ctx.inc(g.dma_start(out=ctx.io["out"][128 * tc:128 * tc + 128, :],
                                in_=t["ot3"][:, 2048 * buf:2048 * buf + 2048],
                                accum_op=ALU.add), "di" if buf == 0 else "dj",
                    16, mark="w3_od%d" % tc)


def declare_io(nc, io, debug=False):
    io["xtok"] = nc.dram_tensor("xtok", [T, D], F32, kind="ExternalInput").ap()
    io["xT"] = nc.dram_tensor("xT", [D, T], F32, kind="ExternalInput").ap()
    io["wqkv"] = nc.dram_tensor("wqkv", [D, 3 * D], F32, kind="ExternalInput").ap()
    io["wout"] = nc.dram_tensor("wout", [D, D], F32, kind="ExternalInput").ap()
    io["wgS"] = nc.dram_tensor("wgS", [NIC, 128, D], F32, kind="ExternalInput").ap()
    io["wuS"] = nc.dram_tensor("wuS", [NIC, 128, D], F32, kind="ExternalInput").ap()
    io["wdT"] = nc.dram_tensor("wdT", [FI, D], F32, kind="ExternalInput").ap()
    io["gsub_t"] = nc.dram_tensor("gsub_t", [128, 64], F32, kind="ExternalInput").ap()
    io["wmsl"] = nc.dram_tensor("wmsl", [3, 128, 16384], F32, kind="ExternalInput").ap()
    io["bqkv_t"] = nc.dram_tensor("bqkv_t", [128, 48], F32, kind="ExternalInput").ap()
    io["slopes_r"] = nc.dram_tensor("slopes_r", [1, 16], F32, kind="ExternalInput").ap()
    io["bqv"] = nc.dram_tensor("bqv", [1, D], F32, kind="ExternalInput").ap()
    io["bout_r"] = nc.dram_tensor("bout_r", [1, D], F32, kind="ExternalInput").ap()
    io["mrecv"] = nc.dram_tensor("mrecv", [1, 1], F32, kind="ExternalInput").ap()
    io["msend"] = nc.dram_tensor("msend", [1, 1], F32, kind="ExternalInput").ap()
    io["out"] = nc.dram_tensor("out", [T, D], F32, kind="ExternalOutput").ap()
    io["rgd"] = nc.dram_tensor("rgd", [FI, T], F32).ap()
    io["wdq"] = nc.dram_tensor("wdq", [FI, D], BF16).ap()
    io["ccw_in"] = nc.dram_tensor("ccw_in", [1, 8], F32).ap()
    io["ccs_in"] = nc.dram_tensor("ccs_in", [2048, 128], F32).ap()
    io["ccs_out"] = nc.dram_tensor("ccs_out", [2048, 128], F32).ap()
    io["ccw_out"] = nc.dram_tensor("ccw_out", [1, 8], F32).ap()
    if debug:
        io["dbg_xq"] = nc.dram_tensor("dbg_xq", [128, 16 * T], BF16,
                                      kind="ExternalOutput").ap()
        io["dbg_st"] = nc.dram_tensor("dbg_st", [128, 16 * 16], F32,
                                      kind="ExternalOutput").ap()
        io["dbg_rgd"] = nc.dram_tensor("dbg_rgd", [FI, T], F32,
                                       kind="ExternalOutput").ap()
        io["dbg_qscb"] = nc.dram_tensor("dbg_qscb", [128, T], F32,
                                        kind="ExternalOutput").ap()


def alloc_common(ctx, es):
    a = ctx.alloc
    a(es, "xbig", [128, 16 * T], BF16)
    a(es, "ones1f", [1, 128], F32); a(es, "onescf", [128, 1], F32)
    a(es, "identf", [128, 128], F32)
    a(es, "iota_p", [128, 1], F32)
    a(es, "c_v1", [128, 4], F32)
    a(es, "c_t16a", [128, 16], F32); a(es, "c_t16b", [128, 16], F32)
    a(es, "c_t16x", [16, 128], F32)
    a(es, "ident", [128, 128], BF16)
    a(es, "slr", [1, 16], F32); a(es, "slb", [128, 16], F32)
    a(es, "nslb", [128, 16], F32)
    a(es, "dhv", [128, 16], F32); a(es, "kdv", [128, 16], F32)
    a(es, "qdt", [128, 16], F32)
    a(es, "khw", [128, 256], F32)
    a(es, "bqkvs", [128, 48], F32); a(es, "gsub", [128, 64], F32)
    a(es, "bqvr", [1, D], F32); a(es, "boutr", [1, D], F32)
    a(es, "mrecvs", [1, 1], F32); a(es, "msends", [1, 1], F32)
    a(es, "swb", [128, 16], F32)
    a(es, "msum", [128, 16], F32); a(es, "sqsum", [128, 16], F32)
    a(es, "m_t", [128, 16], F32); a(es, "rstd_t", [128, 16], F32)
    a(es, "mx_t", [128, 16], F32); a(es, "sx_t", [128, 16], F32)
    a(es, "invsx_t", [128, 16], F32)
    a(es, "st_a", [128, 16], F32); a(es, "st_b", [128, 16], F32)
    a(es, "st_c", [128, 16], F32); a(es, "st_d", [128, 16], F32)
    a(es, "mxz_t", [128, 16], F32); a(es, "sq_t", [128, 16], F32)
    a(es, "rstdz_t", [128, 16], F32); a(es, "sz_t", [128, 16], F32)
    a(es, "qsc_t", [128, 16], F32); a(es, "invsd_t", [128, 16], F32)
    a(es, "cpool", [128, 2], F32)
    ctx.nc.const_aps.register(F32, LN_EPS, ctx.t["cpool"][:, 0:1])
    ctx.nc.const_aps.register(F32, SUB_EPS, ctx.t["cpool"][:, 1:2])


def build_full(debug=False, upto=99):
    nc = bass.Bass("TRN2", target_bir_lowering=False)
    es = ExitStack()
    io = {}
    declare_io(nc, io, debug)
    block = es.enter_context(nc.Block())
    ctx = Ctx(nc, block, es)
    ctx.io = io
    for s in ["dsy", "dgp", "dsc", "ten", "vec", "scl", "gps", "cc",
              "ch0", "ch1", "ch2", "ch3", "ch4", "ch5", "fv", "fs", "fg",
              "da", "db", "dc", "dd", "de", "df", "dg", "dh", "di", "dj"]:
        ctx.add_sem(s)
    for e in ENGS:
        ctx.add_sem("b_" + e)

    ctx.ps["pG"] = es.enter_context(nc.psum_tensor("pG", [128, 2048], F32))
    ctx.ps["pU"] = es.enter_context(nc.psum_tensor("pU", [128, 2048], F32))
    ctx.ps["pD"] = ctx.ps["pG"]
    ctx.ps["pA"] = ctx.ps["pG"]
    ctx.ps["pB"] = ctx.ps["pU"]

    alloc_common(ctx, es)
    a = ctx.alloc

    es_early = ExitStack()
    a(es_early, "c_qdrow", [1, 2048], F32)
    a(es_early, "ti32", [128, 2048], I32)

    with ExitStack() as es_c:
        a(es_c, "c_row", [128, 128], F32); a(es_c, "c_col", [128, 128], F32)
        a(es_c, "c_ige", [128, 128], F32); a(es_c, "c_diff", [128, 128], F32)
        a(es_c, "c_dte", [128, 256], F32)
        if upto >= 0:
            emit_const(ctx)
        ctx.barrier()

    with ExitStack() as es_w:
        with ExitStack() as es_wm:
            a(es_wm, "wm_stage", [128, 8192], F32)
            a(es_wm, "wm_acc", [128, 16], F32)
            a(es_wm, "wm_p", [128, 4], F32); a(es_wm, "wm_r", [1, 16], F32)
            if upto >= 1:
                emit_wmean(ctx)
            ctx.barrier()

        with ExitStack() as es_p0:
            a(es_p0, "rrow", [1, 2048], F32)
            a(es_p0, "xh", [128, 16 * 1024], F32)
            a(es_p0, "mxa", [128, 1024], F32)
            ctx.t["sbq"] = ctx.t["mxa"]
            a(es_p0, "mbf", [128, 2048], F32)
            a(es_p0, "rbf", [128, 2048], F32)
            if upto >= 2:
                emit_p0(ctx)
            ctx.barrier()
            if debug:
                @block.sync
                def _(sy):
                    ctx.inc(sy.dma_start(out=io["dbg_xq"], in_=ctx.t["xbig"][:]),
                            "dsy", 16)
                ctx.barrier()

    with ExitStack() as es_q:
        a(es_q, "qscb", [128, 2048], F32)
        with ExitStack() as es_gu:
            a(es_gu, "gsq", [128, 64], F32)
            a(es_gu, "wstg_g", [128, 4096], F32); a(es_gu, "wstg_u", [128, 4096], F32)
            a(es_gu, "wqg", [128, 4096], BF16); a(es_gu, "wqu", [128, 4096], BF16)
            a(es_gu, "mxz", [128, 2048], F32); a(es_gu, "sq", [128, 2048], F32)
            a(es_gu, "gu_t2", [128, 1024], F32)
            a(es_gu, "gu_r", [128, 1024], F32); a(es_gu, "gu_t3", [128, 1024], F32)
            a(es_gu, "rgt", [128, 2048], F32)
            a(es_gu, "wdstg", [128, 2048], F32)
            a(es_gu, "wdqb", [128, 2048], BF16)
            if upto >= 3:
                emit_gateup(ctx)
            ctx.barrier()

        with ExitStack() as es_dn:
            a(es_dn, "rstg", [128, 1024], F32)
            a(es_dn, "tiq", [128, 1024], I32)
            a(es_dn, "wdt2", [128, 2048], BF16)
            a(es_dn, "ot", [128, 8192], F32)
            if upto >= 4:
                emit_down(ctx)
            ctx.barrier()
        if debug:
            @block.sync
            def _(sy):
                ctx.inc(sy.dma_start(out=io["dbg_qscb"], in_=ctx.t["qscb"][:]),
                        "dsy", 16)
            ctx.barrier()
    es_early.close()

    with ExitStack() as es_xn:
        a(es_xn, "xst2", [128, 4096], F32)
        a(es_xn, "mbf2", [128, 2048], F32)
        a(es_xn, "rbf2", [128, 2048], F32)
        a(es_xn, "c_qdrow2", [1, 2048], F32)
        a(es_xn, "rrow2", [1, 2048], F32)
        if upto >= 5:
            emit_xn(ctx)
        ctx.barrier()

    with ExitStack() as es_late:
        a(es_late, "att", [128, 16 * T], BF16)
        a(es_late, "bqvr2", [1, 512], F32)
        a(es_late, "bvb", [128, 2048], BF16)
        a(es_late, "qd", [128, 2048], BF16)
        a(es_late, "dt", [128, 2048], BF16)
        with ExitStack() as es_d:
            a(es_d, "d_row", [128, 128], F32)
            a(es_d, "d_col", [128, 128], F32)
            a(es_d, "d_ige", [128, 128], F32)
            a(es_d, "d_diff", [128, 128], F32)
            a(es_d, "d_dte", [128, 256], F32)
            if upto >= 6:
                emit_decay(ctx)
            ctx.barrier()
        with ExitStack() as es_at:
            a(es_at, "wstg2", [128, 4096], BF16)
            a(es_at, "qt", [128, 4096], BF16)
            a(es_at, "kt", [128, 4096], BF16)
            a(es_at, "vtok", [128, 4096], BF16)
            a(es_at, "ktok", [128, 4096], BF16)
            a(es_at, "ssend", [128, 256], F32)
            ctx.t["srecv"] = ctx.t["ssend"]
            ctx.t["sfull"] = ctx.t["ssend"]
            a(es_at, "sbf", [128, 256], BF16)
            a(es_at, "stsb", [128, 512], BF16)
            a(es_at, "qtb", [128, 512], BF16)
            a(es_at, "ktmp2", [128, 512], BF16)
            ctx.t["ktmp"] = ctx.t["ktmp2"]
            a(es_at, "msb", [128, 1], F32)
            a(es_at, "mrb", [128, 1], F32)
            if upto >= 6:
                emit_attn(ctx)
            ctx.barrier()

        with ExitStack() as es_w3:
            a(es_w3, "xst3", [128, 4096], F32)
            a(es_w3, "ot3", [128, 4096], F32)
            if upto >= 7:
                emit_wout(ctx)
            ctx.barrier()

    if debug:
        @block.sync
        def _(sy):
            for i, n in enumerate(["m_t", "rstd_t", "mx_t", "sx_t", "invsx_t",
                                   "swb", "mxz_t", "sq_t", "rstdz_t", "sz_t",
                                   "qsc_t", "invsd_t"]):
                ctx.inc(sy.dma_start(out=io["dbg_st"][:, 16 * i:16 * i + 16],
                                     in_=ctx.t[n][:]), "dsy", 16)
            ctx.inc(sy.dma_start(out=io["dbg_rgd"], in_=io["rgd"]), "dsy", 16)

    @block.sync
    def _(sy):
        for s_ in ["dsy", "da", "db", "dc", "dd", "de", "df", "dg"]:
            ctx.wait(sy, s_)
    @block.vector
    def _(v):
        for s_ in ["dh", "di", "dj"]:
            ctx.wait(v, s_)
    @block.gpsimd
    def _(g):
        ctx.wait(g, "dgp")
    @block.scalar
    def _(s):
        ctx.wait(s, "dsc")

    es.close()
    return nc


def shard_inputs(L0_attn_x, L0_attn_Wqkv, L0_attn_bqkv, L0_attn_Wout,
                 L0_attn_bout, L0_attn_slopes, L0_ffn_w_gate, L0_ffn_w_up,
                 L0_ffn_w_down, L0_ffn_subln_weight):
    f32 = np.float32
    wgS = np.ascontiguousarray(np.asarray(L0_ffn_w_gate, f32).T
                               .reshape(16, 128, 64, 128).transpose(2, 1, 0, 3)
                               .reshape(64, 128, 2048))
    wuS = np.ascontiguousarray(np.asarray(L0_ffn_w_up, f32).T
                               .reshape(16, 128, 64, 128).transpose(2, 1, 0, 3)
                               .reshape(64, 128, 2048))
    wdT = np.ascontiguousarray(np.asarray(L0_ffn_w_down, f32).T)
    gsub_t = np.ascontiguousarray(np.asarray(L0_ffn_subln_weight, f32).reshape(64, 128).T)
    bqkv = np.asarray(L0_attn_bqkv, f32)
    bqkv_t = np.ascontiguousarray(bqkv.reshape(48, 128).T)
    slopes_r = np.ascontiguousarray(np.asarray(L0_attn_slopes, f32).reshape(1, 16))
    bqv = np.ascontiguousarray(bqkv[4096:].reshape(1, D))
    bout_r = np.ascontiguousarray(np.asarray(L0_attn_bout, f32).reshape(1, D))
    wg_f = np.asarray(L0_ffn_w_gate, f32).reshape(-1)
    wu_f = np.asarray(L0_ffn_w_up, f32).reshape(-1)
    wd_f = np.asarray(L0_ffn_w_down, f32).reshape(-1)
    x = np.asarray(L0_attn_x, f32)
    maps = []
    n8 = 2097152
    for c in range(8):
        b, hf = c // 2, c % 2
        xs = np.ascontiguousarray(x[b, 2048 * hf:2048 * hf + 2048, :])
        xTs = np.ascontiguousarray(xs.T)
        wmsl = np.ascontiguousarray(np.stack([
            wg_f[c * n8:(c + 1) * n8].reshape(128, 16384),
            wu_f[c * n8:(c + 1) * n8].reshape(128, 16384),
            wd_f[c * n8:(c + 1) * n8].reshape(128, 16384)]))
        maps.append({
            "xtok": xs, "xT": xTs, "wgS": wgS, "wuS": wuS, "wdT": wdT,
            "wqkv": np.ascontiguousarray(np.asarray(L0_attn_Wqkv, f32)),
            "wout": np.ascontiguousarray(np.asarray(L0_attn_Wout, f32)),
            "gsub_t": gsub_t, "wmsl": wmsl,
            "bqkv_t": bqkv_t, "slopes_r": slopes_r, "bqv": bqv,
            "bout_r": bout_r,
            "mrecv": np.array([[float(hf)]], dtype=f32),
            "msend": np.array([[float(1 - hf)]], dtype=f32),
        })
    return maps


_NC_CACHE = {}


def kernel(**inputs):
    if "nc" not in _NC_CACHE:
        _NC_CACHE["nc"] = build_full(debug=False)
    nc = _NC_CACHE["nc"]
    maps = shard_inputs(**inputs)
    res = run_bass_kernel_spmd(nc, maps, core_ids=list(range(8)))
    out = np.concatenate([np.asarray(res.results[c]["out"], dtype=np.float32)
                          .reshape(1, T, D) for c in range(8)])
    return out.reshape(4, 4096, D)


# version 06 (final)
# speedup vs baseline: 1.2728x; 1.2642x over previous
import numpy as np
from contextlib import ExitStack

import concourse.bass as bass
import concourse.mybir as mybir
from concourse.bass_utils import run_bass_kernel_spmd

F32 = mybir.dt.float32
BF16 = mybir.dt.bfloat16
I32 = mybir.dt.int32
AF = mybir.ActivationFunctionType
ALU = mybir.AluOpType
AX = mybir.AxisListType

D = 2048
T = 2048
HEADS = 16
HD = 128
FI = 8192
NFC = D // 128
NTC = T // 128
NIC = FI // 128
NHG = HEADS // 2
LN_EPS = 1e-5
SUB_EPS = 1e-6

ENGS = ["sync", "gpsimd", "vector", "scalar", "tensor"]


class Ctx:
    def __init__(self, nc, block, es):
        self.nc, self.block, self.es = nc, block, es
        self.sem, self.cnt = {}, {}
        self.mark = {}
        self.io = {}
        self.ps = {}
        self.t = {}

    def add_sem(self, name):
        self.sem[name] = self.es.enter_context(self.nc.semaphore(name))
        self.cnt[name] = 0

    def inc(self, instr, name, amt=1, mark=None):
        instr.then_inc(self.sem[name], amt)
        self.cnt[name] += amt
        if mark is not None:
            self.mark[mark] = (name, self.cnt[name])
        return self.cnt[name]

    def setmark(self, mark, name):
        self.mark[mark] = (name, self.cnt[name])

    def wait(self, eng, name, value=None):
        eng.wait_ge(self.sem[name], self.cnt[name] if value is None else value)

    def waitm(self, eng, mark):
        name, val = self.mark[mark]
        eng.wait_ge(self.sem[name], val)

    def fence(self, eng, instr, name):
        self.inc(instr, name)
        self.wait(eng, name)

    def barrier(self):
        vals = {}
        allsems = ["dsy", "dgp", "dsc", "ten", "vec", "scl", "gps", "cc",
                   "ch0", "ch1", "ch2", "ch3", "ch4", "ch5", "fv", "fs", "fg",
                   "da", "db", "dc", "dd", "de", "df", "dg", "dh", "di", "dj"]
        cur = {s: self.cnt[s] for s in allsems if s in self.cnt}
        for e in ENGS:
            self.cnt["b_" + e] += 1
            vals[e] = self.cnt["b_" + e]

        def mk(e):
            def _(eng):
                for s, v in cur.items():
                    if v:
                        eng.wait_ge(self.sem[s], v)
                eng.sem_inc(self.sem["b_" + e], 1)
                for o in ENGS:
                    if o != e:
                        eng.wait_ge(self.sem["b_" + o], vals[o])
            return _
        for e in ENGS:
            getattr(self.block, e)(mk(e))

    def alloc(self, es, name, shape, dtype):
        t = es.enter_context(self.nc.sbuf_tensor(name, shape, dtype))
        self.t[name] = t
        return t



def nrecip(ctx, v, out_ap, in_ap, tmp_ap):
    ins = v.reciprocal(out=out_ap, in_=in_ap)
    ctx.fence(v, ins, "fv")
    ins = v.tensor_tensor(out=tmp_ap, in0=out_ap, in1=in_ap, op=ALU.mult)
    ctx.fence(v, ins, "fv")
    ins = v.tensor_scalar(out=tmp_ap, in0=tmp_ap, scalar1=-1.0, scalar2=2.0,
                          op0=ALU.mult, op1=ALU.add)
    ctx.fence(v, ins, "fv")
    ins = v.tensor_tensor(out=out_ap, in0=out_ap, in1=tmp_ap, op=ALU.mult)
    ctx.fence(v, ins, "fv")



def bcast_dram_row(ctx, io_ap, dstname, tag):
    block, t = ctx.block, ctx.t
    for k in range(4):
        @block.sync
        def _(sy, k=k):
            if k > 0:
                ctx.waitm(sy, "bdm_%s%d" % (tag, k - 1))
            ctx.inc(sy.dma_start(out=t["bqvr2"][0:1, 0:512],
                                 in_=io_ap[0:1, 512 * k:512 * k + 512]),
                    "dc", 16, mark="bdr_%s%d" % (tag, k))

        @block.tensor
        def _(te, k=k):
            ctx.waitm(te, "bdr_%s%d" % (tag, k))
            if k == 0 and "bc_pBfree" in ctx.mark:
                ctx.waitm(te, "bc_pBfree")
            if k > 0:
                ctx.waitm(te, "bde_%s%d" % (tag, k - 1))
            ctx.inc(te.matmul(ctx.ps["pB"][:, 0:512], t["ones1f"][:],
                              t["bqvr2"][0:1, 0:512], start=True, stop=True),
                    "ten", mark="bdm_%s%d" % (tag, k))

        @block.scalar
        def _(s, k=k):
            ctx.waitm(s, "bdm_%s%d" % (tag, k))
            ctx.inc(s.copy(out=t[dstname][:, 512 * k:512 * k + 512],
                           in_=ctx.ps["pB"][:, 0:512]), "scl",
                    mark="bde_%s%d" % (tag, k))
    ctx.setmark("bc_%s" % dstname, "scl")
    ctx.setmark("bc_pBfree", "scl")


def emit_const(ctx):
    block, t = ctx.block, ctx.t

    @block.sync
    def _(sy):
        for name, dst in [("bqkv_t", t["bqkvs"]), ("gsub_t", t["gsub"]),
                          ("slopes_r", t["slr"]), ("mrecv", t["mrecvs"]),
                          ("msend", t["msends"])]:
            ctx.inc(sy.dma_start(out=dst[:], in_=ctx.io[name]), "dsy", 16)
        ctx.setmark("c_loads", "dsy")

    @block.gpsimd
    def _(g):
        ctx.inc(g.iota(out=t["iota_p"][:], pattern=[[0, 1]], base=0,
                       channel_multiplier=1, allow_small_or_imprecise_dtypes=True),
                "gps")
        ctx.inc(g.iota(out=t["c_row"][:], pattern=[[1, 128]], base=0,
                       channel_multiplier=0, allow_small_or_imprecise_dtypes=True),
                "gps")
        ctx.inc(g.iota(out=t["c_col"][:], pattern=[[0, 128]], base=0,
                       channel_multiplier=1,
                       allow_small_or_imprecise_dtypes=True), "gps", mark="c_iota")

    @block.vector
    def _(v):
        ctx.inc(v.memset(t["ones1f"][:], 1.0), "vec")
        ctx.inc(v.memset(t["onescf"][:], 1.0), "vec")
        ctx.inc(v.memset(t["cpool"][:, 0:1], LN_EPS), "vec")
        ctx.inc(v.memset(t["cpool"][:, 1:2], SUB_EPS), "vec")
        ctx.waitm(v, "c_iota")
        ctx.inc(v.tensor_tensor(out=t["identf"][:], in0=t["c_row"][:],
                                in1=t["c_col"][:], op=ALU.is_equal), "vec")
        ctx.inc(v.tensor_tensor(out=t["ident"][:], in0=t["c_row"][:],
                                in1=t["c_col"][:], op=ALU.is_equal), "vec",
                mark="c_base")

    @block.tensor
    def _(te):
        ctx.waitm(te, "c_loads")
        ctx.waitm(te, "c_base")
        ctx.inc(te.matmul(ctx.ps["pA"][:, 0:16], t["ones1f"][:], t["slr"][:],
                          start=True, stop=True), "ten", mark="c_slmm")

    @block.scalar
    def _(s):
        ctx.waitm(s, "c_slmm")
        ctx.fence(s, s.copy(out=t["slb"][:], in_=ctx.ps["pA"][:, 0:16]), "fs")
        ctx.inc(s.activation(out=t["dhv"][:], in_=t["slb"][:], func=AF.Exp,
                             scale=-128.0), "scl", mark="c_slb")

    @block.vector
    def _(v):
        ctx.waitm(v, "c_slb")
        for tc in range(NTC):
            ctx.inc(v.tensor_scalar(out=t["st_a"][:, tc:tc + 1], in0=t["iota_p"][:],
                                    scalar1=float(128 * tc - 2048), scalar2=None,
                                    op0=ALU.add), "vec")
        ins = v.tensor_scalar(out=t["st_b"][:, 0:1], in0=t["iota_p"][:],
                              scalar1=-128.0, scalar2=None, op0=ALU.add)
        ctx.fence(v, ins, "fv")
        ctx.inc(v.tensor_scalar(out=t["c_t16a"][:], in0=t["slb"][:],
                                scalar1=t["st_b"][:, 0:1].opt(), scalar2=None,
                                op0=ALU.mult), "vec")
        ctx.inc(v.tensor_scalar(out=t["c_t16b"][:], in0=t["slb"][:],
                                scalar1=t["iota_p"][:, 0:1].opt(), scalar2=-1.0,
                                op0=ALU.mult, op1=ALU.mult), "vec")
        for tc in range(NTC):
            ctx.inc(v.tensor_scalar(out=t["khw"][:, 16 * tc:16 * tc + 16],
                                    in0=t["slb"][:],
                                    scalar1=t["st_a"][:, tc:tc + 1].opt(),
                                    scalar2=None, op0=ALU.mult), "vec")
        ctx.inc(v.tensor_scalar(out=t["nslb"][:], in0=t["slb"][:], scalar1=-1.0,
                                scalar2=None, op0=ALU.mult), "vec", mark="c_pre")

    @block.scalar
    def _(s):
        ctx.waitm(s, "c_pre")
        ctx.inc(s.activation(out=t["kdv"][:], in_=t["c_t16a"][:], func=AF.Exp),
                "scl")
        ctx.inc(s.activation(out=t["qdt"][:], in_=t["c_t16b"][:], func=AF.Exp),
                "scl")
        ctx.inc(s.activation(out=t["khw"][:], in_=t["khw"][:], func=AF.Exp),
                "scl", mark="c_exps")



def bcast_row(ctx, rowname, rowoff, width, dstname, wait_mark=None, dtype_note=""):
    block, t = ctx.block, ctx.t
    nk = width // 512
    for k in range(nk):
        @block.tensor
        def _(te, k=k):
            if k == 0 and wait_mark is not None:
                ctx.waitm(te, wait_mark)
            if k == 0 and "bc_pBfree" in ctx.mark:
                ctx.waitm(te, "bc_pBfree")
            if k > 0:
                ctx.waitm(te, "bc_%s%d" % (dstname, k - 1))
            ctx.inc(te.matmul(ctx.ps["pB"][:, 0:512], t["ones1f"][:],
                              t[rowname][0:1, rowoff + 512 * k:rowoff + 512 * k + 512],
                              start=True, stop=True), "ten",
                    mark="bcm_%s%d" % (dstname, k))

        @block.scalar
        def _(s, k=k):
            ctx.waitm(s, "bcm_%s%d" % (dstname, k))
            ctx.inc(s.copy(out=t[dstname][:, 512 * k:512 * k + 512],
                           in_=ctx.ps["pB"][:, 0:512]), "scl",
                    mark="bc_%s%d" % (dstname, k))
    ctx.setmark("bc_%s" % dstname, "scl")
    ctx.setmark("bc_pBfree", "scl")


def row_from_cols(ctx, src_ap, ncols, rowname, rowoff, tag):
    block, t = ctx.block, ctx.t

    @block.tensor
    def _(te):
        ctx.wait(te, "vec")
        ctx.wait(te, "scl")
        ctx.inc(te.transpose(out=ctx.ps["pA"][0:ncols, 0:128], in_=src_ap,
                             identity=t["identf"][:]), "ten", mark="rf_t" + tag)

    @block.scalar
    def _(s):
        ctx.waitm(s, "rf_t" + tag)
        ctx.wait(s, "dg")
        ctx.inc(s.copy(out=t["c_t16x"][0:ncols, :], in_=ctx.ps["pA"][0:ncols, 0:128]),
                "scl", mark="rf_c" + tag)

    @block.gpsimd
    def _(g):
        ctx.waitm(g, "rf_c" + tag)
        ctx.inc(g.dma_start(out=t[rowname][0:1, rowoff:rowoff + 128 * ncols],
                            in_=t["c_t16x"][0:ncols, :]), "dg", 16, mark="rf_d" + tag)


def emit_wmean(ctx):
    block, t = ctx.block, ctx.t
    stg, acc = t["wm_stage"], t["wm_acc"]

    for idx in range(12):
        w, k, buf = idx // 4, idx % 4, idx % 2

        @block.sync
        def _(sy, idx=idx, w=w, k=k, buf=buf):
            if idx >= 2:
                ctx.waitm(sy, "wm_a%d" % (idx - 2))
            ctx.inc(sy.dma_start(out=stg[:, 4096 * buf:4096 * buf + 4096],
                                 in_=ctx.io["wmsl"][w, :, 4096 * k:4096 * k + 4096]),
                    "da" if buf == 0 else "db", 16, mark="wm_d%d" % idx)

        @block.scalar
        def _(s, idx=idx, buf=buf):
            ctx.waitm(s, "wm_d%d" % idx)
            sl = stg[:, 4096 * buf:4096 * buf + 4096]
            ctx.inc(s.activation(out=sl, in_=sl, func=AF.Abs,
                                 accum_out=acc[:, idx:idx + 1]),
                    "scl", mark="wm_a%d" % idx)

    @block.vector
    def _(v):
        ctx.waitm(v, "wm_a11")
        for w in range(3):
            ctx.inc(v.tensor_reduce(out=t["wm_p"][:, w:w + 1], in_=acc[:, 4 * w:4 * w + 4],
                                    axis=AX.X, op=ALU.add), "vec")
        ctx.inc(v.memset(t["wm_p"][:, 3:4], 0.0), "vec", mark="wm_p")

    @block.tensor
    def _(te):
        ctx.waitm(te, "wm_p")
        ctx.inc(te.matmul(ctx.ps["pA"][0:1, 0:4], t["onescf"][:], t["wm_p"][:, 0:4],
                          start=True, stop=True), "ten", mark="wm_mm")

    @block.scalar
    def _(s):
        ctx.waitm(s, "wm_mm")
        ctx.fence(s, s.copy(out=t["wm_r"][0:1, 0:4], in_=ctx.ps["pA"][0:1, 0:4]),
                  "fs")
        ctx.inc(s.memzero(t["wm_r"][0:1, 4:8]), "scl", mark="wm_r0")

    @block.gpsimd
    def _(g):
        ctx.waitm(g, "wm_r0")
        ctx.inc(g.dma_start(out=ctx.io["ccw_in"], in_=t["wm_r"][0:1, 0:8]),
                "dgp", 16)
        ctx.wait(g, "dgp")
        ctx.inc(g.collective_compute(
            "AllReduce", ALU.add, replica_groups=[list(range(8))],
            ins=[ctx.io["ccw_in"]], outs=[ctx.io["ccw_out"]]), "cc")
        ctx.wait(g, "cc")
        ctx.inc(g.dma_start(out=t["wm_r"][0:1, 0:8], in_=ctx.io["ccw_out"]),
                "dgp", 16, mark="wm_rb")

    @block.vector
    def _(v):
        ctx.waitm(v, "wm_rb")
        ins = v.tensor_scalar(out=t["wm_r"][0:1, 4:7], in0=t["wm_r"][0:1, 0:3],
                              scalar1=1.0 / 16777216.0, scalar2=1e-5, op0=ALU.mult,
                              op1=ALU.max)
        ctx.fence(v, ins, "fv")
        ctx.inc(v.memset(t["wm_r"][0:1, 7:8], 1.0), "vec")
        ctx.inc(v.memset(t["wm_r"][0:1, 11:16], 1.0), "vec")
        nrecip(ctx, v, t["wm_r"][0:1, 8:11], t["wm_r"][0:1, 4:7],
               t["wm_r"][0:1, 12:15])
        ctx.inc(v.memset(t["wm_r"][0:1, 11:16], 1.0), "vec", mark="wm_sc")

    @block.tensor
    def _(te):
        ctx.waitm(te, "wm_sc")
        ctx.inc(te.matmul(ctx.ps["pA"][:, 0:16], t["ones1f"][:], t["wm_r"][0:1, 0:16],
                          start=True, stop=True), "ten", mark="wm_bm")

    @block.scalar
    def _(s):
        ctx.waitm(s, "wm_bm")
        ctx.inc(s.copy(out=t["swb"][:], in_=ctx.ps["pA"][:, 0:16]), "scl",
                mark="wm_done")


def emit_p0(ctx):
    block, t = ctx.block, ctx.t
    xstat = t["xh"]

    for tc in range(NTC):
        buf = tc % 2

        @block.sync
        def _(sy, tc=tc, buf=buf):
            if tc >= 2:
                ctx.waitm(sy, "p0_s%d" % (tc - 2))
            ctx.inc(sy.dma_start(out=xstat[:, 2048 * buf:2048 * buf + 2048],
                                 in_=ctx.io["xtok"][128 * tc:128 * tc + 128, :]),
                    "da" if buf == 0 else "db", 16, mark="p0_d%d" % tc)

        @block.scalar
        def _(s, tc=tc, buf=buf):
            ctx.waitm(s, "p0_d%d" % tc)
            sl = xstat[:, 2048 * buf:2048 * buf + 2048]
            ins = s.activation(out=sl, in_=sl, func=AF.Copy, bias=0.0,
                               accum_out=t["msum"][:, tc:tc + 1])
            ctx.fence(s, ins, "fs")
            ctx.inc(s.activation(out=sl, in_=sl, func=AF.Square,
                                 accum_out=t["sqsum"][:, tc:tc + 1]),
                    "scl", mark="p0_s%d" % tc)

    @block.vector
    def _(v):
        ctx.waitm(v, "p0_s%d" % (NTC - 1))
        ctx.inc(v.tensor_scalar(out=t["m_t"][:], in0=t["msum"][:],
                                scalar1=1.0 / 2048.0, scalar2=None, op0=ALU.mult),
                "vec")
        ins = v.tensor_scalar(out=t["st_a"][:], in0=t["sqsum"][:],
                              scalar1=1.0 / 2048.0, scalar2=None, op0=ALU.mult)
        ctx.fence(v, ins, "fv")
        ins = v.tensor_tensor(out=t["st_b"][:], in0=t["m_t"][:], in1=t["m_t"][:],
                              op=ALU.mult)
        ctx.fence(v, ins, "fv")
        ctx.inc(v.tensor_tensor(out=t["st_a"][:], in0=t["st_a"][:], in1=t["st_b"][:],
                                op=ALU.subtract), "vec", mark="p0_var")

    @block.scalar
    def _(s):
        ctx.waitm(s, "p0_var")
        ctx.inc(s.activation(out=t["st_b"][:], in_=t["st_a"][:], func=AF.Sqrt,
                             bias=LN_EPS), "scl", mark="p0_sd")

    @block.vector
    def _(v):
        ctx.waitm(v, "p0_sd")
        nrecip(ctx, v, t["rstd_t"][:], t["st_b"][:], t["st_c"][:])
        ctx.inc(v.tensor_copy(out=t["st_c"][:], in_=t["rstd_t"][:]), "vec",
                mark="p0_rstd")

    row_from_cols(ctx, t["m_t"][:], 16, "c_qdrow", 0, "p0m")
    row_from_cols(ctx, t["rstd_t"][:], 16, "rrow", 0, "p0r")
    bcast_row(ctx, "c_qdrow", 0, 2048, "mbf", wait_mark="rf_dp0m")
    bcast_row(ctx, "rrow", 0, 2048, "rbf", wait_mark="rf_dp0r")

    for hf in range(2):
        hsl = slice(1024 * hf, 1024 * hf + 1024)

        @block.sync
        def _(sy, hf=hf):
            if hf == 1:
                ctx.wait(sy, "gps")
                ctx.wait(sy, "vec")
            else:
                ctx.waitm(sy, "bc_rbf")
            ctx.inc(sy.dma_start(
                out=t["xh"][:].rearrange("p (fc t) -> p fc t", fc=NFC),
                in_=ctx.io["xT"].rearrange("(fc p) t -> p fc t", p=128)[:, :, hsl]),
                "dc", 16, mark="p0_xh%d" % hf)

        @block.vector
        def _(v, hf=hf):
            if hf == 1:
                ctx.wait(v, "ten")
                ctx.waitm(v, "p0_q0_%d" % (NFC - 1))
            ctx.inc(v.memset(t["mxa"][:], 0.0), "vec", mark="p0_mx0_%d" % hf)

        for fc in range(NFC):
            @block.vector
            def _(v, hf=hf, fc=fc):
                if fc == 0:
                    ctx.waitm(v, "p0_xh%d" % hf)
                xsl = t["xh"][:, 1024 * fc:1024 * fc + 1024]
                ins = v.tensor_tensor(out=xsl, in0=xsl, in1=t["mbf"][:, hsl],
                                      op=ALU.subtract)
                ctx.fence(v, ins, "fv")
                ins = v.tensor_tensor(out=xsl, in0=xsl, in1=t["rbf"][:, hsl],
                                      op=ALU.mult)
                ctx.fence(v, ins, "fv")
                ins = v.scalar_tensor_tensor(out=t["mxa"][:], in0=xsl, scalar=1.0,
                                             in1=t["mxa"][:], op0=ALU.mult,
                                             op1=ALU.max)
                ctx.fence(v, ins, "fv")
                ctx.inc(v.scalar_tensor_tensor(out=t["mxa"][:], in0=xsl, scalar=-1.0,
                                               in1=t["mxa"][:], op0=ALU.mult,
                                               op1=ALU.max),
                        "vec", mark="p0_mx%d_%d" % (hf, fc))

        for tp in range(8):
            pbuf = "pA" if tp % 2 == 0 else "pB"

            @block.tensor
            def _(te, hf=hf, tp=tp, pbuf=pbuf):
                if tp == 0:
                    ctx.waitm(te, "p0_mx%d_%d" % (hf, NFC - 1))
                if tp >= 2:
                    ctx.waitm(te, "p0_red%d_%d" % (hf, tp - 2))
                ctx.inc(te.transpose(out=ctx.ps[pbuf][:, 0:128],
                                     in_=t["mxa"][:, 128 * tp:128 * tp + 128],
                                     identity=t["identf"][:]), "ten",
                        mark="p0_tp%d_%d" % (hf, tp))

            @block.vector
            def _(v, hf=hf, tp=tp, pbuf=pbuf):
                ctx.waitm(v, "p0_tp%d_%d" % (hf, tp))
                ctx.inc(v.tensor_reduce(out=t["mx_t"][:, 8 * hf + tp:8 * hf + tp + 1],
                                        in_=ctx.ps[pbuf][:, 0:128],
                                        axis=AX.X, op=ALU.max), "vec",
                        mark="p0_red%d_%d" % (hf, tp))

        c8 = slice(8 * hf, 8 * hf + 8)

        @block.vector
        def _(v, hf=hf, c8=c8):
            ctx.waitm(v, "p0_red%d_7" % hf)
            ins = v.tensor_scalar(out=t["st_a"][:, c8], in0=t["mx_t"][:, c8],
                                  scalar1=1e-5, scalar2=None, op0=ALU.max)
            ctx.fence(v, ins, "fv")
            nrecip(ctx, v, t["st_b"][:, c8], t["st_a"][:, c8], t["st_d"][:, c8])
            ins = v.tensor_scalar(out=t["sx_t"][:, c8], in0=t["st_b"][:, c8],
                                  scalar1=127.0, scalar2=None, op0=ALU.mult)
            ctx.fence(v, ins, "fv")
            nrecip(ctx, v, t["invsx_t"][:, c8], t["sx_t"][:, c8], t["st_d"][:, c8])
            ctx.inc(v.tensor_copy(out=t["st_d"][:, c8], in_=t["invsx_t"][:, c8]),
                    "vec", mark="p0_sx%d" % hf)

        row_from_cols(ctx, t["sx_t"][:, c8], 8, "rrow", 0, "p0x%d" % hf)
        bcast_row(ctx, "rrow", 0, 1024, "sbq", wait_mark="rf_dp0x%d" % hf)

        for fc in range(NFC):
            b2 = fc % 2

            @block.vector
            def _(v, hf=hf, fc=fc, b2=b2):
                if fc == 0:
                    ctx.waitm(v, "bc_sbq1")
                if fc >= 2:
                    ctx.waitm(v, "p0_cl%d_%d" % (hf, fc - 2))
                ctx.inc(v.tensor_tensor(out=t["ti32"][:, 1024 * b2:1024 * b2 + 1024],
                                        in0=t["xh"][:, 1024 * fc:1024 * fc + 1024],
                                        in1=t["sbq"][:, 0:1024], op=ALU.mult),
                        "vec", mark="p0_q%d_%d" % (hf, fc))

            @block.gpsimd
            def _(g, hf=hf, fc=fc, b2=b2):
                ctx.waitm(g, "p0_q%d_%d" % (hf, fc))
                dst = ctx.t["xbig"][:, 2048 * fc + 1024 * hf:
                                    2048 * fc + 1024 * hf + 1024]
                ctx.inc(g.tensor_scalar(out=dst,
                                        in0=t["ti32"][:, 1024 * b2:1024 * b2 + 1024],
                                        scalar1=127.0, scalar2=-128.0,
                                        op0=ALU.min, op1=ALU.max), "gps",
                        mark="p0_cl%d_%d" % (hf, fc))


def emit_gateup(ctx):
    block, t = ctx.block, ctx.t
    swb = t["swb"]

    @block.vector
    def _(v):
        ctx.inc(v.memset(t["mxz"][:], 0.0), "vec")
        ctx.inc(v.memset(t["sq"][:], 0.0), "vec")
        ctx.waitm(v, "c_loads")
        ctx.inc(v.tensor_tensor(out=t["gsq"][:], in0=t["gsub"][:], in1=t["gsub"][:],
                                op=ALU.mult), "vec", mark="gu_z")

    def gu_strips(ctx, ic):
        block = ctx.block

        @block.sync
        def _(sy, ic=ic):
            if ic >= 2:
                ctx.waitm(sy, "gu_q%d" % (ic - 2))
            ch = "dd" if ic % 2 == 0 else "dc"
            for wio, stg in [("wgS", t["wstg_g"]), ("wuS", t["wstg_u"])]:
                ctx.inc(sy.dma_start(out=stg[:, 2048 * (ic % 2):2048 * (ic % 2) + 2048],
                                     in_=ctx.io[wio][ic]), ch, 16)
            ctx.setmark("gu_w%d" % ic, ch)

    def gu_quant(ctx, ic):
        block = ctx.block
        buf = ic % 2

        @block.vector
        def _(v, ic=ic, buf=buf):
            ctx.waitm(v, "gu_w%d" % ic)
            if ic == 0:
                ctx.waitm(v, "gu_z")
            if ic >= 1:
                ctx.waitm(v, "gu_q%d" % (ic - 1))
            if ic >= 2:
                ctx.waitm(v, "gu_mm%d_1" % (ic - 2))
                ctx.waitm(v, "gu_wd2_%d" % (ic - 2))
            for wi, (stg, wq) in enumerate([(t["wstg_g"], t["wqg"]),
                                            (t["wstg_u"], t["wqu"])]):
                ins = v.tensor_scalar(out=t["ti32"][:, 0:2048],
                                      in0=stg[:, 2048 * buf:2048 * buf + 2048],
                                      scalar1=swb[:, 8 + wi:9 + wi].opt(),
                                      scalar2=None, op0=ALU.mult)
                ctx.fence(v, ins, "fv")
                ins = v.tensor_scalar(out=wq[:, 2048 * buf:2048 * buf + 2048],
                                      in0=t["ti32"][:, 0:2048],
                                      scalar1=1.0, scalar2=-1.0,
                                      op0=ALU.min, op1=ALU.max)
                if wi == 0:
                    ctx.fence(v, ins, "fv")
                else:
                    ctx.inc(ins, "ch1")
            ctx.setmark("gu_q%d" % ic, "ch1")

    gu_strips(ctx, 0)
    gu_quant(ctx, 0)
    gu_strips(ctx, 1)

    for ic in range(NIC):
        buf = ic % 2

        @block.sync
        def _(sy, ic=ic, buf=buf):
            if ic >= 1:
                ctx.waitm(sy, "gu_rg%d_1" % (ic - 1))
                ctx.inc(sy.dma_start(out=ctx.io["rgd"][128 * (ic - 1):128 * ic, :],
                                     in_=t["rgt"][:]), "de", 16,
                        mark="gu_rgd%d" % (ic - 1))
                ctx.waitm(sy, "gu_wd1_%d" % (ic - 1))
            ctx.inc(sy.dma_start(out=t["wdstg"][:],
                                 in_=ctx.io["wdT"][128 * ic:128 * ic + 128, :]),
                    "df", 16, mark="gu_wdd%d" % ic)
        if ic + 2 < NIC:
            gu_strips(ctx, ic + 2)
        if ic + 1 < NIC:
            gu_quant(ctx, ic + 1)

        for half in range(2):
            e = 2 * ic + half

            @block.tensor
            def _(te, ic=ic, buf=buf, half=half):
                if half == 0:
                    ctx.waitm(te, "gu_q%d" % ic)
                if ic >= 1:
                    ctx.waitm(te, "gu_rel%d_%d" % (ic - 1, half))
                    ctx.waitm(te, "gu_ps%d_%d" % (ic - 1, half))
                last = None
                for fc in range(NFC):
                    for gu, wq in [(0, t["wqg"]), (1, t["wqu"])]:
                        lhsT = wq[:, 2048 * buf + 128 * fc:2048 * buf + 128 * fc + 128]
                        for tg in (2 * half, 2 * half + 1):
                            ps = ctx.ps["pG" if gu == 0 else "pU"]
                            last = te.matmul(
                                ps[:, 512 * tg:512 * tg + 512], lhsT,
                                ctx.t["xbig"][:, 2048 * fc + 512 * tg:
                                              2048 * fc + 512 * tg + 512],
                                start=(fc == 0), stop=(fc == NFC - 1))
                ctx.inc(last, "ten", mark="gu_mm%d_%d" % (ic, half))

            @block.scalar
            def _(s, ic=ic, half=half, e=e):
                ctx.waitm(s, "gu_mm%d_%d" % (ic, half))
                if e >= 1:
                    ctx.waitm(s, "gu_ps%d_%d" % ((e - 1) // 2, (e - 1) % 2))
                for k, tg in enumerate((2 * half, 2 * half + 1)):
                    sl = t["gu_t2"][:, 512 * k:512 * k + 512]
                    ctx.inc(s.activation(out=sl,
                                         in_=ctx.ps["pG"][:, 512 * tg:512 * tg + 512],
                                         func=AF.Relu), "ch2")
                ctx.setmark("gu_rel%d_%d" % (ic, half), "ch2")

            @block.gpsimd
            def _(g, ic=ic, half=half):
                ctx.waitm(g, "gu_rel%d_%d" % (ic, half))
                for k in range(2):
                    sl = t["gu_t2"][:, 512 * k:512 * k + 512]
                    ctx.inc(g.tensor_tensor(out=sl, in0=sl, in1=sl, op=ALU.mult),
                            "gps")
                ctx.setmark("gu_sq%d_%d" % (ic, half), "gps")

            @block.vector
            def _(v, ic=ic, half=half):
                ctx.waitm(v, "gu_sq%d_%d" % (ic, half))
                e_ = 2 * ic + half
                if e_ >= 1:
                    ctx.waitm(v, "gu_rg%d_%d" % ((e_ - 1) // 2, (e_ - 1) % 2))
                for k, tg in enumerate((2 * half, 2 * half + 1)):
                    ctx.inc(v.tensor_tensor(out=t["gu_r"][:, 512 * k:512 * k + 512],
                                            in0=t["gu_t2"][:, 512 * k:512 * k + 512],
                                            in1=ctx.ps["pU"][:, 512 * tg:512 * tg + 512],
                                            op=ALU.mult), "ch2")
                ctx.setmark("gu_ps%d_%d" % (ic, half), "ch2")

            @block.gpsimd
            def _(g, ic=ic, half=half, e=e):
                ctx.waitm(g, "gu_ps%d_%d" % (ic, half))
                if e >= 1:
                    ctx.waitm(g, "gu_mxq%d_%d" % ((e - 1) // 2, (e - 1) % 2))
                ctx.inc(g.tensor_tensor(out=t["gu_t3"][:], in0=t["gu_r"][:],
                                        in1=t["gu_r"][:], op=ALU.mult), "ch5")
                ctx.setmark("gu_t3_%d_%d" % (ic, half), "ch5")

            @block.vector
            def _(v, ic=ic, half=half):
                ctx.waitm(v, "gu_ps%d_%d" % (ic, half))
                ctx.waitm(v, "gu_t3_%d_%d" % (ic, half))
                if ic >= 1 and half == 0:
                    ctx.waitm(v, "gu_rgd%d" % (ic - 1))
                hsl = slice(1024 * half, 1024 * half + 1024)
                ctx.inc(v.tensor_scalar(out=t["rgt"][:, hsl], in0=t["gu_r"][:],
                                        scalar1=t["gsub"][:, ic:ic + 1].opt(),
                                        scalar2=None, op0=ALU.mult), "ch3",
                        mark="gu_rg%d_%d" % (ic, half))
                ctx.inc(v.scalar_tensor_tensor(out=t["mxz"][:, hsl],
                                               in0=t["gu_t3"][:],
                                               scalar=t["gsq"][:, ic:ic + 1].opt(),
                                               in1=t["mxz"][:, hsl],
                                               op0=ALU.mult, op1=ALU.max), "vec",
                        mark="gu_mxq%d_%d" % (ic, half))

            @block.gpsimd
            def _(g, ic=ic, half=half):
                hsl = slice(1024 * half, 1024 * half + 1024)
                ctx.waitm(g, "gu_t3_%d_%d" % (ic, half))
                ctx.inc(g.tensor_tensor(out=t["sq"][:, hsl], in0=t["sq"][:, hsl],
                                        in1=t["gu_t3"][:], op=ALU.add), "gps",
                        mark="gu_sqa%d_%d" % (ic, half))

        @block.scalar
        def _(s, ic=ic):
            ctx.waitm(s, "gu_q%d" % min(ic + 1, NIC - 1))
            ctx.waitm(s, "gu_wdd%d" % ic)
            if ic >= 1:
                ctx.waitm(s, "gu_wd2_%d" % (ic - 1))
            ctx.inc(s.activation(out=t["ti32"][:, 0:2048], in_=t["wdstg"][:],
                                 func=AF.Copy, bias=0.0,
                                 scale=swb[:, 10:11].opt()), "ch4",
                    mark="gu_wd1_%d" % ic)

        @block.gpsimd
        def _(g, ic=ic):
            ctx.waitm(g, "gu_wd1_%d" % ic)
            if ic >= 1:
                ctx.waitm(g, "gu_wdq%d" % (ic - 1))
            ins = g.tensor_scalar(out=t["wdqb"][:], in0=t["ti32"][:, 0:2048],
                                  scalar1=1.0, scalar2=-1.0,
                                  op0=ALU.min, op1=ALU.max)
            ctx.inc(ins, "ch5", mark="gu_wd2_%d" % ic)
            ctx.waitm(g, "gu_wd2_%d" % ic)
            ctx.inc(g.dma_start(out=ctx.io["wdq"][128 * ic:128 * ic + 128, :],
                                in_=t["wdqb"][:]), "dgp", 16, mark="gu_wdq%d" % ic)

    @block.sync
    def _(sy):
        ctx.waitm(sy, "gu_rg%d_1" % (NIC - 1))
        ctx.inc(sy.dma_start(out=ctx.io["rgd"][128 * (NIC - 1):128 * NIC, :],
                             in_=t["rgt"][:]), "de", 16, mark="gu_rgd%d" % (NIC - 1))

    for src_, dst, op in [("mxz", "mxz_t", ALU.max), ("sq", "sq_t", ALU.add)]:
        for tp in range(NTC):
            pbuf = "pA" if tp % 2 == 0 else "pB"

            @block.tensor
            def _(te, src_=src_, tp=tp, pbuf=pbuf):
                if tp == 0 and src_ == "mxz":
                    ctx.waitm(te, "gu_mxq%d_1" % (NIC - 1))
                    ctx.wait(te, "vec")
                if tp == 0 and src_ == "sq":
                    ctx.waitm(te, "ep_mxz_%d" % (NTC - 1))
                    ctx.waitm(te, "gu_sqa%d_1" % (NIC - 1))
                if tp >= 2:
                    ctx.waitm(te, "ep_%s_%d" % (src_, tp - 2))
                ctx.inc(te.transpose(out=ctx.ps[pbuf][:, 0:128],
                                     in_=t[src_][:, 128 * tp:128 * tp + 128],
                                     identity=t["identf"][:]), "ten",
                        mark="ept_%s_%d" % (src_, tp))

            @block.vector
            def _(v, src_=src_, dst=dst, tp=tp, op=op, pbuf=pbuf):
                ctx.waitm(v, "ept_%s_%d" % (src_, tp))
                ctx.inc(v.tensor_reduce(out=t[dst][:, tp:tp + 1],
                                        in_=ctx.ps[pbuf][:, 0:128],
                                        axis=AX.X, op=op), "vec",
                        mark="ep_%s_%d" % (src_, tp))

    @block.scalar
    def _(s):
        ctx.waitm(s, "ep_mxz_%d" % (NTC - 1))
        ctx.wait(s, "vec")
        ctx.inc(s.activation(out=t["mxz_t"][:], in_=t["mxz_t"][:], func=AF.Sqrt),
                "scl", mark="gu_mxs")

    @block.vector
    def _(v):
        ctx.waitm(v, "ep_sq_%d" % (NTC - 1))
        ctx.waitm(v, "gu_mxs")
        k1, v1 = t["st_a"], t["st_b"]
        ins = v.tensor_tensor(out=k1[:], in0=t["invsx_t"][:], in1=t["invsx_t"][:],
                              op=ALU.mult)
        ctx.fence(v, ins, "fv")
        ins = v.tensor_tensor(out=k1[:], in0=k1[:], in1=t["invsx_t"][:], op=ALU.mult)
        ctx.fence(v, ins, "fv")
        ins = v.tensor_scalar(out=k1[:], in0=k1[:], scalar1=swb[:, 4:5].opt(),
                              scalar2=swb[:, 4:5].opt(), op0=ALU.mult, op1=ALU.mult)
        ctx.fence(v, ins, "fv")
        ins = v.tensor_scalar(out=k1[:], in0=k1[:], scalar1=swb[:, 5:6].opt(),
                              scalar2=None, op0=ALU.mult)
        ctx.fence(v, ins, "fv")
        ins = v.tensor_tensor(out=v1[:], in0=k1[:], in1=k1[:], op=ALU.mult)
        ctx.fence(v, ins, "fv")
        ins = v.tensor_tensor(out=v1[:], in0=v1[:], in1=t["sq_t"][:], op=ALU.mult)
        ctx.fence(v, ins, "fv")
        ctx.inc(v.tensor_scalar(out=v1[:], in0=v1[:], scalar1=1.0 / 8192.0,
                                scalar2=None, op0=ALU.mult), "vec", mark="gu_var")

    @block.scalar
    def _(s):
        ctx.waitm(s, "gu_var")
        ctx.inc(s.activation(out=t["st_c"][:], in_=t["st_b"][:], func=AF.Sqrt,
                             bias=SUB_EPS), "scl", mark="gu_sd")

    @block.vector
    def _(v):
        ctx.waitm(v, "gu_sd")
        nrecip(ctx, v, t["rstdz_t"][:], t["st_c"][:], t["st_d"][:])
        ins = v.tensor_tensor(out=t["st_b"][:], in0=t["rstdz_t"][:], in1=t["st_a"][:],
                              op=ALU.mult)
        ctx.fence(v, ins, "fv")
        ins = v.tensor_tensor(out=t["st_c"][:], in0=t["st_b"][:], in1=t["mxz_t"][:],
                              op=ALU.mult)
        ctx.fence(v, ins, "fv")
        ins = v.tensor_scalar(out=t["st_c"][:], in0=t["st_c"][:], scalar1=1e-5,
                              scalar2=None, op0=ALU.max)
        ctx.fence(v, ins, "fv")
        ins = v.reciprocal(out=t["st_d"][:], in_=t["st_c"][:])
        ctx.fence(v, ins, "fv")
        ins = v.tensor_scalar(out=t["sz_t"][:], in0=t["st_d"][:], scalar1=127.0,
                              scalar2=None, op0=ALU.mult)
        ctx.fence(v, ins, "fv")
        ins = v.tensor_tensor(out=t["qsc_t"][:], in0=t["st_b"][:], in1=t["sz_t"][:],
                              op=ALU.mult)
        ctx.fence(v, ins, "fv")
        ins = v.reciprocal(out=t["st_d"][:], in_=t["sz_t"][:])
        ctx.fence(v, ins, "fv")
        ctx.inc(v.tensor_scalar(out=t["invsd_t"][:], in0=t["st_d"][:],
                                scalar1=swb[:, 6:7].opt(), scalar2=None,
                                op0=ALU.mult), "vec", mark="gu_tok")

    row_from_cols(ctx, t["qsc_t"][:], 16, "c_qdrow", 0, "guq")
    bcast_row(ctx, "c_qdrow", 0, 2048, "qscb", wait_mark="rf_dguq")


def emit_down(ctx):
    block, t = ctx.block, ctx.t
    for q in range(4):
        qsl = slice(512 * q, 512 * q + 512)

        for i in range(NIC):
            buf = i % 2

            @block.sync
            def _(sy, q=q, i=i, buf=buf):
                if i >= 2:
                    ctx.waitm(sy, "dn_q%d_%d" % (q, i - 2))
                elif q >= 1:
                    ctx.waitm(sy, "dn_q%d_%d" % (q - 1, NIC - 2 + i))
                ctx.inc(sy.dma_start(out=t["rstg"][:, 512 * buf:512 * buf + 512],
                                     in_=ctx.io["rgd"][128 * i:128 * i + 128, qsl]),
                        "da" if buf == 0 else "db", 16, mark="dn_d%d_%d" % (q, i))

            @block.vector
            def _(v, q=q, i=i, buf=buf):
                ctx.waitm(v, "dn_d%d_%d" % (q, i))
                if i >= 2:
                    ctx.waitm(v, "dn_c%d_%d" % (q, i - 2))
                elif q >= 1:
                    ctx.waitm(v, "dn_c%d_%d" % (q - 1, NIC - 2 + i))
                ctx.inc(v.tensor_tensor(out=t["tiq"][:, 512 * buf:512 * buf + 512],
                                        in0=t["rstg"][:, 512 * buf:512 * buf + 512],
                                        in1=t["qscb"][:, qsl], op=ALU.mult),
                        "ch4", mark="dn_q%d_%d" % (q, i))

            @block.gpsimd
            def _(g, q=q, i=i, buf=buf):
                ctx.waitm(g, "dn_q%d_%d" % (q, i))
                if q >= 1 and i == 0:
                    ctx.waitm(g, "dn_mm%d_%d" % (q - 1, 2 * NIC - 1))
                ctx.inc(g.tensor_scalar(out=ctx.t["xbig"][:, 512 * i:512 * i + 512],
                                        in0=t["tiq"][:, 512 * buf:512 * buf + 512],
                                        scalar1=127.0, scalar2=-128.0,
                                        op0=ALU.min, op1=ALU.max), "ch0",
                        mark="dn_c%d_%d" % (q, i))

        for fgp in range(2):
            for i in range(NIC):
                wbuf = i % 2
                idx = fgp * NIC + i

                @block.scalar
                def _(s, q=q, fgp=fgp, i=i, wbuf=wbuf, idx=idx):
                    if i % 2 == 0:
                        if idx >= 4:
                            ctx.waitm(s, "dn_mm%d_%d" % (q, idx - 3))
                        elif q >= 1:
                            ctx.waitm(s, "dn_mm%d_%d" % (q - 1, 2 * NIC - 4 + idx + 1))
                        nb = min(2, NIC - i)
                        ctx.inc(s.dma_start(
                            out=t["wdt2"][:, 2048 * ((i // 2) % 2):
                                          2048 * ((i // 2) % 2) + 1024 * nb],
                            in_=ctx.io["wdq"]
                            .rearrange("(ib p) c -> p ib c", p=128)
                            [:, i:i + nb, 1024 * fgp:1024 * fgp + 1024]),
                            "dsc" if (i // 2) % 2 == 0 else "df", 16,
                            mark="dn_wt%d_%d" % (q, idx))

                @block.tensor
                def _(te, q=q, fgp=fgp, i=i, wbuf=wbuf, idx=idx):
                    ctx.waitm(te, "dn_wt%d_%d" % (q, idx - (i % 2)))
                    if fgp == 0:
                        ctx.waitm(te, "dn_c%d_%d" % (q, i))
                    if i == 0:
                        if fgp == 1:
                            ctx.waitm(te, "dn_ev%d_0" % q)
                        elif q >= 1:
                            ctx.waitm(te, "dn_ev%d_1" % (q - 1))
                    base = 2048 * ((i // 2) % 2) + 1024 * (i % 2)
                    last = None
                    for fg2 in range(2):
                        ps = ctx.ps["pG" if fg2 == 0 else "pU"]
                        for tc in range(4):
                            last = te.matmul(
                                ps[:, 512 * tc:512 * tc + 512],
                                ctx.t["xbig"][:, 512 * i + 128 * tc:
                                              512 * i + 128 * tc + 128],
                                t["wdt2"][:, base + 512 * fg2:
                                          base + 512 * fg2 + 512],
                                start=(i == 0), stop=(i == NIC - 1))
                    ctx.inc(last, "ten", mark="dn_mm%d_%d" % (q, idx))

            @block.vector
            def _(v, q=q, fgp=fgp):
                ctx.waitm(v, "dn_mm%d_%d" % (q, fgp * NIC + NIC - 1))
                if q >= 1 and fgp == 0:
                    ctx.waitm(v, "dn_od%d" % (q - 1))
                for fg2 in range(2):
                    ps = ctx.ps["pG" if fg2 == 0 else "pU"]
                    fg = 2 * fgp + fg2
                    for tc in range(4):
                        ctx.inc(v.tensor_scalar(
                            out=t["ot"][:, 2048 * tc + 512 * fg:
                                        2048 * tc + 512 * fg + 512],
                            in0=ps[:, 512 * tc:512 * tc + 512],
                            scalar1=t["invsd_t"][:, 4 * q + tc:4 * q + tc + 1].opt(),
                            scalar2=None, op0=ALU.mult), "ch1")
                ctx.setmark("dn_ev%d_%d" % (q, fgp), "ch1")

        @block.sync
        def _(sy, q=q):
            ctx.waitm(sy, "dn_ev%d_1" % q)
            for tc in range(4):
                ctx.inc(sy.dma_start(
                    out=ctx.io["out"][512 * q + 128 * tc:512 * q + 128 * tc + 128, :],
                    in_=t["ot"][:, 2048 * tc:2048 * tc + 2048]), "de", 16)
            ctx.setmark("dn_od%d" % q, "de")


def emit_xn(ctx):
    block, t = ctx.block, ctx.t

    row_from_cols(ctx, t["m_t"][:], 16, "c_qdrow2", 0, "x2m")
    row_from_cols(ctx, t["rstd_t"][:], 16, "rrow2", 0, "x2r")
    bcast_row(ctx, "c_qdrow2", 0, 2048, "mbf2", wait_mark="rf_dx2m")
    bcast_row(ctx, "rrow2", 0, 2048, "rbf2", wait_mark="rf_dx2r")

    for fc in range(NFC):
        buf = fc % 2

        @block.sync
        def _(sy, fc=fc, buf=buf):
            if fc >= 2:
                ctx.waitm(sy, "x2_%d" % (fc - 2))
            ctx.inc(sy.dma_start(out=t["xst2"][:, 2048 * buf:2048 * buf + 2048],
                                 in_=ctx.io["xT"][128 * fc:128 * fc + 128, :]),
                    "da" if buf == 0 else "db", 16, mark="x2d_%d" % fc)

        @block.vector
        def _(v, fc=fc, buf=buf):
            ctx.waitm(v, "x2d_%d" % fc)
            if fc == 0:
                ctx.waitm(v, "bc_rbf2")
            sl = t["xst2"][:, 2048 * buf:2048 * buf + 2048]
            ins = v.tensor_tensor(out=sl, in0=sl, in1=t["mbf2"][:], op=ALU.subtract)
            ctx.fence(v, ins, "fv")
            ctx.inc(v.tensor_tensor(out=ctx.t["xbig"][:, 2048 * fc:2048 * fc + 2048],
                                    in0=sl, in1=t["rbf2"][:], op=ALU.mult),
                    "ch0", mark="x2_%d" % fc)



def emit_decay(ctx):
    block, t = ctx.block, ctx.t

    @block.gpsimd
    def _(g):
        ctx.inc(g.iota(out=t["d_row"][:], pattern=[[1, 128]], base=0,
                       channel_multiplier=0, allow_small_or_imprecise_dtypes=True),
                "gps")
        ctx.inc(g.iota(out=t["d_col"][:], pattern=[[0, 128]], base=0,
                       channel_multiplier=1,
                       allow_small_or_imprecise_dtypes=True), "gps", mark="d_iota")

    @block.vector
    def _(v):
        ctx.waitm(v, "d_iota")
        ctx.inc(v.tensor_tensor(out=t["d_ige"][:], in0=t["d_row"][:],
                                in1=t["d_col"][:], op=ALU.is_ge), "vec")
        ins = v.tensor_tensor(out=t["d_diff"][:], in0=t["d_row"][:],
                              in1=t["d_col"][:], op=ALU.subtract)
        ctx.fence(v, ins, "fv")
        ctx.inc(v.tensor_scalar(out=t["d_diff"][:], in0=t["d_diff"][:],
                                scalar1=0.0, scalar2=None, op0=ALU.max),
                "vec", mark="d_base")

    for h in range(HEADS):
        @block.scalar
        def _(s, h=h):
            if h == 0:
                ctx.waitm(s, "d_base")
            if h >= 2:
                ctx.waitm(s, "d_dtv%d" % (h - 2))
            ctx.inc(s.activation(out=t["d_dte"][:, 128 * (h % 2):128 * (h % 2) + 128],
                                 in_=t["d_diff"][:], func=AF.Exp,
                                 scale=t["nslb"][:, h:h + 1].opt()),
                    "scl", mark="d_dts%d" % h)

        @block.vector
        def _(v, h=h):
            ctx.waitm(v, "d_dts%d" % h)
            ctx.inc(v.tensor_tensor(out=t["dt"][:, 128 * h:128 * h + 128],
                                    in0=t["d_dte"][:, 128 * (h % 2):128 * (h % 2) + 128],
                                    in1=t["d_ige"][:], op=ALU.mult),
                    "vec", mark="d_dtv%d" % h)

    for h in range(HEADS):
        @block.scalar
        def _(s, h=h):
            if h == 0:
                ctx.waitm(s, "d_base")
            ctx.inc(s.activation(out=t["qd"][:, 128 * h:128 * h + 128],
                                 in_=t["d_row"][:], func=AF.Exp,
                                 scale=t["nslb"][:, h:h + 1].opt()),
                    "scl", mark="d_qb%d" % h)


def emit_attn(ctx):
    block, t = ctx.block, ctx.t
    pG, pU = ctx.ps["pG"], ctx.ps["pU"]
    pufree = [None, None]

    @block.tensor
    def _(te):
        ctx.wait(te, "vec")
        ctx.wait(te, "scl")
        te.matmul(pG[:, 0:1], t["ones1f"][:], t["msends"][0:1, 0:1],
                  start=True, stop=True)
        ctx.inc(te.matmul(pG[:, 512:513], t["ones1f"][:], t["mrecvs"][0:1, 0:1],
                          start=True, stop=True), "ten", mark="at_msk")

    @block.scalar
    def _(s):
        ctx.waitm(s, "at_msk")
        s.copy(out=t["msb"][:, 0:1], in_=pG[:, 0:1])
        ctx.inc(s.copy(out=t["mrb"][:, 0:1], in_=pG[:, 512:513]), "scl",
                mark="at_msk2")

    bcast_dram_row(ctx, ctx.io["bqv"], "bvb", "atb")

    for hg in range(NHG):
        h0 = 2 * hg
        def emit_proj(proj, colbase, hg=hg, h0=h0):
            @block.gpsimd
            def _(g, hg=hg, proj=proj, colbase=colbase):
                if not (hg == 0 and proj == "k"):
                    ctx.waitm(g, "at_wcons")
                ctx.inc(g.dma_start(
                    out=t["wstg2"][:].rearrange("p (fc o) -> p fc o", fc=NFC),
                    in_=ctx.io["wqkv"].rearrange("(fc p) o -> p fc o", p=128)
                    [:, :, colbase:colbase + 256]), "dh", 16,
                    mark="at_w%s%d" % (proj, hg))

            if proj in ("q", "k"):
                dst = t["qt"] if proj == "q" else t["kt"]
                bcol = (0 if proj == "q" else 16) + h0
                for qh in range(2):
                    for tg in range(4):
                        gi = qh * 4 + tg
                        par = gi % 2
                        pb = pU[:, 1024 + 512 * par:1536 + 512 * par]

                        @block.tensor
                        def _(te, hg=hg, proj=proj, qh=qh, tg=tg, gi=gi, pb=pb,
                              par=par):
                            ctx.waitm(te, "at_w%s%d" % (proj, hg))
                            if pufree[par] is not None:
                                ctx.waitm(te, pufree[par])
                            last = None
                            for fc in range(NFC):
                                last = te.matmul(
                                    pb, t["wstg2"][:, 256 * fc + 128 * qh:
                                                   256 * fc + 128 * qh + 128],
                                    ctx.t["xbig"][:, 2048 * fc + 512 * tg:
                                                  2048 * fc + 512 * tg + 512],
                                    start=(fc == 0), stop=(fc == NFC - 1))
                            ctx.inc(last, "ten", mark="at_pm%s%d_%d" % (proj, hg, gi))

                        @block.vector
                        def _(v, hg=hg, proj=proj, qh=qh, tg=tg, gi=gi, pb=pb,
                              dst=dst, bcol=bcol):
                            ctx.waitm(v, "at_pm%s%d_%d" % (proj, hg, gi))
                            if hg >= 1 and gi == 0 and proj == "q":
                                ctx.waitm(v, "at_scandone%d" % (hg - 1))
                                ctx.waitm(v, "at_scanmm%d" % (hg - 1))
                            ctx.inc(v.tensor_scalar(
                                out=dst[:, 2048 * qh + 512 * tg:
                                        2048 * qh + 512 * tg + 512],
                                in0=pb,
                                scalar1=t["bqkvs"][:, bcol + qh:bcol + qh + 1].opt(),
                                scalar2=None, op0=ALU.add), "ch1",
                                mark="at_pe%s%d_%d" % (proj, hg, gi))
                        pufree[par] = "at_pe%s%d_%d" % (proj, hg, gi)
                ctx.setmark("at_wcons", "ten")
            else:
                for tc in range(NTC):
                    par = tc % 2
                    pb = pU[:, 1024 + 512 * par:1280 + 512 * par]

                    @block.tensor
                    def _(te, hg=hg, tc=tc, pb=pb, par=par):
                        ctx.waitm(te, "at_wv%d" % hg)
                        if pufree[par] is not None:
                            ctx.waitm(te, pufree[par])
                        last = None
                        for fc in range(NFC):
                            last = te.matmul(
                                pb, ctx.t["xbig"][:, 2048 * fc + 128 * tc:
                                                  2048 * fc + 128 * tc + 128],
                                t["wstg2"][:, 256 * fc:256 * fc + 256],
                                start=(fc == 0), stop=(fc == NFC - 1))
                        ctx.inc(last, "ten", mark="at_vm%d_%d" % (hg, tc))

                    @block.vector
                    def _(v, hg=hg, tc=tc, pb=pb, h0=h0):
                        ctx.waitm(v, "at_vm%d_%d" % (hg, tc))
                        if hg >= 1 and tc == 0:
                            ctx.waitm(v, "at_scandone%d" % (hg - 1))
                            ctx.waitm(v, "at_scanmm%d" % (hg - 1))
                        ctx.inc(v.tensor_tensor(
                            out=t["vtok"][:, 256 * tc:256 * tc + 256], in0=pb,
                            in1=t["bvb"][:, h0 * 128:h0 * 128 + 256],
                            op=ALU.add), "ch1", mark="at_ve%d_%d" % (hg, tc))
                    pufree[tc % 2] = "at_ve%d_%d" % (hg, tc)
                ctx.setmark("at_wcons", "ten")


        emit_proj("k", D + h0 * 128)
        emit_proj("v", 2 * D + h0 * 128)
        for h2 in range(2):
            for tc in range(NTC):
                i2 = h2 * NTC + tc
                pb = pG[:, 1024 + 512 * (i2 % 2):1152 + 512 * (i2 % 2)]

                @block.tensor
                def _(te, hg=hg, h2=h2, tc=tc, i2=i2, pb=pb):
                    ctx.waitm(te, "at_pek%d_%d" % (hg, h2 * 4 + tc // 4))
                    if i2 < 2 and hg >= 1:
                        ctx.waitm(te, "at_oe%d_%d_%d" % (hg - 1, NTC - 1, i2))
                    if i2 >= 2:
                        ctx.waitm(te, "at_kte%d_%d" % (hg, i2 - 2))
                    ctx.inc(te.matmul(
                        pb, t["kt"][:, 2048 * h2 + 128 * tc:
                                    2048 * h2 + 128 * tc + 128],
                        t["ident"][:], start=True, stop=True), "ten",
                        mark="at_ktt%d_%d" % (hg, i2))

                @block.vector
                def _(v, hg=hg, h2=h2, tc=tc, i2=i2, pb=pb):
                    ctx.waitm(v, "at_ktt%d_%d" % (hg, i2))
                    if hg >= 1 and i2 == 0:
                        ctx.waitm(v, "at_scandone%d" % (hg - 1))
                        ctx.waitm(v, "at_scanmm%d" % (hg - 1))
                    ctx.inc(v.tensor_copy(
                        out=t["ktok"][:, 256 * tc + 128 * h2:
                                      256 * tc + 128 * h2 + 128],
                        in_=pb), "ch2", mark="at_kte%d_%d" % (hg, i2))

        for h2 in range(2):
            h = h0 + h2
            psl = pG[:, 512 * h2:512 * h2 + 128]
            for tc in range(NTC):
                buf = tc % 2

                @block.vector
                def _(v, hg=hg, h2=h2, h=h, tc=tc, buf=buf):
                    ctx.waitm(v, "at_kte%d_%d" % (hg, h2 * NTC + tc))
                    if tc >= 2:
                        ctx.waitm(v, "at_slm%d_%d_%d" % (hg, h2, tc - 2))
                    elif h2 == 1:
                        ctx.waitm(v, "at_slm%d_0_%d" % (hg, NTC - 2 + tc))
                    elif hg >= 1:
                        ctx.waitm(v, "at_slm%d_1_%d" % (hg - 1, NTC - 2 + tc))
                    ctx.inc(v.tensor_scalar(
                        out=t["ktmp"][:, 128 * buf:128 * buf + 128],
                        in0=t["ktok"][:, 256 * tc + 128 * h2:
                                      256 * tc + 128 * h2 + 128],
                        scalar1=t["khw"][:, 16 * tc + h:16 * tc + h + 1].opt(),
                        scalar2=None, op0=ALU.mult), "ch3",
                        mark="at_kh%d_%d_%d" % (hg, h2, tc))

                @block.tensor
                def _(te, hg=hg, h2=h2, tc=tc, buf=buf, psl=psl):
                    ctx.waitm(te, "at_kh%d_%d_%d" % (hg, h2, tc))
                    ctx.waitm(te, "at_ve%d_%d" % (hg, tc))
                    if tc == 0:
                        if hg == 0 and h2 == 0:
                            ctx.waitm(te, "at_msk2")
                        if hg >= 1:
                            ctx.waitm(te, "at_st%d_%d_%d" % (hg - 1, NTC - 1, h2))
                    ctx.inc(te.matmul(psl,
                                      t["ktmp"][:, 128 * buf:128 * buf + 128],
                                      t["vtok"][:, 256 * tc + 128 * h2:
                                                256 * tc + 128 * h2 + 128],
                                      start=(tc == 0), stop=(tc == NTC - 1)),
                            "ten", mark="at_slm%d_%d_%d" % (hg, h2, tc))

            @block.vector
            def _(v, hg=hg, h2=h2, psl=psl):
                ctx.waitm(v, "at_slm%d_%d_%d" % (hg, h2, NTC - 1))
                ctx.waitm(v, "at_msk2")
                if hg >= 1 and h2 == 0:
                    ctx.waitm(v, "at_snd%d" % (hg - 1))
                    ctx.waitm(v, "at_sbf%d_%d_1" % (hg - 1, NTC - 1))
                ctx.inc(v.tensor_scalar(out=t["ssend"][:, 128 * h2:128 * h2 + 128],
                                        in0=psl,
                                        scalar1=t["msb"][:, 0:1].opt(),
                                        scalar2=None, op0=ALU.mult), "ch4",
                        mark="at_ss%d_%d" % (hg, h2))

        @block.gpsimd
        def _(g, hg=hg):
            ctx.waitm(g, "at_ss%d_1" % hg)
            for h2 in range(2):
                ctx.inc(g.dma_start(
                    out=ctx.io["ccs_in"][256 * hg + 128 * h2:256 * hg + 128 * h2 + 128, :],
                    in_=t["ssend"][:, 128 * h2:128 * h2 + 128]), "dgp", 16)
            ctx.setmark("at_snd%d" % hg, "dgp")
            ctx.waitm(g, "at_snd%d" % hg)
            ctx.inc(g.collective_compute(
                "AllReduce", ALU.add,
                replica_groups=[[0, 1], [2, 3], [4, 5], [6, 7]],
                ins=[ctx.io["ccs_in"][256 * hg:256 * hg + 256, :]],
                outs=[ctx.io["ccs_out"][256 * hg:256 * hg + 256, :]]), "cc")
            ctx.wait(g, "cc")
            if hg >= 1:
                ctx.waitm(g, "at_si%d" % (hg - 1))
            for h2 in range(2):
                ctx.inc(g.dma_start(
                    out=t["srecv"][:, 128 * h2:128 * h2 + 128],
                    in_=ctx.io["ccs_out"][256 * hg + 128 * h2:256 * hg + 128 * h2 + 128, :]),
                    "dg", 16)
            ctx.setmark("at_sr%d" % hg, "dg")

        emit_proj("q", h0 * 128)

        @block.vector
        def _(v, hg=hg):
            ctx.waitm(v, "at_sr%d" % hg)
            if hg >= 1:
                ctx.waitm(v, "at_sbf%d_%d_1" % (hg - 1, NTC - 1))
            ctx.inc(v.tensor_scalar(out=t["sfull"][:], in0=t["srecv"][:],
                                    scalar1=t["mrb"][:, 0:1].opt(),
                                    scalar2=None, op0=ALU.mult), "ch4",
                    mark="at_si%d" % hg)

        @block.scalar
        def _(s, hg=hg):
            ctx.waitm(s, "at_si%d" % hg)
            if hg >= 1:
                ctx.waitm(s, "at_mm23_%d_%d_1" % (hg - 1, NTC - 1))
            ctx.inc(s.copy(out=t["sbf"][:], in_=t["sfull"][:]), "scl",
                    mark="at_sbf%d_init" % hg)

        for tc in range(NTC):
            for h2 in range(2):
                h = h0 + h2
                buf = tc % 2
                pT = pG[:, 512 * h2:512 * h2 + 128]
                pO = pG[:, 1024 + 512 * h2:1152 + 512 * h2]
                pV = pU[:, 512 * h2:512 * h2 + 128]
                vtsl = t["vtok"][:, 256 * tc + 128 * h2:256 * tc + 128 * h2 + 128]
                qtsl = t["qt"][:, 2048 * h2 + 128 * tc:2048 * h2 + 128 * tc + 128]
                ktfsl = t["kt"][:, 2048 * h2 + 128 * tc:2048 * h2 + 128 * tc + 128]
                ktsl = t["ktok"][:, 256 * tc + 128 * h2:256 * tc + 128 * h2 + 128]
                ssl = slice(256 * h2 + 128 * buf, 256 * h2 + 128 * buf + 128)

                @block.tensor
                def _(te, hg=hg, tc=tc, h2=h2, pT=pT, ktfsl=ktfsl, qtsl=qtsl):
                    if tc == 0:
                        ctx.waitm(te, "at_peq%d_7" % hg)
                        ctx.waitm(te, "at_ss%d_%d" % (hg, h2))
                    else:
                        ctx.waitm(te, "at_st%d_%d_%d" % (hg, tc - 1, h2))
                    ctx.inc(te.matmul(pT, ktfsl, qtsl, start=True, stop=True),
                            "ten", mark="at_mm1_%d_%d_%d" % (hg, tc, h2))

                @block.vector
                def _(v, hg=hg, tc=tc, h2=h2, h=h, buf=buf, pT=pT, qtsl=qtsl,
                      ktsl=ktsl, ssl=ssl):
                    ctx.waitm(v, "at_mm1_%d_%d_%d" % (hg, tc, h2))
                    if tc >= 2:
                        ctx.waitm(v, "at_mm23_%d_%d_%d" % (hg, tc - 2, h2))
                        ctx.waitm(v, "at_mm4_%d_%d_%d" % (hg, tc - 2, h2))
                    elif hg >= 1:
                        ctx.waitm(v, "at_mm23_%d_%d_%d" % (hg - 1, NTC - 2 + tc, h2))
                        ctx.waitm(v, "at_mm4_%d_%d_%d" % (hg - 1, NTC - 2 + tc, h2))
                    ins = v.tensor_tensor(out=t["stsb"][:, ssl], in0=pT,
                                          in1=t["dt"][:, 128 * h:128 * h + 128],
                                          op=ALU.mult)
                    ctx.inc(ins, "ch5", mark="at_st%d_%d_%d" % (hg, tc, h2))
                    ctx.inc(v.tensor_tensor(out=t["qtb"][:, ssl], in0=qtsl,
                                            in1=t["qd"][:, 128 * h:128 * h + 128],
                                            op=ALU.mult),
                            "ch5", mark="at_qt%d_%d_%d" % (hg, tc, h2))
                    ctx.inc(v.tensor_scalar(out=t["ktmp2"][:, ssl], in0=ktsl,
                                            scalar1=t["kdv"][:, h:h + 1].opt(),
                                            scalar2=None, op0=ALU.mult),
                            "ch5", mark="at_km%d_%d_%d" % (hg, tc, h2))

                @block.tensor
                def _(te, hg=hg, tc=tc, h2=h2, buf=buf, pO=pO, vtsl=vtsl, ssl=ssl):
                    ctx.waitm(te, "at_qt%d_%d_%d" % (hg, tc, h2))
                    if tc >= 1:
                        ctx.waitm(te, "at_oe%d_%d_%d" % (hg, tc - 1, h2))
                        ctx.waitm(te, "at_sbf%d_%d_%d" % (hg, tc - 1, h2))
                    else:
                        ctx.waitm(te, "at_sbf%d_init" % hg)
                        ctx.waitm(te, "at_kte%d_%d" % (hg, 2 * NTC - 1))
                        if hg >= 1:
                            ctx.waitm(te, "at_oe%d_%d_%d" % (hg - 1, NTC - 1, h2))
                    te.matmul(pO, vtsl, t["stsb"][:, ssl], start=True, stop=False)
                    ctx.inc(te.matmul(pO, t["sbf"][:, 128 * h2:128 * h2 + 128],
                                      t["qtb"][:, ssl], start=False, stop=True),
                            "ten", mark="at_mm23_%d_%d_%d" % (hg, tc, h2))

                @block.tensor
                def _(te, hg=hg, tc=tc, h2=h2, buf=buf, pV=pV, vtsl=vtsl, ssl=ssl):
                    ctx.waitm(te, "at_km%d_%d_%d" % (hg, tc, h2))
                    if tc >= 1:
                        ctx.waitm(te, "at_sup%d_%d_%d" % (hg, tc - 1, h2))
                    elif hg >= 1:
                        ctx.waitm(te, "at_sup%d_%d_%d" % (hg - 1, NTC - 1, h2))
                    ctx.inc(te.matmul(pV, t["ktmp2"][:, ssl], vtsl,
                                      start=True, stop=True), "ten",
                            mark="at_mm4_%d_%d_%d" % (hg, tc, h2))

                @block.scalar
                def _(s, hg=hg, tc=tc, h2=h2, pO=pO, h=h):
                    ctx.waitm(s, "at_mm23_%d_%d_%d" % (hg, tc, h2))
                    ctx.inc(s.copy(out=ctx.t["att"][:, 2048 * h + 128 * tc:
                                                    2048 * h + 128 * tc + 128],
                                   in_=pO), "scl",
                            mark="at_oe%d_%d_%d" % (hg, tc, h2))

                @block.vector
                def _(v, hg=hg, tc=tc, h2=h2, h=h, pV=pV):
                    ctx.waitm(v, "at_mm4_%d_%d_%d" % (hg, tc, h2))
                    ctx.inc(v.scalar_tensor_tensor(
                        out=t["sfull"][:, 128 * h2:128 * h2 + 128],
                        in0=t["sfull"][:, 128 * h2:128 * h2 + 128],
                        scalar=t["dhv"][:, h:h + 1].opt(), in1=pV,
                        op0=ALU.mult, op1=ALU.add), "ch4",
                        mark="at_sup%d_%d_%d" % (hg, tc, h2))

                @block.scalar
                def _(s, hg=hg, tc=tc, h2=h2):
                    ctx.waitm(s, "at_sup%d_%d_%d" % (hg, tc, h2))
                    ctx.waitm(s, "at_mm23_%d_%d_%d" % (hg, tc, h2))
                    ctx.inc(s.copy(out=t["sbf"][:, 128 * h2:128 * h2 + 128],
                                   in_=t["sfull"][:, 128 * h2:128 * h2 + 128]),
                            "scl", mark="at_sbf%d_%d_%d" % (hg, tc, h2))

        ctx.setmark("at_scandone%d" % hg, "ch5")
        ctx.setmark("at_scanmm%d" % hg, "ten")


def emit_wout(ctx):
    block, t = ctx.block, ctx.t
    pG = ctx.ps["pG"]

    @block.gpsimd
    def _(g):
        ctx.inc(g.dma_start(
            out=ctx.t["xbig"][:].rearrange("p (fc o) -> p fc o", fc=NFC),
            in_=ctx.io["wout"].rearrange("(fc p) o -> p fc o", p=128),
            max_dma_last_dim=8192), "dgp", 16, mark="w3_w")

    bcast_dram_row(ctx, ctx.io["bout_r"], "bvb", "w3b")

    for tc in range(NTC):
        buf = tc % 2

        @block.sync
        def _(sy, tc=tc, buf=buf):
            if tc >= 2:
                ctx.waitm(sy, "w3_e%d" % ((tc - 2) * 4 + 3))
            ctx.inc(sy.dma_start(out=t["xst3"][:, 2048 * buf:2048 * buf + 2048],
                                 in_=ctx.io["xtok"][128 * tc:128 * tc + 128, :]),
                    "da" if buf == 0 else "db", 16, mark="w3_xd%d" % tc)

        for fg in range(4):
            gi = tc * 4 + fg
            pb = pG[:, 512 * (gi % 2):512 * (gi % 2) + 512]

            @block.tensor
            def _(te, tc=tc, fg=fg, gi=gi, pb=pb):
                if gi == 0:
                    ctx.waitm(te, "w3_w")
                if gi >= 2:
                    ctx.waitm(te, "w3_e%d" % (gi - 2))
                last = None
                for fc in range(NFC):
                    last = te.matmul(
                        pb, ctx.t["att"][:, 2048 * fc + 128 * tc:
                                         2048 * fc + 128 * tc + 128],
                        ctx.t["xbig"][:, 2048 * fc + 512 * fg:
                                      2048 * fc + 512 * fg + 512],
                        start=(fc == 0), stop=(fc == NFC - 1))
                ctx.inc(last, "ten", mark="w3_m%d" % gi)

            @block.vector
            def _(v, tc=tc, fg=fg, gi=gi, pb=pb, buf=buf):
                ctx.waitm(v, "w3_m%d" % gi)
                if fg == 0:
                    ctx.waitm(v, "w3_xd%d" % tc)
                    if tc == 0:
                        ctx.waitm(v, "bc_bvb")
                    if tc >= 2:
                        ctx.waitm(v, "w3_od%d" % (tc - 2))
                ins = v.tensor_tensor(out=t["ot3"][:, 2048 * buf + 512 * fg:
                                                   2048 * buf + 512 * fg + 512],
                                      in0=pb, in1=t["bvb"][:, 512 * fg:512 * fg + 512],
                                      op=ALU.add)
                ctx.fence(v, ins, "fv")
                ctx.inc(v.tensor_tensor(
                    out=t["ot3"][:, 2048 * buf + 512 * fg:2048 * buf + 512 * fg + 512],
                    in0=t["ot3"][:, 2048 * buf + 512 * fg:2048 * buf + 512 * fg + 512],
                    in1=t["xst3"][:, 2048 * buf + 512 * fg:2048 * buf + 512 * fg + 512],
                    op=ALU.add), "ch1", mark="w3_e%d" % gi)

        @block.gpsimd
        def _(g, tc=tc, buf=buf):
            ctx.waitm(g, "w3_e%d" % (tc * 4 + 3))
            ctx.inc(g.dma_start(out=ctx.io["out"][128 * tc:128 * tc + 128, :],
                                in_=t["ot3"][:, 2048 * buf:2048 * buf + 2048],
                                accum_op=ALU.add), "di" if buf == 0 else "dj",
                    16, mark="w3_od%d" % tc)


def declare_io(nc, io, debug=False):
    io["xtok"] = nc.dram_tensor("xtok", [T, D], F32, kind="ExternalInput").ap()
    io["xT"] = nc.dram_tensor("xT", [D, T], F32, kind="ExternalInput").ap()
    io["wqkv"] = nc.dram_tensor("wqkv", [D, 3 * D], F32, kind="ExternalInput").ap()
    io["wout"] = nc.dram_tensor("wout", [D, D], F32, kind="ExternalInput").ap()
    io["wgS"] = nc.dram_tensor("wgS", [NIC, 128, D], F32, kind="ExternalInput").ap()
    io["wuS"] = nc.dram_tensor("wuS", [NIC, 128, D], F32, kind="ExternalInput").ap()
    io["wdT"] = nc.dram_tensor("wdT", [FI, D], F32, kind="ExternalInput").ap()
    io["gsub_t"] = nc.dram_tensor("gsub_t", [128, 64], F32, kind="ExternalInput").ap()
    io["wmsl"] = nc.dram_tensor("wmsl", [3, 128, 16384], F32, kind="ExternalInput").ap()
    io["bqkv_t"] = nc.dram_tensor("bqkv_t", [128, 48], F32, kind="ExternalInput").ap()
    io["slopes_r"] = nc.dram_tensor("slopes_r", [1, 16], F32, kind="ExternalInput").ap()
    io["bqv"] = nc.dram_tensor("bqv", [1, D], F32, kind="ExternalInput").ap()
    io["bout_r"] = nc.dram_tensor("bout_r", [1, D], F32, kind="ExternalInput").ap()
    io["mrecv"] = nc.dram_tensor("mrecv", [1, 1], F32, kind="ExternalInput").ap()
    io["msend"] = nc.dram_tensor("msend", [1, 1], F32, kind="ExternalInput").ap()
    io["out"] = nc.dram_tensor("out", [T, D], F32, kind="ExternalOutput").ap()
    io["rgd"] = nc.dram_tensor("rgd", [FI, T], F32).ap()
    io["wdq"] = nc.dram_tensor("wdq", [FI, D], BF16).ap()
    io["ccw_in"] = nc.dram_tensor("ccw_in", [1, 8], F32).ap()
    io["ccs_in"] = nc.dram_tensor("ccs_in", [2048, 128], F32).ap()
    io["ccs_out"] = nc.dram_tensor("ccs_out", [2048, 128], F32).ap()
    io["ccw_out"] = nc.dram_tensor("ccw_out", [1, 8], F32).ap()
    if debug:
        io["dbg_xq"] = nc.dram_tensor("dbg_xq", [128, 16 * T], BF16,
                                      kind="ExternalOutput").ap()
        io["dbg_st"] = nc.dram_tensor("dbg_st", [128, 16 * 16], F32,
                                      kind="ExternalOutput").ap()
        io["dbg_rgd"] = nc.dram_tensor("dbg_rgd", [FI, T], F32,
                                       kind="ExternalOutput").ap()
        io["dbg_qscb"] = nc.dram_tensor("dbg_qscb", [128, T], F32,
                                        kind="ExternalOutput").ap()


def alloc_common(ctx, es):
    a = ctx.alloc
    a(es, "xbig", [128, 16 * T], BF16)
    a(es, "ones1f", [1, 128], F32); a(es, "onescf", [128, 1], F32)
    a(es, "identf", [128, 128], F32)
    a(es, "iota_p", [128, 1], F32)
    a(es, "c_v1", [128, 4], F32)
    a(es, "c_t16a", [128, 16], F32); a(es, "c_t16b", [128, 16], F32)
    a(es, "c_t16x", [16, 128], F32)
    a(es, "ident", [128, 128], BF16)
    a(es, "slr", [1, 16], F32); a(es, "slb", [128, 16], F32)
    a(es, "nslb", [128, 16], F32)
    a(es, "dhv", [128, 16], F32); a(es, "kdv", [128, 16], F32)
    a(es, "qdt", [128, 16], F32)
    a(es, "khw", [128, 256], F32)
    a(es, "bqkvs", [128, 48], F32); a(es, "gsub", [128, 64], F32)
    a(es, "bqvr", [1, D], F32); a(es, "boutr", [1, D], F32)
    a(es, "mrecvs", [1, 1], F32); a(es, "msends", [1, 1], F32)
    a(es, "swb", [128, 16], F32)
    a(es, "msum", [128, 16], F32); a(es, "sqsum", [128, 16], F32)
    a(es, "m_t", [128, 16], F32); a(es, "rstd_t", [128, 16], F32)
    a(es, "mx_t", [128, 16], F32); a(es, "sx_t", [128, 16], F32)
    a(es, "invsx_t", [128, 16], F32)
    a(es, "st_a", [128, 16], F32); a(es, "st_b", [128, 16], F32)
    a(es, "st_c", [128, 16], F32); a(es, "st_d", [128, 16], F32)
    a(es, "mxz_t", [128, 16], F32); a(es, "sq_t", [128, 16], F32)
    a(es, "rstdz_t", [128, 16], F32); a(es, "sz_t", [128, 16], F32)
    a(es, "qsc_t", [128, 16], F32); a(es, "invsd_t", [128, 16], F32)
    a(es, "cpool", [128, 2], F32)
    ctx.nc.const_aps.register(F32, LN_EPS, ctx.t["cpool"][:, 0:1])
    ctx.nc.const_aps.register(F32, SUB_EPS, ctx.t["cpool"][:, 1:2])


def build_full(debug=False, upto=99):
    nc = bass.Bass("TRN2", target_bir_lowering=False)
    es = ExitStack()
    io = {}
    declare_io(nc, io, debug)
    block = es.enter_context(nc.Block())
    ctx = Ctx(nc, block, es)
    ctx.io = io
    for s in ["dsy", "dgp", "dsc", "ten", "vec", "scl", "gps", "cc",
              "ch0", "ch1", "ch2", "ch3", "ch4", "ch5", "fv", "fs", "fg",
              "da", "db", "dc", "dd", "de", "df", "dg", "dh", "di", "dj"]:
        ctx.add_sem(s)
    for e in ENGS:
        ctx.add_sem("b_" + e)

    ctx.ps["pG"] = es.enter_context(nc.psum_tensor("pG", [128, 2048], F32))
    ctx.ps["pU"] = es.enter_context(nc.psum_tensor("pU", [128, 2048], F32))
    ctx.ps["pD"] = ctx.ps["pG"]
    ctx.ps["pA"] = ctx.ps["pG"]
    ctx.ps["pB"] = ctx.ps["pU"]

    alloc_common(ctx, es)
    a = ctx.alloc

    es_early = ExitStack()
    a(es_early, "c_qdrow", [1, 2048], F32)
    a(es_early, "ti32", [128, 2048], I32)

    with ExitStack() as es_c:
        a(es_c, "c_row", [128, 128], F32); a(es_c, "c_col", [128, 128], F32)
        a(es_c, "c_ige", [128, 128], F32); a(es_c, "c_diff", [128, 128], F32)
        a(es_c, "c_dte", [128, 256], F32)
        if upto >= 0:
            emit_const(ctx)
        ctx.barrier()

    with ExitStack() as es_w:
        with ExitStack() as es_wm:
            a(es_wm, "wm_stage", [128, 8192], F32)
            a(es_wm, "wm_acc", [128, 16], F32)
            a(es_wm, "wm_p", [128, 4], F32); a(es_wm, "wm_r", [1, 16], F32)
            if upto >= 1:
                emit_wmean(ctx)
            ctx.barrier()

        with ExitStack() as es_p0:
            a(es_p0, "rrow", [1, 2048], F32)
            a(es_p0, "xh", [128, 16 * 1024], F32)
            a(es_p0, "mxa", [128, 1024], F32)
            ctx.t["sbq"] = ctx.t["mxa"]
            a(es_p0, "mbf", [128, 2048], F32)
            a(es_p0, "rbf", [128, 2048], F32)
            if upto >= 2:
                emit_p0(ctx)
            ctx.barrier()
            if debug:
                @block.sync
                def _(sy):
                    ctx.inc(sy.dma_start(out=io["dbg_xq"], in_=ctx.t["xbig"][:]),
                            "dsy", 16)
                ctx.barrier()

    with ExitStack() as es_q:
        a(es_q, "qscb", [128, 2048], F32)
        with ExitStack() as es_gu:
            a(es_gu, "gsq", [128, 64], F32)
            a(es_gu, "wstg_g", [128, 4096], F32); a(es_gu, "wstg_u", [128, 4096], F32)
            a(es_gu, "wqg", [128, 4096], BF16); a(es_gu, "wqu", [128, 4096], BF16)
            a(es_gu, "mxz", [128, 2048], F32); a(es_gu, "sq", [128, 2048], F32)
            a(es_gu, "gu_t2", [128, 1024], F32)
            a(es_gu, "gu_r", [128, 1024], F32); a(es_gu, "gu_t3", [128, 1024], F32)
            a(es_gu, "rgt", [128, 2048], F32)
            a(es_gu, "wdstg", [128, 2048], F32)
            a(es_gu, "wdqb", [128, 2048], BF16)
            if upto >= 3:
                emit_gateup(ctx)
            ctx.barrier()

        with ExitStack() as es_dn:
            a(es_dn, "rstg", [128, 1024], F32)
            a(es_dn, "tiq", [128, 1024], I32)
            a(es_dn, "wdt2", [128, 4096], BF16)
            a(es_dn, "ot", [128, 8192], F32)
            if upto >= 4:
                emit_down(ctx)
            ctx.barrier()
        if debug:
            @block.sync
            def _(sy):
                ctx.inc(sy.dma_start(out=io["dbg_qscb"], in_=ctx.t["qscb"][:]),
                        "dsy", 16)
            ctx.barrier()
    es_early.close()

    with ExitStack() as es_xn:
        a(es_xn, "xst2", [128, 4096], F32)
        a(es_xn, "mbf2", [128, 2048], F32)
        a(es_xn, "rbf2", [128, 2048], F32)
        a(es_xn, "c_qdrow2", [1, 2048], F32)
        a(es_xn, "rrow2", [1, 2048], F32)
        if upto >= 5:
            emit_xn(ctx)
        ctx.barrier()

    with ExitStack() as es_late:
        a(es_late, "att", [128, 16 * T], BF16)
        a(es_late, "bqvr2", [1, 512], F32)
        a(es_late, "bvb", [128, 2048], BF16)
        a(es_late, "qd", [128, 2048], BF16)
        a(es_late, "dt", [128, 2048], BF16)
        with ExitStack() as es_d:
            a(es_d, "d_row", [128, 128], F32)
            a(es_d, "d_col", [128, 128], F32)
            a(es_d, "d_ige", [128, 128], F32)
            a(es_d, "d_diff", [128, 128], F32)
            a(es_d, "d_dte", [128, 256], F32)
            if upto >= 6:
                emit_decay(ctx)
            ctx.barrier()
        with ExitStack() as es_at:
            a(es_at, "wstg2", [128, 4096], BF16)
            a(es_at, "qt", [128, 4096], BF16)
            a(es_at, "kt", [128, 4096], BF16)
            a(es_at, "vtok", [128, 4096], BF16)
            a(es_at, "ktok", [128, 4096], BF16)
            a(es_at, "ssend", [128, 256], F32)
            ctx.t["srecv"] = ctx.t["ssend"]
            ctx.t["sfull"] = ctx.t["ssend"]
            a(es_at, "sbf", [128, 256], BF16)
            a(es_at, "stsb", [128, 512], BF16)
            a(es_at, "qtb", [128, 512], BF16)
            a(es_at, "ktmp2", [128, 512], BF16)
            ctx.t["ktmp"] = ctx.t["ktmp2"]
            a(es_at, "msb", [128, 1], F32)
            a(es_at, "mrb", [128, 1], F32)
            if upto >= 6:
                emit_attn(ctx)
            ctx.barrier()

        with ExitStack() as es_w3:
            a(es_w3, "xst3", [128, 4096], F32)
            a(es_w3, "ot3", [128, 4096], F32)
            if upto >= 7:
                emit_wout(ctx)
            ctx.barrier()

    if debug:
        @block.sync
        def _(sy):
            for i, n in enumerate(["m_t", "rstd_t", "mx_t", "sx_t", "invsx_t",
                                   "swb", "mxz_t", "sq_t", "rstdz_t", "sz_t",
                                   "qsc_t", "invsd_t"]):
                ctx.inc(sy.dma_start(out=io["dbg_st"][:, 16 * i:16 * i + 16],
                                     in_=ctx.t[n][:]), "dsy", 16)
            ctx.inc(sy.dma_start(out=io["dbg_rgd"], in_=io["rgd"]), "dsy", 16)

    @block.sync
    def _(sy):
        for s_ in ["dsy", "da", "db", "dc", "dd", "de", "df", "dg"]:
            ctx.wait(sy, s_)
    @block.vector
    def _(v):
        for s_ in ["dh", "di", "dj"]:
            ctx.wait(v, s_)
    @block.gpsimd
    def _(g):
        ctx.wait(g, "dgp")
    @block.scalar
    def _(s):
        ctx.wait(s, "dsc")

    es.close()
    return nc


def shard_inputs(L0_attn_x, L0_attn_Wqkv, L0_attn_bqkv, L0_attn_Wout,
                 L0_attn_bout, L0_attn_slopes, L0_ffn_w_gate, L0_ffn_w_up,
                 L0_ffn_w_down, L0_ffn_subln_weight):
    f32 = np.float32
    wgS = np.ascontiguousarray(np.asarray(L0_ffn_w_gate, f32).T
                               .reshape(16, 128, 64, 128).transpose(2, 1, 0, 3)
                               .reshape(64, 128, 2048))
    wuS = np.ascontiguousarray(np.asarray(L0_ffn_w_up, f32).T
                               .reshape(16, 128, 64, 128).transpose(2, 1, 0, 3)
                               .reshape(64, 128, 2048))
    wdT = np.ascontiguousarray(np.asarray(L0_ffn_w_down, f32).T)
    gsub_t = np.ascontiguousarray(np.asarray(L0_ffn_subln_weight, f32).reshape(64, 128).T)
    bqkv = np.asarray(L0_attn_bqkv, f32)
    bqkv_t = np.ascontiguousarray(bqkv.reshape(48, 128).T)
    slopes_r = np.ascontiguousarray(np.asarray(L0_attn_slopes, f32).reshape(1, 16))
    bqv = np.ascontiguousarray(bqkv[4096:].reshape(1, D))
    bout_r = np.ascontiguousarray(np.asarray(L0_attn_bout, f32).reshape(1, D))
    wg_f = np.asarray(L0_ffn_w_gate, f32).reshape(-1)
    wu_f = np.asarray(L0_ffn_w_up, f32).reshape(-1)
    wd_f = np.asarray(L0_ffn_w_down, f32).reshape(-1)
    x = np.asarray(L0_attn_x, f32)
    maps = []
    n8 = 2097152
    for c in range(8):
        b, hf = c // 2, c % 2
        xs = np.ascontiguousarray(x[b, 2048 * hf:2048 * hf + 2048, :])
        xTs = np.ascontiguousarray(xs.T)
        wmsl = np.ascontiguousarray(np.stack([
            wg_f[c * n8:(c + 1) * n8].reshape(128, 16384),
            wu_f[c * n8:(c + 1) * n8].reshape(128, 16384),
            wd_f[c * n8:(c + 1) * n8].reshape(128, 16384)]))
        maps.append({
            "xtok": xs, "xT": xTs, "wgS": wgS, "wuS": wuS, "wdT": wdT,
            "wqkv": np.ascontiguousarray(np.asarray(L0_attn_Wqkv, f32)),
            "wout": np.ascontiguousarray(np.asarray(L0_attn_Wout, f32)),
            "gsub_t": gsub_t, "wmsl": wmsl,
            "bqkv_t": bqkv_t, "slopes_r": slopes_r, "bqv": bqv,
            "bout_r": bout_r,
            "mrecv": np.array([[float(hf)]], dtype=f32),
            "msend": np.array([[float(1 - hf)]], dtype=f32),
        })
    return maps


_NC_CACHE = {}


def kernel(**inputs):
    if "nc" not in _NC_CACHE:
        _NC_CACHE["nc"] = build_full(debug=False)
    nc = _NC_CACHE["nc"]
    maps = shard_inputs(**inputs)
    res = run_bass_kernel_spmd(nc, maps, core_ids=list(range(8)))
    out = np.concatenate([np.asarray(res.results[c]["out"], dtype=np.float32)
                          .reshape(1, T, D) for c in range(8)])
    return out.reshape(4, 4096, D)
